# version 19
# baseline (speedup 1.0000x reference)
import os
import math
import numpy as np
import ml_dtypes
import concourse.bass as bass
import concourse.mybir as mybir
from concourse.bass_utils import run_bass_kernel_spmd
from concourse.bass import IndirectOffsetOnAxis
from contextlib import ExitStack

F32 = mybir.dt.float32
BF16 = mybir.dt.bfloat16
I32 = mybir.dt.int32
AF = mybir.ActivationFunctionType
ALU = mybir.AluOpType

NCORES = 8
DM = 2048
TPC = 2048
SEQ = 8192
BIS = 36
SEL = 1024
FFNH = 8192
DEBUG = os.environ.get("MK_DEBUG", "0") == "1"
ONLY = os.environ.get("MK_ONLY", "")

ENGS = ("tensor", "vector", "scalar", "gpsimd", "sync")


class Sched:

    def __init__(self, nc, ctx):
        self.nc = nc
        self.ctx = ctx
        self.q = {e: [] for e in ENGS}
        self.sems = {}
        self.track = {}
        self.waited = {e: {} for e in ENGS}
        self.dma_cts = set()
        self.dma_max_waited = {}
        self.dma_selfwait = {}
        for e in ENGS:
            self.sems["ct_" + e] = [self.ctx.enter_context(nc.semaphore("ct_" + e)), 0]

    def sem(self, name):
        if name not in self.sems:
            self.sems[name] = [self.ctx.enter_context(self.nc.semaphore(name)), 0]
        return self.sems[name]

    def _key(self, t):
        h = getattr(t, "tensor", t)
        return getattr(h, "name", None) or id(h)

    def _need_waits(self, eng, ins, outs):
        need = {}
        for t in ins:
            rec = self.track.get(self._key(t))
            if rec and rec.get("w"):
                ct, val = rec["w"]
                need[ct] = max(need.get(ct, 0), val)
        for t in outs:
            rec = self.track.get(self._key(t))
            if rec:
                if rec.get("w"):
                    ct, val = rec["w"]
                    need[ct] = max(need.get(ct, 0), val)
                for ct, val in rec.get("r", {}).items():
                    need[ct] = max(need.get(ct, 0), val)
        out = []
        for ct, val in need.items():
            if val <= 0:
                continue
            if ct in self.dma_cts:
                val = self.sems[ct][1]
                self.dma_max_waited[ct] = max(self.dma_max_waited.get(ct, 0), val)
            if self.waited[eng].get(ct, 0) >= val:
                continue
            if ct == "ct_" + eng and eng == "tensor":
                continue
            out.append((ct, val))
            self.waited[eng][ct] = val
        return out

    def _mark(self, eng, ins, outs, ctname, val):
        for t in ins:
            rec = self.track.setdefault(self._key(t), {"w": None, "r": {}})
            r = rec["r"]
            r[ctname] = max(r.get(ctname, 0), val)
        for t in outs:
            self.track[self._key(t)] = {"w": (ctname, val), "r": {}}

    def op(self, eng, ins, outs, fn, inc=1, ct=None):
        ctname = ct or ("ct_" + eng)
        sem_entry = self.sem(ctname)
        pre_selfwait = None
        if inc >= 16:
            self.dma_cts.add(ctname)
            if self.dma_max_waited.get(ctname, 0) > self.dma_selfwait.get(ctname, 0):
                pre_selfwait = sem_entry[1]
                self.dma_selfwait[ctname] = sem_entry[1]
        waits = self._need_waits(eng, ins, outs)
        if pre_selfwait:
            waits = [(ctname, pre_selfwait)] + [w for w in waits if w[0] != ctname or w[1] > pre_selfwait]
            self.waited[eng][ctname] = max(self.waited[eng].get(ctname, 0), pre_selfwait)
        sem_entry[1] += inc
        val = sem_entry[1]
        self._mark(eng, ins, outs, ctname, val if inc else val + 1)
        sems = self.sems

        def run(e, waits=waits, fn=fn, sem=sem_entry[0], val=val, inc=inc):
            for wct, wval in waits:
                e.wait_ge(sems[wct][0], wval)
            instr = fn(e)
            if inc:
                instr.then_inc(sem, inc)
        self.q[eng].append(run)
        return val

    def raw(self, eng, fn):
        self.q[eng].append(fn)

    def barrier(self):
        snap = [(name, ent[1]) for name, ent in self.sems.items() if ent[1] > 0]
        sems = self.sems
        for e in ENGS:
            todo = [(n, v) for (n, v) in snap
                    if self.waited[e].get(n, 0) < v and not (n == "ct_" + e)]
            for n, v in todo:
                self.waited[e][n] = v

            def run(eng_obj, todo=todo):
                for n, v in todo:
                    eng_obj.wait_ge(sems[n][0], v)
            self.q[e].append(run)

    def flush(self, blk):
        for e in ENGS:
            fns = self.q[e]
            getattr(blk, e)(lambda eng, fns=fns: [fn(eng) for fn in fns])

    def dma(self, queue, out, in_, ct=None, **kw):
        ctname = ct or ("dma_" + queue)
        return self.op(queue, [in_], [out],
                       lambda e: e.dma_start(out=out, in_=in_, **kw), inc=16, ct=ctname)

    def dmat(self, queue, out, in_, ct=None):
        ctname = ct or ("dma_" + queue)
        return self.op(queue, [in_], [out],
                       lambda e: e.dma_start_transpose(out, in_), inc=16, ct=ctname)

    def mm(self, outs, lhsT, rhs, start, stop, inc=True):
        return self.op("tensor", [lhsT, rhs], outs if (stop or start) else [],
                       lambda e: e.matmul(outs[0], lhsT, rhs, start=start, stop=stop),
                       inc=1 if stop else 0)

    def trans(self, out_psum, in_sb, ident):
        return self.op("tensor", [in_sb, ident], [out_psum],
                       lambda e: e.transpose(out_psum, in_sb, ident))

    def v(self, ins, outs, fn):
        return self.op("vector", ins, outs, fn)

    def s(self, ins, outs, fn):
        return self.op("scalar", ins, outs, fn)

    def g(self, ins, outs, fn, inc=1, ct=None):
        return self.op("gpsimd", ins, outs, fn, inc=inc, ct=ct)


def _np_bf16(x):
    return x.astype(ml_dtypes.bfloat16)


def host_constants():
    c = {}
    t = np.zeros((128, 128), np.float32)
    for po in range(128):
        h = po // 64
        t[h * 64:po, po] = 1.0
    c["tri_strict"] = t
    o2 = np.zeros((128, 2), np.float32)
    o2[:64, 0] = 1.0; o2[64:, 1] = 1.0
    c["ones2"] = o2
    m2 = np.zeros((2, 128), np.float32)
    m2[0, :64] = 1.0; m2[1, 64:] = 1.0
    c["map2"] = m2
    bm = np.zeros((128, 128), np.float32)
    bm[:64, :64] = 1.0; bm[64:, 64:] = 1.0
    c["bmask"] = bm
    c["ident128"] = np.eye(128, dtype=np.float32)
    bc = np.zeros((128, 1), np.float32)
    bc[64:, 0] = float(SEL)
    c["bcol1024"] = bc
    mp = np.zeros((64, 256), np.float32)
    for h in range(4):
        for q in range(64):
            mp[: q + 1, h * 64 + q] = 1.0
    c["maskPT"] = np.concatenate([mp, mp], axis=0)
    return c


class Prog:
    def __init__(self):
        self.nc = bass.Bass("TRN2", target_bir_lowering=False, debug=False,
                            dynamic_dma_scratch_size=32768)
        self.T = {}
        self.ctx = ExitStack()

    def inp(self, name, shape, dt=F32):
        self.T[name] = self.nc.dram_tensor(name, list(shape), dt, kind="ExternalInput").ap()
        return self.T[name]

    def scr(self, name, shape, dt=F32):
        self.T[name] = self.nc.dram_tensor(name, list(shape), dt, kind="Internal").ap()
        return self.T[name]

    def outp(self, name, shape, dt=F32):
        self.T[name] = self.nc.dram_tensor(name, list(shape), dt, kind="ExternalOutput").ap()
        return self.T[name]

    def sb(self, name, shape, dt):
        return self.ctx.enter_context(self.nc.sbuf_tensor(name, list(shape), dt))

    def ps(self, name, shape, dt=F32):
        return self.ctx.enter_context(self.nc.psum_tensor(name, list(shape), dt))

    def declare(self):
        inp = self.inp
        inp("x", (TPC, DM))
        inp("wqT_a", (DM, DM)); inp("wkT_a", (DM, 512)); inp("wvT_a", (DM, 512))
        inp("bq_a", (128, 16)); inp("bk_a", (128, 4)); inp("bv_b", (128, 512))
        inp("wq_f", (DM, 256)); inp("wk_f", (DM, 256)); inp("wv_f", (DM, 256))
        inp("wo_f", (DM, DM))
        inp("w1_f", (DM, 1024)); inp("w3_f", (DM, 1024)); inp("w2_f", (1024, DM))
        inp("g1b", (128, DM)); inp("g2b", (128, DM)); inp("wrb", (128, DM))
        inp("maskT", (SEL, SEL))
        inp("cosx", (128, 2 * SEL)); inp("sinx", (128, 2 * SEL))
        inp("selmy", (128, 16)); inp("iota1", (128, 1)); inp("iota16", (128, 1))
        inp("tri_strict", (128, 128)); inp("ones2", (128, 2)); inp("map2", (2, 128))
        inp("bmask", (128, 128))
        inp("ident128", (128, 128)); inp("bcol1024", (128, 1)); inp("maskPT", (128, 256))
        inp("core_meta", (128, 16))
        self.outp("out_x", (TPC, DM))
        scr = self.scr
        scr("xa", (TPC, DM)); scr("xb", (TPC, DM))
        scr("ybuf", (TPC, DM)); scr("yb16", (TPC, DM), BF16)
        scr("lg_in", (1, TPC)); scr("lg_out", (NCORES, TPC))
        scr("off_my", (TPC, 1), I32)
        scr("rw_my", (TPC, 1))
        scr("ysel_in", (2 * SEL + 1, DM), BF16); scr("ysel_out", (2 * SEL, DM), BF16)
        scr("o_in", (256, 2 * SEL), BF16); scr("o_out", (DM, 2 * SEL), BF16)
        scr("h2_in", (256, DM), BF16); scr("h2_out", (2 * SEL, DM), BF16)
        scr("ffn_in", (2 * SEL + 1, DM), BF16); scr("ffn_out", (2 * SEL + 1, DM), BF16)
        scr("zrow", (4, SEL)); scr("snap", (128, 32 * 4 * 132))
        scr("qfm_d", (128, 16 * 2048), BF16); scr("x2_d", (256, DM), BF16)
        scr("h_d", (2 * SEL, DM), BF16)
        scr("gla_in", (128, 528)); scr("gla_out", (NCORES * 128, 528))
        if DEBUG:
            self.outp("dbg_y", (TPC, DM))
            self.outp("dbg_lg", (NCORES, TPC))
            self.outp("dbg_flags", (128, 128))
            self.outp("dbg_prefix", (128, 128))
            self.outp("dbg_rw", (128, 128))
            self.outp("dbg_offmy", (128, 16), I32)
            self.outp("dbg_ysel", (2 * SEL, DM), BF16)
            self.outp("dbg_x3", (2 * SEL, DM), BF16)
            self.outp("dbg_x1", (TPC, DM))
            self.outp("dbg_x2", (TPC, DM))
            self.outp("dbg_zrow", (4, SEL))
            self.outp("dbg_cumb", (128, 4 * 2049))
            self.outp("dbg_kfm", (128, 4 * 2048), BF16)
            self.outp("dbg_vtm", (128, 16 * 512), BF16)
            self.outp("dbg_qfmd", (128, 16 * 2048), BF16)
            self.outp("dbg_snap", (128, 32 * 528))
            self.outp("dbg_glao", (NCORES * 512, 132))
            self.outp("dbg_oout", (DM, 2 * SEL), BF16)
            self.outp("dbg_x2d", (256, DM), BF16)

    def build(self):
        self.declare()
        with self.ctx:
            blk = self.ctx.enter_context(self.nc.Block())
            sc = Sched(self.nc, self.ctx)
            self.sc = sc
            self.load_consts()
            sc.barrier()
            if ONLY == "gla":
                self.copy_dram(self.T["x"], self.T["xa"], (TPC, DM), F32, tag="cx")
                sc.barrier()
                self.ln_logits(src=self.T["xa"], call=2, logits=False)
                sc.barrier()
                self.gla()
                sc.barrier()
                self.copy_dram(self.T["xb"], self.T["out_x"], (TPC, DM), F32, tag="co")
                sc.barrier()
                sc.flush(blk)
                return self.nc
            self.ln_logits(src=self.T["x"], call=0)
            sc.barrier()
            self.router(call=0)
            sc.barrier()
            self.ysel(call=0)
            sc.barrier()
            self.tb(call=0)
            sc.barrier()
            self.epilogue(x_src=self.T["x"], x_dst=self.T["xa"], call=0, dbg="dbg_x1")
            sc.barrier()
            self.gla()
            sc.barrier()
            if DEBUG:
                self.copy_dram(self.T["xb"], self.T["dbg_x2"], (TPC, DM), F32)
            self.router(call=1)
            sc.barrier()
            self.ysel(call=1)
            sc.barrier()
            self.tb(call=1)
            sc.barrier()
            self.epilogue(x_src=self.T["xb"], x_dst=self.T["out_x"], call=1, dbg=None)
            sc.barrier()
            sc.flush(blk)
        return self.nc

    def tb(self, call):
        sc, T, C = self.sc, self.T, self.consts
        tag = f"tb{call}"
        with ExitStack() as px:
            sb = lambda n, s, d: px.enter_context(self.nc.sbuf_tensor(tag + n, list(s), d))
            ps = lambda n, s: px.enter_context(self.nc.psum_tensor(tag + n, list(s), F32))
            h2t = sb("h2t", (128, 2048), BF16)
            stq = sb("stq", (128, 8), F32)
            sqs = sb("sqs", (128, 2048), BF16)
            idxb = sb("idxb", (128, 4), I32)
            idxf = sb("idxf", (128, 4), F32)
            def mkpool(stk, nm, n=3, shape=(128, 512)):
                ts = [stk.enter_context(self.nc.psum_tensor(tag + nm + str(i),
                      list(shape), F32)) for i in range(n)]
                ii = [0]
                def nxt():
                    t = ts[ii[0] % n]; ii[0] += 1
                    return t
                return nxt
            with ExitStack() as ax:
                asb = lambda n, s, d: ax.enter_context(self.nc.sbuf_tensor(tag + n, list(s), d))
                qfm = asb("qfm", (128, 2 * 2048), BF16)
                kfm = asb("kfm", (128, 2 * 2048), BF16)
                vtm = asb("vtm", (128, 16 * 256), BF16)
                with ExitStack() as bx:
                    bsb = lambda n, s, d: bx.enter_context(self.nc.sbuf_tensor(tag + n, list(s), d))
                    pacc_next = mkpool(bx, "pqk")
                    g1b = bsb("g1b", (128, DM), BF16)
                    sc.dma("gpsimd", g1b[:], T["g1b"], ct="ldc2")
                    htis = [bsb(f"hti{r}", (128, DM), BF16) for r in range(2)]
                    htos = [bsb(f"hto{r}", (128, DM), BF16) for r in range(2)]
                    hsts = [bsb(f"hst{r}", (128, 4), F32) for r in range(2)]
                    hsqs = [bsb(f"hsq{r}", (128, DM), BF16) for r in range(2)]
                    for i0 in range(0, 16, 2):
                        for i in (i0, i0 + 1):
                            hti = htis[i % 2]
                            sc.dma("sync", hti[:],
                                   T["ysel_out"][i * 128:(i + 1) * 128, :],
                                   ct="ldh" + str(i % 2))
                        for i in (i0, i0 + 1):
                            hti, hto = htis[i % 2], htos[i % 2]
                            hst, hsq = hsts[i % 2], hsqs[i % 2]
                            sc.s([hti], [hsq, hst], lambda e, hti=hti, hsq=hsq, hst=hst: e.activation(
                                hsq[:], hti[:], AF.Square, accum_out=hst[:, 0:1]))
                            sc.s([hst], [hst], lambda e, hst=hst: e.activation(
                                hst[:, 1:2], hst[:, 0:1], AF.Ln, scale=1.0 / DM,
                                bias=self.ccol[:, 1:2]))
                            sc.s([hst], [hst], lambda e, hst=hst: e.activation(
                                hst[:, 2:3], hst[:, 1:2], AF.Exp, scale=-0.5))
                            sc.v([hti, hst, g1b], [hto], lambda e, hti=hti, hst=hst, hto=hto: e.scalar_tensor_tensor(
                                hto[:], hti[:], hst[:, 2:3], g1b[:], op0=ALU.mult, op1=ALU.mult))
                        for i in (i0, i0 + 1):
                            sc.dma("sync", T["h_d"][i * 128:(i + 1) * 128, :],
                                   htos[i % 2][:], ct="sthd" + str(i % 2))
                    yfm = bsb("yfm", (128, 16 * 2048), BF16)
                    for cch in range(16):
                        sc.dmat("scalar", yfm[:, cch * 2048:(cch + 1) * 2048],
                                T["h_d"][:, cch * 128:(cch + 1) * 128], ct="tpy")
                    s_wq = bsb("wq", (128, 16 * 256), BF16)
                    s_wk = bsb("wk", (128, 16 * 256), BF16)
                    s_wv = bsb("wv", (128, 16 * 256), BF16)
                    for nm, t in [("wq_f", s_wq), ("wk_f", s_wk), ("wv_f", s_wv)]:
                        sc.dma("gpsimd", t.ap().rearrange("p (c o) -> p c o", o=256),
                               T[nm].rearrange("(c p) o -> p c o", p=128), ct="ldw")
                    for dst, w, scale in [(qfm, s_wq, 1.0 / math.sqrt(128.0)),
                                          (kfm, s_wk, 1.0)]:
                        for h2 in range(2):
                            for ft in range(4):
                                pa = pacc_next()
                                for f in range(16):
                                    sc.mm([pa[:]], w[:, f * 256 + h2 * 128:f * 256 + h2 * 128 + 128],
                                          yfm[:, f * 2048 + ft * 512:f * 2048 + ft * 512 + 512],
                                          start=(f == 0), stop=(f == 15))
                                sl = dst[:, h2 * 2048 + ft * 512:h2 * 2048 + ft * 512 + 512]
                                sc.s([pa], [dst], lambda e, pa=pa, sl=sl, scale=scale: e.activation(
                                    sl, pa[:], AF.Copy, scale=scale))
                    for tt in range(16):
                        pa = pacc_next()
                        for f in range(16):
                            sc.mm([pa[:, 0:256]],
                                  yfm[:, f * 2048 + tt * 128:f * 2048 + tt * 128 + 128],
                                  s_wv[:, f * 256:(f + 1) * 256], start=(f == 0), stop=(f == 15))
                        sl = vtm[:, tt * 256:(tt + 1) * 256]
                        sc.s([pa], [vtm], lambda e, pa=pa, sl=sl: e.activation(sl, pa[:, 0:256], AF.Copy))
                sc.barrier()
                mt = asb("mt", (128, 8 * 1024), BF16)
                sc.dma("gpsimd", mt.ap().rearrange("p (c o) -> p c o", o=1024),
                       T["maskT"].rearrange("(c p) o -> p c o", p=128), ct="ldw")
                cosx = asb("cosx", (128, 2 * SEL), BF16)
                sinx = asb("sinx", (128, 2 * SEL), BF16)
                sc.dma("gpsimd", cosx[:], T["cosx"], ct="ldc2")
                sc.dma("gpsimd", sinx[:], T["sinx"], ct="ldc2")
                rot = asb("rot", (128, 2048), BF16)
                tmp = asb("tmp", (128, 2048), BF16)
                for t_ in (qfm, kfm):
                    for h in range(2):
                        sc.dma("sync", rot[0:64, :], t_[64:128, h * 2048:(h + 1) * 2048], ct="rsw")
                        sc.dma("sync", rot[64:128, :], t_[0:64, h * 2048:(h + 1) * 2048], ct="rsw")
                        sc.v([t_, cosx], [tmp], lambda e, t_=t_, h=h: e.tensor_mul(
                            tmp[:], t_[:, h * 2048:(h + 1) * 2048], cosx[:]))
                        sc.v([rot, sinx], [rot], lambda e: e.tensor_mul(
                            rot[:], rot[:], sinx[:]))
                        sc.v([tmp, rot], [t_], lambda e, t_=t_, h=h: e.tensor_add(
                            t_[:, h * 2048:(h + 1) * 2048], tmp[:], rot[:]))
                ptall = asb("ptall", (128, 8 * 1024), BF16)
                onesb = asb("ones1", (128, 1), BF16)
                sc.v([], [onesb], lambda e: e.memset(onesb[:], 1.0))
                otm = asb("otm", (128, 8 * 128), BF16)
                ofm = asb("ofm", (128, 2 * 2048), BF16)
                zcolv = asb("zcol", (128, 8), F32)
                p_sts = [ax.enter_context(self.nc.psum_tensor(tag + f"pst{i}", [128, 1024], F32))
                         for i in range(2)]
                p_zc = ax.enter_context(self.nc.psum_tensor(tag + "pzc", [128, 8], F32))
                p_oA = ax.enter_context(self.nc.psum_tensor(tag + "poA", [128, 512], F32))
                p_oB = ax.enter_context(self.nc.psum_tensor(tag + "poB", [128, 512], F32))
                for b in range(2):
                    for h in range(2):
                        for kc in range(8):
                            p_st = p_sts[kc % 2]
                            for qh in range(2):
                                sc.mm([p_st[:, qh * 512:(qh + 1) * 512]],
                                      kfm[:, h * 2048 + b * 1024 + kc * 128:
                                          h * 2048 + b * 1024 + kc * 128 + 128],
                                      qfm[:, h * 2048 + b * 1024 + qh * 512:
                                          h * 2048 + b * 1024 + qh * 512 + 512],
                                      start=True, stop=True)
                            if os.environ.get("MK_NOMASK", "0") != "1":
                                msl = mt[:, kc * 1024:(kc + 1) * 1024]
                                sc.v([mt, p_st], [p_st], lambda e, p_st=p_st, msl=msl:
                                     e.tensor_tensor(p_st[:], p_st[:], msl, op=ALU.add))
                            for qh in range(2):
                                sc.s([p_st], [ptall], lambda e, p_st=p_st, qh=qh, kc=kc:
                                     e.activation(
                                         ptall[:, kc * 1024 + qh * 512:kc * 1024 + qh * 512 + 512],
                                         p_st[:, qh * 512:(qh + 1) * 512], AF.Exp))
                        vs0 = (8 * b) * 256 + h * 128
                        for qq in range(8):
                            dst = p_oA if qq < 4 else p_oB
                            reg = dst[:, (qq % 4) * 128:(qq % 4) * 128 + 128]
                            for kc in range(8):
                                sc.mm([reg],
                                      ptall[:, kc * 1024 + qq * 128:kc * 1024 + qq * 128 + 128],
                                      vtm[:, vs0 + kc * 256: vs0 + kc * 256 + 128],
                                      start=(kc == 0), stop=(kc == 7))
                            for kc in range(8):
                                sc.mm([p_zc[:, qq:qq + 1]],
                                      ptall[:, kc * 1024 + qq * 128:kc * 1024 + qq * 128 + 128],
                                      onesb[:], start=(kc == 0), stop=(kc == 7))
                        sc.v([p_zc], [zcolv], lambda e: e.reciprocal(zcolv[:], p_zc[:]))
                        for qq in range(8):
                            srcp = (p_oA if qq < 4 else p_oB)[:, (qq % 4) * 128:(qq % 4) * 128 + 128]
                            sc.v([p_oA, p_oB, zcolv], [otm], lambda e, srcp=srcp, qq=qq:
                                 e.tensor_scalar(otm[:, qq * 128:(qq + 1) * 128], srcp,
                                                 zcolv[:, qq:qq + 1], None, op0=ALU.mult))
                        for qq in range(8):
                            sc.dmat("scalar", ofm[:, h * 2048 + b * 1024 + qq * 128:
                                                  h * 2048 + b * 1024 + qq * 128 + 128],
                                    otm[:, qq * 128:(qq + 1) * 128], ct="tpo")
                for h in range(2):
                    sc.dma("sync", T["o_in"][h * 128:(h + 1) * 128, :],
                           ofm[:, h * 2048:(h + 1) * 2048], ct="sto")
                sc.g([T["o_in"]], [T["o_out"]], lambda e: e.collective_compute(
                    "AllGather", ALU.bypass, replica_groups=[list(range(NCORES))],
                    ins=[T["o_in"]], outs=[T["o_out"]]), inc=1, ct="cc")
            sc.barrier()
            with ExitStack() as cx:
                csb = lambda n, s, d: cx.enter_context(self.nc.sbuf_tensor(tag + n, list(s), d))
                pacc_next = mkpool(cx, "pow")
                x2t = csb("x2t", (128, 2 * 2048), F32)
                g2b = csb("g2b", (128, DM), BF16)
                sc.dma("gpsimd", g2b[:], T["g2b"], ct="ldc2")
                s_wo = csb("wo", (128, 16 * 2048), BF16)
                sc.dma("gpsimd", s_wo.ap().rearrange("p (c o) -> p c o", o=2048),
                       T["wo_f"].rearrange("(c p) o -> p c o", p=128), ct="ldw")
                iosl = csb("iosl", (128, 16 * 128), BF16)
                ys_my = csb("ysmy", (128, 2 * 2048), BF16)
                o_view = T["o_out"].rearrange("r (c p) -> (r c) p", p=128)
                for tc in range(2):
                    sc.v([C["iota1"], C["core_meta"]], [idxf], lambda e, tc=tc: e.tensor_scalar(
                        idxf[:, 0:1], C["iota1"][:], C["core_meta"][:, 3:4], float(tc * 128),
                        op0=ALU.add, op1=ALU.add))
                    sc.v([idxf], [idxb], lambda e: e.tensor_copy(idxb[:, 0:1], idxf[:, 0:1]))
                    sc.op("gpsimd", [T["ysel_out"], idxb], [ys_my],
                          lambda e, tc=tc: e.indirect_dma_start(
                              out=ys_my[:, tc * 2048:(tc + 1) * 2048], out_offset=None,
                              in_=T["ysel_out"],
                              in_offset=IndirectOffsetOnAxis(idxb[:, 0:1], 0)),
                          inc=16, ct="gys")
                    for hc in range(16):
                        sc.v([C["iota16"], C["core_meta"]], [idxf], lambda e, hc=hc, tc=tc:
                             e.tensor_scalar(idxf[:, 1:2], C["iota16"][:],
                                             C["core_meta"][:, 4:5], float(tc + hc * 2048),
                                             op0=ALU.add, op1=ALU.add))
                        sc.v([idxf], [idxb], lambda e: e.tensor_copy(idxb[:, 1:2], idxf[:, 1:2]))
                        sc.op("gpsimd", [T["o_out"], idxb], [iosl],
                              lambda e, hc=hc: e.indirect_dma_start(
                                  out=iosl[:, hc * 128:(hc + 1) * 128], out_offset=None,
                                  in_=o_view,
                                  in_offset=IndirectOffsetOnAxis(idxb[:, 1:2], 0)),
                              inc=16, ct="gos")
                    for ft in range(4):
                        pa = pacc_next()
                        for hc in range(16):
                            sc.mm([pa[:]], iosl[:, hc * 128:(hc + 1) * 128],
                                  s_wo[:, hc * 2048 + ft * 512:hc * 2048 + ft * 512 + 512],
                                  start=(hc == 0), stop=(hc == 15))
                        ysl = ys_my[:, tc * 2048 + ft * 512: tc * 2048 + ft * 512 + 512]
                        x2sl = x2t[:, tc * 2048 + ft * 512: tc * 2048 + ft * 512 + 512]
                        sc.v([pa, ys_my], [x2t], lambda e, pa=pa, ysl=ysl, x2sl=x2sl: e.tensor_tensor(
                            x2sl, pa[:], ysl, op=ALU.add))
                    x2sl = x2t[:, tc * 2048:(tc + 1) * 2048]
                    sc.s([x2t], [sqs, stq], lambda e, x2sl=x2sl: e.activation(
                        sqs[:], x2sl, AF.Square, accum_out=stq[:, 0:1]))
                    sc.s([stq], [stq], lambda e: e.activation(
                        stq[:, 1:2], stq[:, 0:1], AF.Ln, scale=1.0 / DM, bias=self.ccol[:, 1:2]))
                    sc.s([stq], [stq], lambda e: e.activation(
                        stq[:, 2:3], stq[:, 1:2], AF.Exp, scale=-0.5))
                    sc.v([x2t, stq, g2b], [h2t], lambda e, x2sl=x2sl: e.scalar_tensor_tensor(
                        h2t[:], x2sl, stq[:, 2:3], g2b[:], op0=ALU.mult, op1=ALU.mult))
                    sc.dma("sync", T["h2_in"][tc * 128:(tc + 1) * 128, :], h2t[:], ct="sth")
                    sc.v([x2t], [sqs], lambda e, x2sl=x2sl: e.tensor_copy(sqs[:], x2sl))
                    sc.dma("sync", T["x2_d"][tc * 128:(tc + 1) * 128, :], sqs[:], ct="stx2d")
                sc.g([T["h2_in"]], [T["h2_out"]], lambda e: e.collective_compute(
                    "AllGather", ALU.bypass, replica_groups=[list(range(NCORES))],
                    ins=[T["h2_in"]], outs=[T["h2_out"]]), inc=1, ct="cc")
            sc.barrier()
            with ExitStack() as dx:
                dsb = lambda n, s, d: dx.enter_context(self.nc.sbuf_tensor(tag + n, list(s), d))
                pacc_next = mkpool(dx, "pff")
                s_w1 = dsb("w1", (128, 16 * 1024), BF16)
                s_w3 = dsb("w3", (128, 16 * 1024), BF16)
                s_w2 = dsb("w2", (128, 8 * 2048), BF16)
                sc.dma("gpsimd", s_w1.ap().rearrange("p (c o) -> p c o", o=1024),
                       T["w1_f"].rearrange("(c p) o -> p c o", p=128), ct="ldw")
                sc.dma("gpsimd", s_w3.ap().rearrange("p (c o) -> p c o", o=1024),
                       T["w3_f"].rearrange("(c p) o -> p c o", p=128), ct="ldw")
                sc.dma("gpsimd", s_w2.ap().rearrange("p (c o) -> p c o", o=2048),
                       T["w2_f"].rearrange("(c p) o -> p c o", p=128), ct="ldw")
                h2fm = dsb("h2fm", (128, 16 * 512), BF16)
                spr = dsb("spr", (128, 8 * 512), BF16)
                su = dsb("su", (128, 512), BF16)
                stg = dsb("stg", (128, 2048), BF16)
                for quar in range(4):
                    for cch in range(16):
                        sc.dmat("scalar", h2fm[:, cch * 512:(cch + 1) * 512],
                                T["h2_out"][quar * 512:(quar + 1) * 512,
                                            cch * 128:(cch + 1) * 128], ct="tph")
                    for fc in range(8):
                        pa = pacc_next()
                        for f in range(16):
                            sc.mm([pa[:]],
                                  s_w1[:, f * 1024 + fc * 128:f * 1024 + fc * 128 + 128],
                                  h2fm[:, f * 512:(f + 1) * 512],
                                  start=(f == 0), stop=(f == 15))
                        pb = pacc_next()
                        for f in range(16):
                            sc.mm([pb[:]],
                                  s_w3[:, f * 1024 + fc * 128:f * 1024 + fc * 128 + 128],
                                  h2fm[:, f * 512:(f + 1) * 512],
                                  start=(f == 0), stop=(f == 15))
                        sc.s([pa], [su], lambda e, pa=pa: e.activation(su[:], pa[:], AF.Silu))
                        psl = spr[:, fc * 512:(fc + 1) * 512]
                        sc.v([pb, su], [spr], lambda e, pb=pb, psl=psl: e.tensor_tensor(
                            psl, pb[:], su[:], op=ALU.mult))
                    for tt in range(4):
                        for ft in range(4):
                            pa = pacc_next()
                            for fc in range(8):
                                sc.mm([pa[:]],
                                      spr[:, fc * 512 + tt * 128:fc * 512 + tt * 128 + 128],
                                      s_w2[:, fc * 2048 + ft * 512:fc * 2048 + ft * 512 + 512],
                                      start=(fc == 0), stop=(fc == 7))
                            sc.s([pa], [stg], lambda e, pa=pa, ft=ft: e.activation(
                                stg[:, ft * 512:(ft + 1) * 512], pa[:], AF.Copy))
                        sc.dma("sync", T["ffn_in"][quar * 512 + tt * 128:
                                                   quar * 512 + tt * 128 + 128, :],
                               stg[:], ct="stf")
                for tc in range(2):
                    sc.dma("sync", stg[:], T["x2_d"][tc * 128:(tc + 1) * 128, :], ct="ldx2d")
                    sc.v([C["iota1"], C["core_meta"]], [idxf], lambda e, tc=tc: e.tensor_scalar(
                        idxf[:, 2:3], C["iota1"][:], C["core_meta"][:, 3:4], float(tc * 128),
                        op0=ALU.add, op1=ALU.add))
                    sc.v([idxf], [idxb], lambda e: e.tensor_copy(idxb[:, 2:3], idxf[:, 2:3]))
                    sc.op("gpsimd", [stg, idxb, T["ffn_in"]], [T["ffn_in"]],
                          lambda e: e.indirect_dma_start(
                              out=T["ffn_in"],
                              out_offset=IndirectOffsetOnAxis(idxb[:, 2:3], 0),
                              in_=stg[:], in_offset=None,
                              compute_op=ALU.add),
                          inc=16, ct="scax")
                sc.dma("sync", T["ffn_in"][2 * SEL:2 * SEL + 1, :],
                       self.zero16[0:1, :], ct="zft")
                sc.g([T["ffn_in"]], [T["ffn_out"]], lambda e: e.collective_compute(
                    "AllReduce", ALU.add, replica_groups=[list(range(NCORES))],
                    ins=[T["ffn_in"]], outs=[T["ffn_out"]]), inc=1, ct="cc")
            if DEBUG and call == 0:
                self.copy_dram(T["ffn_out"], T["dbg_x3"], (2 * SEL, DM), BF16, tag="dcx3")
                self.copy_dram(T["zrow"], T["dbg_zrow"], (4, SEL), F32, tag="dcz") if False else None
                with ExitStack() as zx:
                    zt = zx.enter_context(self.nc.sbuf_tensor("dbgzt", [4, SEL], F32))
                    sc.dma("sync", zt[:], T["zrow"], ct="dbg")
                    sc.dma("sync", T["dbg_zrow"], zt[:], ct="dbg")
                self.copy_dram(T["o_out"], T["dbg_oout"], (DM, 2 * SEL), BF16, tag="dco")
                with ExitStack() as zx:
                    zt = zx.enter_context(self.nc.sbuf_tensor("dbgx2t", [128, DM], BF16))
                    for i in range(2):
                        sc.dma("sync", zt[:], T["x2_d"][i * 128:(i + 1) * 128, :], ct="dbg")
                        sc.dma("sync", T["dbg_x2d"][i * 128:(i + 1) * 128, :], zt[:], ct="dbg")

    def gla(self):
        sc, T, C = self.sc, self.T, self.consts
        tag = "gl"
        with ExitStack() as px:
            sb = lambda n, s, d: px.enter_context(self.nc.sbuf_tensor(tag + n, list(s), d))
            ps = lambda n, s, d=F32: px.enter_context(self.nc.psum_tensor(tag + n, list(s), d))
            ident = sb("ident", (128, 128), BF16)
            maskPT = sb("maskPT", (128, 256), BF16)
            sc.dma("gpsimd", ident[:], T["ident128"], ct="ldc2")
            sc.dma("gpsimd", maskPT[:], T["maskPT"], ct="ldc2")
            kfm = sb("kfm", (128, 4 * 2048), BF16)
            vtm = sb("vtm", (128, 16 * 512), BF16)
            bqa = sb("bqa", (128, 16), F32)
            bka = sb("bka", (128, 4), F32)
            bqs = sb("bqs", (128, 16), F32)
            sc.dma("sync", bqa[:], T["bq_a"], ct="ldb")
            sc.dma("sync", bka[:], T["bk_a"], ct="ldb")
            bvb = sb("bvb", (128, 512), F32)
            sc.dma("sync", bvb[:], T["bv_b"], ct="ldb")
            sc.v([bqa], [bqs], lambda e: e.tensor_scalar(
                bqs[:], bqa[:], 1.0 / math.sqrt(128.0), None, op0=ALU.mult))
            etot = sb("etot", (128, 4 * 32), F32)
            midx = ExitStack()
            cumb = midx.enter_context(self.nc.sbuf_tensor(
                tag + "cumb", [128, 4 * 2049], F32))
            with ExitStack() as axx:
                asb = lambda n, s, d: axx.enter_context(self.nc.sbuf_tensor(tag + n, list(s), d))
                yfm = asb("yfm", (128, 16 * 2048), BF16)
                for cch in range(16):
                    sc.dmat("scalar", yfm[:, cch * 2048:(cch + 1) * 2048],
                            T["yb16"][:, cch * 128:(cch + 1) * 128], ct="tpy2")
                _pjs = [axx.enter_context(self.nc.psum_tensor(tag + f"pj{i}", [128, 512], F32)) for i in range(3)]
                _pji = [0]
                def pj_next():
                    t = _pjs[_pji[0] % 3]; _pji[0] += 1
                    return t
                with ExitStack() as b1:
                    bsb = lambda n, s, d: b1.enter_context(self.nc.sbuf_tensor(tag + n, list(s), d))
                    s_wvT = bsb("wvT", (128, 16 * 512), BF16)
                    sc.dma("gpsimd", s_wvT.ap().rearrange("p (c o) -> p c o", o=512),
                           T["wvT_a"].rearrange("(c p) o -> p c o", p=128), ct="ldw")
                    wkc = bsb("wkc", (128, 16 * 128), BF16)
                    t1g = bsb("t1g", (128, 2048), F32)
                    gplg = bsb("gplg", (128, 2048), F32)
                    zrow2 = self.zero16
                    for g4 in range(4):
                        sc.dma("gpsimd", wkc.ap().rearrange("p (c o) -> p c o", o=128),
                               T["wkT_a"][:, g4 * 128:(g4 + 1) * 128]
                               .rearrange("(c p) o -> p c o", p=128), ct="ldw")
                        for ft in range(4):
                            pa = pj_next()
                            for f in range(16):
                                sc.mm([pa[:]], wkc[:, f * 128:(f + 1) * 128],
                                      yfm[:, f * 2048 + ft * 512:f * 2048 + ft * 512 + 512],
                                      start=(f == 0), stop=(f == 15))
                            sl = t1g[:, ft * 512:(ft + 1) * 512]
                            sc.s([pa, bka], [t1g], lambda e, pa=pa, sl=sl, g4=g4: e.activation(
                                sl, pa[:], AF.Sigmoid, bias=bka[:, g4:g4 + 1]))
                        sc.v([t1g], [t1g], lambda e: e.tensor_scalar(
                            t1g[:], t1g[:], 0.95, None, op0=ALU.min))
                        ksl = kfm[:, g4 * 2048:(g4 + 1) * 2048]
                        sc.v([t1g], [kfm], lambda e, ksl=ksl: e.tensor_copy(ksl, t1g[:]))
                        sc.s([t1g], [gplg], lambda e: e.activation(
                            gplg[:], t1g[:], AF.Ln, scale=-1.0, bias=self.ccol[:, 2:3]))
                        base = g4 * 2049
                        sc.v([cumb], [cumb], lambda e, base=base: e.memset(
                            cumb[:, base:base + 1], 0.0))
                        sc.v([gplg, zrow2], [cumb], lambda e, base=base: e.tensor_tensor_scan(
                            cumb[:, base + 1:base + 2049], gplg[:], zrow2[:], 0.0,
                            op0=ALU.add, op1=ALU.add))
                    for tt in range(16):
                        pa = pj_next()
                        for f in range(16):
                            sc.mm([pa[:]], yfm[:, f * 2048 + tt * 128:f * 2048 + tt * 128 + 128],
                                  s_wvT[:, f * 512:(f + 1) * 512], start=(f == 0),
                                  stop=(f == 15))
                        sl = vtm[:, tt * 512:(tt + 1) * 512]
                        sc.v([pa, bvb], [vtm], lambda e, pa=pa, sl=sl: e.tensor_tensor(
                            sl, pa[:], bvb[:], op=ALU.add))
                sc.barrier()
                with ExitStack() as b2:
                    bsb = lambda n, s, d: b2.enter_context(self.nc.sbuf_tensor(tag + n, list(s), d))
                    wqc = bsb("wqc", (128, 16 * 128), BF16)
                    qstg = bsb("qstg", (128, 2048), BF16)
                    argb = bsb("argb", (128, 2048), F32)
                    eargb = bsb("eargb", (128, 2048), F32)
                    for hq in range(16):
                        sc.dma("gpsimd", wqc.ap().rearrange("p (c o) -> p c o", o=128),
                               T["wqT_a"][:, hq * 128:(hq + 1) * 128]
                               .rearrange("(c p) o -> p c o", p=128), ct="ldw")
                        if hq % 4 == 0:
                            base = (hq // 4) * 2049
                            bview = cumb.ap()[:, base:base + 2048:64].rearrange(
                                "p k -> p k ()").to_broadcast([128, 32, 64])
                            csl = cumb.ap()[:, base + 1:base + 2049].rearrange(
                                "p (k j) -> p k j", j=64)
                            sc.v([cumb], [argb], lambda e, bview=bview, csl=csl: e.tensor_tensor(
                                argb.ap().rearrange("p (k j) -> p k j", j=64), bview, csl,
                                op=ALU.subtract))
                            sc.s([argb], [eargb], lambda e: e.activation(
                                eargb[:], argb[:], AF.Exp, scale=-1.0))
                        for ft in range(4):
                            pa = pj_next()
                            for f in range(16):
                                sc.mm([pa[:]], wqc[:, f * 128:(f + 1) * 128],
                                      yfm[:, f * 2048 + ft * 512:f * 2048 + ft * 512 + 512],
                                      start=(f == 0), stop=(f == 15))
                            sl = qstg[:, ft * 512:(ft + 1) * 512]
                            sc.s([pa, bqs], [qstg], lambda e, pa=pa, sl=sl, hq=hq: e.activation(
                                sl, pa[:], AF.Relu, scale=1.0 / math.sqrt(128.0),
                                bias=bqs[:, hq:hq + 1]))
                        sc.v([qstg, eargb], [qstg], lambda e: e.tensor_tensor(
                            qstg[:], qstg[:], eargb[:], op=ALU.mult))
                        sc.dma("sync", T["qfm_d"][:, hq * 2048:(hq + 1) * 2048], qstg[:], ct="stq")
            sc.barrier()
            with ExitStack() as bxx:
                bsb = lambda n, s, d: bxx.enter_context(self.nc.sbuf_tensor(tag + n, list(s), d))
                arg = bsb("arg", (128, 2048), F32)
                earg = bsb("earg", (128, 2048), F32)
                for g4 in range(4):
                    base = g4 * 2049
                    bview = cumb.ap()[:, base:base + 2048:64]
                    bview = bview.rearrange("p k -> p k ()").to_broadcast([128, 32, 64])
                    csl = cumb.ap()[:, base + 1:base + 2049].rearrange("p (k j) -> p k j", j=64)
                    sc.v([cumb], [arg], lambda e, bview=bview, csl=csl: e.tensor_tensor(
                        arg.ap().rearrange("p (k j) -> p k j", j=64), bview, csl,
                        op=ALU.subtract))
                    sc.s([arg], [earg], lambda e: e.activation(earg[:], arg[:], AF.Exp))
                    ksl = kfm[:, g4 * 2048:(g4 + 1) * 2048]
                    sc.v([earg, kfm], [kfm], lambda e, ksl=ksl: e.tensor_mul(ksl, ksl, earg[:]))
                    a_ = cumb.ap()[:, base + 64:base + 2049:64]
                    b_ = cumb.ap()[:, base:base + 2048:64]
                    esl = etot[:, g4 * 32:(g4 + 1) * 32]
                    sc.v([cumb], [etot], lambda e, a_=a_, b_=b_, esl=esl: e.tensor_tensor(
                        esl, a_, b_, op=ALU.subtract))
                    sc.s([etot], [etot], lambda e, esl=esl: e.activation(esl, esl, AF.Exp))
            if DEBUG:
                with ExitStack() as zx:
                    zt = zx.enter_context(self.nc.sbuf_tensor("dbcu", [128, 4 * 2049], F32))
                    sc.v([cumb], [zt], lambda e, zt=zt: e.tensor_copy(zt[:], cumb[:]))
                    sc.dma("sync", T["dbg_cumb"], zt[:], ct="dbg")
                sc.dma("sync", T["dbg_kfm"], kfm[:], ct="dbg")
                sc.dma("sync", T["dbg_vtm"], vtm[:], ct="dbg")
            sc.barrier()
            with ExitStack() as cxx:
                csb = lambda n, s, d: cxx.enter_context(self.nc.sbuf_tensor(tag + n, list(s), d))
                Srt = [csb(f"Srt{i}", (128, 4 * 528), F32) for i in range(2)]
                slot = lambda k: (Srt[(k % 8) // 4], ((k % 8) % 4) * 528)
                ktrs = [csb(f"ktr{i}", (128, 512), BF16) for i in range(2)]
                p_trs = [cxx.enter_context(self.nc.psum_tensor(tag + f"ptr{i}", [128, 512], BF16)) for i in range(2)]
                p_ms = [cxx.enter_context(self.nc.psum_tensor(tag + f"pm{i}", [128, 512], F32)) for i in range(2)]
                for i in range(2):
                    sc.v([], [Srt[i]], lambda e, i=i: e.memset(Srt[i][:], 0.0))
                svw = lambda t, o: t.ap()[:, o:o + 528].rearrange(
                    "p (g c) -> p g c", c=132)[:, :, 0:128]
                for k in range(32):
                    tc, oc = slot(k)
                    tn, on = slot(k + 1)
                    par = (k % 2) * 64
                    ktr, p_tr, p_m = ktrs[k % 2], p_trs[k % 2], p_ms[k % 2]
                    for g4 in range(4):
                        ksl = kfm[:, g4 * 2048 + k * 64: g4 * 2048 + k * 64 + 64]
                        sc.trans(p_tr[par:par + 64, g4 * 128:(g4 + 1) * 128], ksl, ident[:])
                    sc.s([p_tr], [ktr], lambda e, par=par, ktr=ktr, p_tr=p_tr: e.activation(
                        ktr[par:par + 64, :], p_tr[par:par + 64, :], AF.Copy))
                    for g4 in range(4):
                        vsl = vtm[(k % 2) * 64:(k % 2) * 64 + 64,
                                  (k // 2) * 512 + g4 * 128: (k // 2) * 512 + g4 * 128 + 128]
                        sc.mm([p_m[:, g4 * 128:(g4 + 1) * 128]],
                              ktr[par:par + 64, g4 * 128:(g4 + 1) * 128], vsl,
                              start=True, stop=True)
                    pmv = p_m.ap().rearrange("p (g c) -> p g c", c=128)
                    ebr = etot.ap()[:, k:128:32].rearrange("p g -> p g ()").to_broadcast([128, 4, 128])
                    sc.v([tc, p_m], [tn], lambda e, a=svw(tc, oc), o=svw(tn, on), pmv=pmv: e.tensor_tensor(
                        o, a, pmv, op=ALU.add))
                    sc.v([tn, etot], [tn], lambda e, o=svw(tn, on), ebr=ebr: e.tensor_tensor(
                        o, o, ebr, op=ALU.mult))
                    sc.v([cumb], [tn], lambda e, tn=tn, on=on, k=k: e.tensor_copy(
                        tn.ap()[:, on + 128:on + 4 * 132:132],
                        cumb.ap()[:, 64 * (k + 1):4 * 2049:2049]))
                    if k % 4 == 3:
                        tsrc, _ = slot(k)
                        sc.dma("sync", T["snap"][:, (k - 3) * 528:(k + 1) * 528],
                               tsrc[:], ct="stsn")
                sc.dma("sync", T["gla_in"], Srt[0][:, 0:528], ct="stgi")
                sc.g([T["gla_in"]], [T["gla_out"]], lambda e: e.collective_compute(
                    "AllGather", ALU.bypass, replica_groups=[list(range(NCORES))],
                    ins=[T["gla_in"]], outs=[T["gla_out"]]), inc=1, ct="cc")
            if DEBUG:
                self.copy_dram(T["qfm_d"], T["dbg_qfmd"], (128, 16 * 2048), BF16, tag="dcq")
                self.copy_dram(T["snap"], T["dbg_snap"], (128, 32 * 528), F32, tag="dcs")
                with ExitStack() as zx:
                    zt = zx.enter_context(self.nc.sbuf_tensor("dbgo", [128, 132], F32))
                    for r in range(NCORES * 4):
                        sc.dma("sync", zt[:], T["gla_out"][r * 128:(r + 1) * 128, :], ct="dbg")
                        sc.dma("sync", T["dbg_glao"][r * 128:(r + 1) * 128, :], zt[:], ct="dbg")
            midx.close()
            sc.barrier()
            with ExitStack() as dxx:
                dsb = lambda n, s, d: dxx.enter_context(self.nc.sbuf_tensor(tag + n, list(s), d))
                qfm = dsb("qfm", (128, 16 * 2048), BF16)
                for hq in range(16):
                    sc.dma("scalar", qfm[:, hq * 2048:(hq + 1) * 2048],
                           T["qfm_d"][:, hq * 2048:(hq + 1) * 2048], ct="ldq")
                lnwrb = dsb("lnwrb", (128, DM), F32)
                sc.dma("sync", lnwrb[:], T["wrb"], ct="ldc2s")
                lnbufs2 = [{"pr": dsb(f"lnpr{i}", (128, DM), F32),
                            "st": dsb(f"lnst{i}", (128, 8), F32),
                            "y16": dsb(f"lny16{i}", (128, DM), BF16),
                            "lgcol": dsb(f"lnlg{i}", (128, 16), F32),
                            "wrb": lnwrb} for i in range(2)]
                for i in range(2):
                    sc.v([], [lnbufs2[i]["lgcol"]], lambda e, i=i: e.memset(
                        lnbufs2[i]["lgcol"][:], 0.0))
                Sstall = dsb("Sstall", (128, 512), F32)
                rbufs = [dsb(f"rbuf{i}", (128, 4 * 132), F32) for i in range(1)]
                eg4a = dsb("eg4a", (128, 4), F32)
                eg4b = dsb("eg4b", (128, 4), F32)
                tS = dsb("tS", (128, 512), F32)
                sstv = Sstall.ap().rearrange("p (g c) -> p g c", c=128)
                sc.v([], [Sstall], lambda e: e.memset(Sstall[:], 0.0))
                for r in range(NCORES - 1):
                    rbuf = rbufs[r % len(rbufs)]
                    sc.dma("sync", rbuf[:], T["gla_out"][r * 128:(r + 1) * 128, :],
                           ct="ldgo")
                    sc.v([rbuf, C["core_meta"]], [eg4a], lambda e, r=r, rbuf=rbuf: e.tensor_scalar(
                        eg4a[:], rbuf.ap()[:, 128:4 * 132:132],
                        C["core_meta"][:, 5 + r:6 + r], None, op0=ALU.mult))
                    sc.s([eg4a], [eg4b], lambda e: e.activation(eg4b[:], eg4a[:], AF.Exp))
                    rv = rbuf.ap().rearrange("p (g c) -> p g c", c=132)[:, :, 0:128]
                    tv = tS.ap().rearrange("p (g c) -> p g c", c=128)
                    sc.v([rbuf, C["core_meta"]], [tS], lambda e, r=r, rv=rv, tv=tv: e.tensor_scalar(
                        tv, rv, C["core_meta"][:, 5 + r:6 + r], None, op0=ALU.mult))
                    ebr = eg4b.ap().rearrange("p g -> p g ()").to_broadcast([128, 4, 128])
                    sc.v([Sstall, eg4b], [Sstall], lambda e, ebr=ebr: e.tensor_tensor(
                        sstv, sstv, ebr, op=ALU.mult))
                    sc.v([Sstall, tS], [Sstall], lambda e: e.tensor_tensor(
                        Sstall[:], Sstall[:], tS[:], op=ALU.add))
                snlp = [dsb(f"snlp{i}", (128, 2 * 528), F32) for i in range(2)]
                Skas = [dsb(f"Ska{i}", (128, 512), F32) for i in range(2)]
                Sk16s = [dsb(f"Sk16{i}", (128, 512), BF16) for i in range(2)]
                egks = [dsb(f"egk{i}", (128, 4), F32) for i in range(2)]
                ptss = [dsb(f"pts{i}", (128, 1024), BF16) for i in range(2)]
                x1ts = [dsb(f"x1t{i}", (128, 2048), F32) for i in range(2)]
                x2gs = [dsb(f"x2g{i}", (128, 2048), F32) for i in range(1)]
                p_pts = [dxx.enter_context(self.nc.psum_tensor(tag + f"ppt{i}", [128, 1024], F32)) for i in range(2)]
                p_og = [dxx.enter_context(self.nc.psum_tensor(tag + f"pog{g4}", [128, 512], F32)) for g4 in range(4)]
                for k in range(32):
                    Skall, Sk16 = Skas[k % 2], Sk16s[k % 2]
                    egk, pts, p_pt = egks[k % 2], ptss[k % 2], p_pts[k % 2]
                    par = (k % 2) * 64
                    snl, soff = snlp[(k // 2) % 2], (k % 2) * 528
                    if k % 2 == 0:
                        sc.dma("sync", snl[:], T["snap"][:, k * 528:(k + 2) * 528],
                               ct="ldsn")
                    sc.s([snl], [egk], lambda e, snl=snl, soff=soff, egk=egk: e.activation(
                        egk[:], snl.ap()[:, soff + 128:soff + 4 * 132:132], AF.Exp))
                    skv = Skall.ap().rearrange("p (g c) -> p g c", c=128)
                    snv = snl.ap()[:, soff:soff + 528].rearrange(
                        "p (g c) -> p g c", c=132)[:, :, 0:128]
                    ebr = egk.ap().rearrange("p g -> p g ()").to_broadcast([128, 4, 128])
                    sc.v([Sstall, egk], [Skall], lambda e, skv=skv, ebr=ebr: e.tensor_tensor(
                        skv, sstv, ebr, op=ALU.mult))
                    sc.v([Skall, snl], [Skall], lambda e, skv=skv, snv=snv: e.tensor_tensor(
                        skv, skv, snv, op=ALU.add))
                    sc.s([Skall], [Sk16], lambda e, Skall=Skall, Sk16=Sk16: e.activation(
                        Sk16[:], Skall[:], AF.Copy))
                    for g4 in range(4):
                        ksl = kfm[:, g4 * 2048 + k * 64: g4 * 2048 + k * 64 + 64]
                        qv4 = qfm.ap().rearrange("p (h t) -> p h t", t=2048)[
                            :, 4 * g4:4 * g4 + 4, k * 64:k * 64 + 64]
                        sc.mm([p_pt[par:par + 64, g4 * 256:(g4 + 1) * 256]], ksl, qv4,
                              start=True, stop=True)
                    mbr = maskPT.ap()[par:par + 64, :].rearrange(
                        "p t -> p () t").to_broadcast([64, 4, 256])
                    ptv = p_pt.ap()[par:par + 64, :].rearrange("p (g t) -> p g t", t=256)
                    psv = pts.ap()[par:par + 64, :].rearrange("p (g t) -> p g t", t=256)
                    sc.v([p_pt, maskPT], [pts], lambda e, psv=psv, ptv=ptv, mbr=mbr: e.tensor_tensor(
                        psv, ptv, mbr, op=ALU.mult))
                    for g4 in range(4):
                        vsl = vtm[(k % 2) * 64:(k % 2) * 64 + 64,
                                  (k // 2) * 512 + g4 * 128: (k // 2) * 512 + g4 * 128 + 128]
                        for qh in range(4):
                            qsl = qfm[:, (4 * g4 + qh) * 2048 + k * 64:
                                      (4 * g4 + qh) * 2048 + k * 64 + 64]
                            reg = p_og[g4][(k % 2) * 64:(k % 2) * 64 + 64,
                                           qh * 128:(qh + 1) * 128]
                            sc.mm([reg], qsl, Sk16[:, g4 * 128:(g4 + 1) * 128],
                                  start=True, stop=False)
                            sc.mm([reg], pts[par:par + 64, g4 * 256 + qh * 64:
                                             g4 * 256 + (qh + 1) * 64], vsl,
                                  start=False, stop=True)
                    if k % 2 == 1:
                        j = k // 2
                        x1t, x2g = x1ts[j % 2], x2gs[j % len(x2gs)]
                        lnbufs2[j % 2]["y"] = x1t
                        sc.dma("scalar", x1t[:], T["xa"][j * 128:(j + 1) * 128, :],
                               ct="ldx1")
                        for g4 in range(4):
                            sc.v([p_og[g4], x1t], [x2g], lambda e, g4=g4, x1t=x1t, x2g=x2g: e.tensor_tensor(
                                x2g[:, g4 * 512:(g4 + 1) * 512], p_og[g4][:],
                                x1t[:, g4 * 512:(g4 + 1) * 512], op=ALU.add))
                        sc.dma("gpsimd",
                               T["xb"][j * 128:(j + 1) * 128, :], x2g[:],
                               ct="stx2" + str(j % 2))
                        self.ln_tile(x2g, j, lnbufs2[j % 2], logits=True)
                sc.v([lnbufs2[0]["lgcol"], lnbufs2[1]["lgcol"]],
                     [lnbufs2[0]["lgcol"]], lambda e: e.tensor_tensor(
                         lnbufs2[0]["lgcol"][:], lnbufs2[0]["lgcol"][:],
                         lnbufs2[1]["lgcol"][:], op=ALU.add))
                self.store_lg(lnbufs2[0]["lgcol"], dxx, p_reuse=p_pts[0])

    def copy_dram(self, src, dst, shape, dt):
        sc = self.sc
        tile = self.sb("cpy_t", (128, shape[1]), dt)
        for i in range(shape[0] // 128):
            sc.dma("sync", tile[:], src[i * 128:(i + 1) * 128, :])
            sc.dma("sync", dst[i * 128:(i + 1) * 128, :], tile[:])

    def load_consts(self):
        sc, T = self.sc, self.T
        self.consts = {}
        specs = [("tri_strict", (128, 128), F32), ("ones2", (128, 2), F32),
                 ("map2", (2, 128), F32), ("bmask", (128, 128), F32),
                 ("bcol1024", (128, 1), F32),
                 ("core_meta", (128, 16), F32),
                 ("selmy", (128, 16), F32), ("iota1", (128, 1), F32),
                 ("iota16", (128, 1), F32)]
        identf = self.sb("c_identf", (128, 128), F32)
        self.identf = identf
        sc.dma("sync", identf[:], T["ident128"])
        for nm, shp, dt in specs:
            t = self.sb("c_" + nm, shp, dt)
            self.consts[nm] = t
            if dt == F32:
                sc.dma("sync", t[:], T[nm])
            else:
                sc.dma("gpsimd", t[:], T[nm])
        self.zero16 = self.sb("zero16", (128, DM), BF16)
        sc.v([], [self.zero16], lambda e: e.memset(self.zero16[:], 0.0))
        self.ccol = self.sb("ccol", (128, 4), F32)
        sc.v([], [self.ccol], lambda e: e.memset(self.ccol[:, 0:1], 1e-5))
        sc.v([self.ccol], [self.ccol], lambda e: e.memset(self.ccol[:, 1:2], 1e-6))
        sc.v([self.ccol], [self.ccol], lambda e: e.memset(self.ccol[:, 2:3], 1.0))

    def ln_logits(self, src, call, logits=True):
        sc, T, C = self.sc, self.T, self.consts
        tag = f"ln{call}"
        px = ExitStack()
        psb = lambda n, s, d: px.enter_context(self.nc.sbuf_tensor(tag + n, list(s), d))
        xt = [psb(f"_x{i}", (128, DM), F32) for i in range(2)]
        yt = [psb(f"_y{i}", (128, DM), F32) for i in range(2)]
        y16 = [psb(f"_y16{i}", (128, DM), BF16) for i in range(2)]
        prs = [psb(f"_pr{i}", (128, DM), F32) for i in range(2)]
        sts = [psb(f"_st{i}", (128, 8), F32) for i in range(2)]
        lgcol = psb("_lg", (128, 16), F32)
        if logits:
            wrb = psb("_wrb", (128, DM), F32)
            sc.dma("sync", wrb[:], T["wrb"], ct="ldc")
        for i0 in range(0, TPC // 128, 2):
            for i in (i0, i0 + 1):
                b = i % 2
                sc.dma("sync", xt[b][:], src[i * 128:(i + 1) * 128, :], ct="ldx" + str(b))
            for i in (i0, i0 + 1):
                b = i % 2
                pr, st = prs[b], sts[b]
                sc.s([xt[b]], [pr, st], lambda e, b=b, pr=pr, st=st: e.activation(
                    pr[:], xt[b][:], AF.Square, accum_out=st[:, 1:2]))
                sc.s([xt[b]], [pr, st], lambda e, b=b, pr=pr, st=st: e.activation(
                    pr[:], xt[b][:], AF.Copy, accum_out=st[:, 0:1]))
                sc.v([st], [st], lambda e, st=st: e.tensor_scalar(
                    st[:, 2:3], st[:, 0:1], 1.0 / DM, None, op0=ALU.mult))
                sc.v([st], [st], lambda e, st=st: e.tensor_scalar(
                    st[:, 6:7], st[:, 2:3], st[:, 2:3], None, op0=ALU.mult))
                sc.v([st], [st], lambda e, st=st: e.tensor_scalar(
                    st[:, 3:4], st[:, 1:2], 1.0 / DM, st[:, 6:7], op0=ALU.mult, op1=ALU.subtract))
                sc.s([st], [st], lambda e, st=st: e.activation(st[:, 7:8], st[:, 3:4], AF.Ln, bias=self.ccol[:, 0:1]))
                sc.s([st], [st], lambda e, st=st: e.activation(st[:, 4:5], st[:, 7:8], AF.Exp, scale=-0.5))
                sc.v([xt[b], st], [yt[b]], lambda e, b=b, st=st: e.tensor_scalar(
                    yt[b][:], xt[b][:], st[:, 2:3], st[:, 4:5],
                    op0=ALU.subtract, op1=ALU.mult))
                sc.v([yt[b]], [y16[b]], lambda e, b=b: e.tensor_copy(y16[b][:], yt[b][:]))
                if logits:
                    sc.v([xt[b], st, wrb], [pr], lambda e, b=b, pr=pr, st=st: e.scalar_tensor_tensor(
                        pr[:], xt[b][:], st[:, 2:3], wrb[:], op0=ALU.subtract, op1=ALU.mult))
                    sc.s([pr, st], [pr, lgcol], lambda e, i=i, pr=pr, st=st: e.activation(
                        pr[:], pr[:], AF.Copy, scale=st[:, 4:5], accum_out=lgcol[:, i:i + 1]))
            for i in (i0, i0 + 1):
                b = i % 2
                if logits:
                    sc.dma("sync", T["ybuf"][i * 128:(i + 1) * 128, :], yt[b][:], ct="sty" + str(b))
                sc.dma("gpsimd", T["yb16"][i * 128:(i + 1) * 128, :], y16[b][:], ct="sty16" + str(b))
        if logits:
            self.store_lg(lgcol, px)
        px.close()
        if DEBUG and call == 0:
            self.copy_dram(T["ybuf"], T["dbg_y"], (TPC, DM), F32)

    def router(self, call):
        sc, T, C = self.sc, self.T, self.consts
        tag = f"rt{call}"
        rx = ExitStack()
        self.ps_scoped = lambda n, s: rx.enter_context(
            self.nc.psum_tensor(tag + n, list(s), F32))
        sc.g([T["lg_in"]], [T["lg_out"]], lambda e: e.collective_compute(
            "AllGather", ALU.bypass, replica_groups=[list(range(NCORES))],
            ins=[T["lg_in"]], outs=[T["lg_out"]]), inc=1, ct="cc")
        ll = self.sb(f"{tag}_ll", (128, 128), F32)
        sc.dma("sync", ll[:], T["lg_out"], ct="ldll")
        lohi = self.sb(f"{tag}_lh", (128, 8), F32)
        cnt = self.sb(f"{tag}_cnt", (128, 1), F32)
        fl = self.sb(f"{tag}_fl", (128, 128), F32)
        p_cnt = self.ps_scoped("_pc", (128, 2))
        ll2 = self.sb(f"{tag}_ll2", (128, 128), F32)
        mcol = self.sb(f"{tag}_mc", (128, 2), F32)
        dcol = self.sb(f"{tag}_dc", (128, 2), F32)
        sc.v([ll], [ll2], lambda e: e.tensor_scalar(
            ll2[:], ll[:], 2.0, None, op0=ALU.mult))
        sc.v([], [lohi], lambda e: e.memset(lohi[:, 0:1], -64.0))
        sc.v([lohi], [lohi], lambda e: e.memset(lohi[:, 1:2], 64.0))
        sc.v([lohi], [lohi], lambda e: e.memset(lohi[:, 2:3], 0.0))
        t2b = lohi.ap()[:, 2:3].to_broadcast([128, 2])
        for it in range(BIS):
            sc.v([ll2, lohi], [fl, cnt], lambda e: e.tensor_scalar(
                fl[:], ll2[:], lohi[:, 2:3], 0.0, op0=ALU.is_ge, op1=ALU.add,
                accum_out=cnt[:, 0:1]))
            sc.mm([p_cnt[:, 0:1]], C["bmask"][:], cnt[:], start=True, stop=True)
            sc.v([p_cnt], [mcol], lambda e: e.tensor_scalar(
                mcol[:, 0:1], p_cnt[:, 0:1], 1024.0, None, op0=ALU.is_ge))
            sc.v([mcol], [mcol], lambda e: e.tensor_scalar(
                mcol[:, 1:2], mcol[:, 0:1], -1.0, 1.0, op0=ALU.mult, op1=ALU.add))
            sc.v([lohi], [dcol], lambda e: e.scalar_tensor_tensor(
                dcol[:], t2b, -0.5, lohi[:, 0:2], op0=ALU.mult, op1=ALU.add))
            sc.v([mcol, dcol], [dcol], lambda e: e.tensor_mul(
                dcol[:], dcol[:], mcol[:]))
            sc.v([lohi, dcol], [lohi], lambda e: e.tensor_sub(
                lohi[:, 0:2], lohi[:, 0:2], dcol[:]))
            sc.v([lohi], [lohi], lambda e: e.tensor_add(
                lohi[:, 2:3], lohi[:, 0:1], lohi[:, 1:2]))
        sc.v([ll, lohi], [fl, cnt], lambda e: e.tensor_scalar(
            fl[:], ll[:], lohi[:, 0:1], 0.0, op0=ALU.is_ge, op1=ALU.add,
            accum_out=cnt[:, 0:1]))
        pref = self.sb(f"{tag}_pref", (128, 128), F32)
        zero1 = self.sb(f"{tag}_z1", (128, 128), F32)
        p_rb = self.ps_scoped("_prb", (128, 1))
        rbase = self.sb(f"{tag}_rb", (128, 1), F32)
        sc.v([], [zero1], lambda e: e.memset(zero1[:], 0.0))
        sc.v([fl, zero1], [pref], lambda e: e.tensor_tensor_scan(
            pref[:], fl[:], zero1[:], 0.0, op0=ALU.add, op1=ALU.add))
        sc.mm([p_rb[:]], C["tri_strict"][:], cnt[:], start=True, stop=True)
        sc.v([p_rb, C["bcol1024"]], [rbase], lambda e: e.tensor_tensor(
            rbase[:], p_rb[:], C["bcol1024"][:], op=ALU.add))
        sc.v([pref, rbase], [pref], lambda e: e.tensor_scalar(
            pref[:], pref[:], rbase[:, 0:1], None, op0=ALU.add))
        ex = self.sb(f"{tag}_ex", (128, 128), F32)
        neg = self.sb(f"{tag}_neg", (128, 1), F32)
        sc.v([lohi], [neg], lambda e: e.tensor_scalar(
            neg[:], lohi[:, 0:1], -1.0, None, op0=ALU.mult))
        sc.s([ll, neg], [ex], lambda e: e.activation(ex[:], ll[:], AF.Exp, bias=neg[:, 0:1]))
        sc.v([ex, fl], [ex], lambda e: e.tensor_mul(ex[:], ex[:], fl[:]))
        zc = self.sb(f"{tag}_zc", (128, 1), F32)
        sc.v([ex], [zc], lambda e: e.reduce_sum(zc[:], ex[:], axis=mybir.AxisListType.X))
        p_z = self.ps_scoped("_pz", (128, 2))
        z2 = self.sb(f"{tag}_z2", (128, 2), F32)
        sc.mm([p_z[0:2, 0:1]], C["ones2"][:], zc[:], start=True, stop=True)
        sc.v([p_z], [z2], lambda e: e.tensor_copy(z2[0:2, 0:1], p_z[0:2, 0:1]))
        sc.mm([p_z[:, 1:2]], C["map2"][0:2, :], z2[0:2, 0:1], start=True, stop=True)
        zinv = self.sb(f"{tag}_zi", (128, 1), F32)
        sc.v([p_z], [zinv], lambda e: e.reciprocal(zinv[:], p_z[:, 1:2]))
        sc.v([zinv], [zinv], lambda e: e.tensor_scalar(
            zinv[:], zinv[:], 0.5, None, op0=ALU.mult))
        rw = self.sb(f"{tag}_rw", (128, 128), F32)
        sc.v([ex, zinv], [rw], lambda e: e.tensor_scalar(
            rw[:], ex[:], zinv[:, 0:1], None, op0=ALU.mult))
        off = self.sb(f"{tag}_off", (128, 128), F32)
        BIG = 2048.0
        sc.v([pref], [off], lambda e: e.tensor_scalar(
            off[:], pref[:], -1.0 - BIG, None, op0=ALU.add))
        sc.v([off, fl], [off], lambda e: e.tensor_mul(off[:], off[:], fl[:]))
        sc.v([off], [off], lambda e: e.tensor_scalar(
            off[:], off[:], BIG, None, op0=ALU.add))
        p_my = self.ps_scoped("_pmy", (128, 32))
        sc.mm([p_my[:, 0:16]], off[:], C["selmy"][:], start=True, stop=True)
        sc.mm([p_my[:, 16:32]], rw[:], C["selmy"][:], start=True, stop=True)
        offi = self.sb(f"{tag}_offi", (128, 16), I32)
        rwmy = self.sb(f"{tag}_rwmy", (128, 16), F32)
        sc.v([p_my], [offi], lambda e: e.tensor_copy(offi[:], p_my[:, 0:16]))
        sc.v([p_my], [rwmy], lambda e: e.tensor_copy(rwmy[:], p_my[:, 16:32]))
        self.offi, self.rwmy = offi, rwmy
        rx.close()
        if DEBUG and call == 0:
            sc.dma("sync", T["dbg_lg"], T["lg_out"], ct="dbg")
            sc.dma("sync", T["dbg_flags"], fl[:], ct="dbg")
            sc.dma("sync", T["dbg_prefix"], pref[:], ct="dbg")
            sc.dma("sync", T["dbg_rw"], rw[:], ct="dbg")
            sc.dma("sync", T["dbg_offmy"], offi[:], ct="dbg")

    _lg_n = [0]

    def store_lg(self, lgcol, stack, p_reuse=None):
        sc, T = self.sc, self.T
        self._lg_n[0] += 1
        p = p_reuse if p_reuse is not None else stack.enter_context(
            self.nc.psum_tensor(f"plgT{self._lg_n[0]}", [128, 128], F32))
        lgT = stack.enter_context(self.nc.sbuf_tensor(
            f"slgT{self._lg_n[0]}", [16, 128], F32))
        sc.trans(p[0:16, 0:128], lgcol[:, 0:16], self.identf[:])
        sc.v([p], [lgT], lambda e, p=p, lgT=lgT: e.tensor_copy(lgT[:], p[0:16, 0:128]))
        sc.dma("sync", T["lg_in"].rearrange("r (i c) -> (r i) c", c=128), lgT[:],
               ct="stlg")

    def ln_tile(self, xsb, i, bufs, logits):
        sc, T = self.sc, self.T
        pr, st, y16 = bufs["pr"], bufs["st"], bufs["y16"]
        sc.s([xsb], [pr, st], lambda e: e.activation(
            pr[:], xsb[:], AF.Square, accum_out=st[:, 1:2]))
        sc.s([xsb], [pr, st], lambda e: e.activation(
            pr[:], xsb[:], AF.Copy, accum_out=st[:, 0:1]))
        sc.v([st], [st], lambda e: e.tensor_scalar(
            st[:, 2:3], st[:, 0:1], 1.0 / DM, None, op0=ALU.mult))
        sc.v([st], [st], lambda e: e.tensor_scalar(
            st[:, 6:7], st[:, 2:3], st[:, 2:3], None, op0=ALU.mult))
        sc.v([st], [st], lambda e: e.tensor_scalar(
            st[:, 3:4], st[:, 1:2], 1.0 / DM, st[:, 6:7], op0=ALU.mult,
            op1=ALU.subtract))
        sc.s([st], [st], lambda e: e.activation(
            st[:, 7:8], st[:, 3:4], AF.Ln, bias=self.ccol[:, 0:1]))
        sc.s([st], [st], lambda e: e.activation(
            st[:, 4:5], st[:, 7:8], AF.Exp, scale=-0.5))
        if logits:
            y, lgcol, wrb = bufs["y"], bufs["lgcol"], bufs["wrb"]
            sc.v([xsb, st], [y], lambda e: e.tensor_scalar(
                y[:], xsb[:], st[:, 2:3], st[:, 4:5],
                op0=ALU.subtract, op1=ALU.mult))
            sc.v([y], [y16], lambda e: e.tensor_copy(y16[:], y[:]))
            sc.v([xsb, st, wrb], [pr], lambda e: e.scalar_tensor_tensor(
                pr[:], xsb[:], st[:, 2:3], wrb[:], op0=ALU.subtract, op1=ALU.mult))
            sc.s([pr, st], [pr, lgcol], lambda e, i=i: e.activation(
                pr[:], pr[:], AF.Copy, scale=st[:, 4:5],
                accum_out=lgcol[:, i:i + 1]))
            sc.dma("gpsimd", T["ybuf"][i * 128:(i + 1) * 128, :], y[:], ct="flny")
        else:
            sc.v([xsb, st], [y16], lambda e: e.tensor_scalar(
                y16[:], xsb[:], st[:, 2:3], st[:, 4:5],
                op0=ALU.subtract, op1=ALU.mult))
        sc.dma("gpsimd", T["yb16"][i * 128:(i + 1) * 128, :], y16[:], ct="flns")

    def ysel(self, call):
        sc, T = self.sc, self.T
        for k in range(16):
            sc.dma("sync", T["ysel_in"][k * 128:(k + 1) * 128, :], self.zero16[:],
                   ct="zys")
        offi = self.offi
        with ExitStack() as px:
            ytiles = [px.enter_context(self.nc.sbuf_tensor(
                f"ys{call}_t{i}", [128, DM], BF16)) for i in range(4)]
            for k in range(16):
                ytile = ytiles[k % 4]
                sc.dma("sync", ytile[:], T["yb16"][k * 128:(k + 1) * 128, :], ct="ldyst")
                sc.op("gpsimd", [ytile, offi, T["ysel_in"]], [T["ysel_in"]],
                      lambda e, k=k, ytile=ytile: e.indirect_dma_start(
                          out=T["ysel_in"],
                          out_offset=IndirectOffsetOnAxis(offi[:, k:k + 1], 0),
                          in_=ytile[:], in_offset=None),
                      inc=16, ct="scat")
        sc.g([T["ysel_in"]], [T["ysel_out"]], lambda e: e.collective_compute(
            "AllReduce", ALU.add, replica_groups=[list(range(NCORES))],
            ins=[T["ysel_in"][0:2 * SEL, :]], outs=[T["ysel_out"]]), inc=1, ct="cc")
        if DEBUG and call == 0:
            self.copy_dram(T["ysel_out"], T["dbg_ysel"], (2 * SEL, DM), BF16, tag="dcy")

    def epilogue(self, x_src, x_dst, call, dbg):
        sc, T = self.sc, self.T
        tag = f"ep{call}"
        fuse_ln = (call == 0)
        with ExitStack() as px:
            sb = lambda n, s, d: px.enter_context(self.nc.sbuf_tensor(tag + n, list(s), d))
            NR = 4
            g16s = [sb(f"g16{i}", (128, DM), BF16) for i in range(NR)]
            xts = [sb(f"xt{i}", (128, DM), F32) for i in range(NR)]
            yts = [sb(f"yt{i}", (128, DM), F32) for i in range(NR)]
            x1s = [sb(f"x1{i}", (128, DM), F32) for i in range(NR)]
            if fuse_ln:
                lnb = [{"pr": sb(f"lpr{i}", (128, DM), F32),
                        "st": sb(f"lst{i}", (128, 8), F32),
                        "y16": sb(f"ly16{i}", (128, DM), BF16)} for i in range(NR)]
            offi, rwmy = self.offi, self.rwmy
            for k0 in range(0, 16, NR):
                for k in range(k0, k0 + NR):
                    b = k % NR
                    g16, xt, yt = g16s[b], xts[b], yts[b]
                    sc.op("gpsimd", [T["ffn_out"], offi], [g16],
                          lambda e, k=k, g16=g16: e.indirect_dma_start(
                              out=g16[:], out_offset=None,
                              in_=T["ffn_out"],
                              in_offset=IndirectOffsetOnAxis(offi[:, k:k + 1], 0)),
                          inc=16, ct="gx3" + str(b))
                    sc.dma("sync", xt[:], x_src[k * 128:(k + 1) * 128, :], ct="lex" + str(b))
                    sc.dma("sync", yt[:], T["ybuf"][k * 128:(k + 1) * 128, :], ct="ley" + str(b))
                for k in range(k0, k0 + NR):
                    b = k % NR
                    g16, xt, yt, x1 = g16s[b], xts[b], yts[b], x1s[b]
                    sc.v([g16, rwmy, xt], [x1], lambda e, k=k, g16=g16, xt=xt, x1=x1: e.scalar_tensor_tensor(
                        x1[:], g16[:], rwmy[:, k:k + 1], xt[:], op0=ALU.mult, op1=ALU.add))
                    sc.v([yt, x1], [x1], lambda e, yt=yt, x1=x1: e.scalar_tensor_tensor(
                        x1[:], yt[:], 0.5, x1[:], op0=ALU.mult, op1=ALU.add))
                    if fuse_ln:
                        self.ln_tile(x1, k, lnb[b], logits=False)
                for k in range(k0, k0 + NR):
                    b = k % NR
                    sc.dma("sync", x_dst[k * 128:(k + 1) * 128, :], x1s[b][:],
                           ct="sex" + str(b))
            if DEBUG and dbg:
                self.copy_dram(x_dst, T[dbg], (TPC, DM), F32, tag="dce" + str(call))

    _dc_n = [0]

    def copy_dram(self, src, dst, shape, dt, tag="dc"):
        sc = self.sc
        self._dc_n[0] += 1
        with ExitStack() as dcx:
            tile = dcx.enter_context(self.nc.sbuf_tensor(
                f"{tag}_t{self._dc_n[0]}", [128, shape[1]], dt))
            for i in range(shape[0] // 128):
                sc.dma("sync", tile[:], src[i * 128:(i + 1) * 128, :], ct="dbg")
                sc.dma("sync", dst[i * 128:(i + 1) * 128, :], tile[:], ct="dbg")


def kernel(**inputs):
    full_x = np.ascontiguousarray(inputs["L0_attn_hidden_states"])
    consts = host_constants()
    prog = Prog()
    nc = prog.build()

    freqs = np.ascontiguousarray(inputs["L0_ffn_freqs_cis"])
    cosT = np.ascontiguousarray(freqs[:, :, 0].T)
    sinT = np.ascontiguousarray(freqs[:, :, 1].T)
    cosx = np.concatenate([cosT, cosT], axis=0)
    cosx = np.concatenate([cosx, cosx], axis=1)
    sinx = np.concatenate([-sinT, sinT], axis=0)
    sinx = np.concatenate([sinx, sinx], axis=1)

    base = {
        "wqT_a": np.ascontiguousarray(inputs["L0_attn_Wq"].T),
        "wkT_a": np.ascontiguousarray(inputs["L0_attn_Wk"].T),
        "wvT_a": np.ascontiguousarray(inputs["L0_attn_Wv"].T),
        "bq_a": np.ascontiguousarray(inputs["L0_attn_bq"].reshape(16, 128).T),
        "bk_a": np.ascontiguousarray(inputs["L0_attn_bk"].reshape(4, 128).T),
        "bv_b": np.broadcast_to(inputs["L0_attn_bv"], (128, 512)).copy(),
        "wo_f": np.ascontiguousarray(inputs["L0_ffn_wo"]),
        "g1b": np.broadcast_to(inputs["L0_ffn_g1"], (128, DM)).copy(),
        "g2b": np.broadcast_to(inputs["L0_ffn_g2"], (128, DM)).copy(),
        "wrb": np.broadcast_to(inputs["L0_ffn_w_router"], (128, DM)).copy(),
        "maskT": np.ascontiguousarray(inputs["L0_ffn_mask"].T),
        "cosx": cosx, "sinx": sinx,
        "tri_strict": consts["tri_strict"], "ones2": consts["ones2"],
        "map2": consts["map2"], "bmask": consts["bmask"],
        "ident128": consts["ident128"],
        "bcol1024": consts["bcol1024"], "maskPT": consts["maskPT"],
        "iota1": np.arange(128, dtype=np.float32).reshape(128, 1),
        "iota16": (np.arange(128, dtype=np.float32) * 16).reshape(128, 1),
    }
    in_maps = []
    for c in range(NCORES):
        B, cb = c // 4, c % 4
        m = dict(base)
        m["x"] = np.ascontiguousarray(full_x[B, cb * TPC:(cb + 1) * TPC, :])
        m["wq_f"] = np.ascontiguousarray(inputs["L0_ffn_wq"][:, 2 * c * 128:(2 * c + 2) * 128])
        m["wk_f"] = np.ascontiguousarray(inputs["L0_ffn_wk"][:, 2 * c * 128:(2 * c + 2) * 128])
        m["wv_f"] = np.ascontiguousarray(inputs["L0_ffn_wv"][:, 2 * c * 128:(2 * c + 2) * 128])
        m["w1_f"] = np.ascontiguousarray(inputs["L0_ffn_w1"][:, c * 1024:(c + 1) * 1024])
        m["w3_f"] = np.ascontiguousarray(inputs["L0_ffn_w3"][:, c * 1024:(c + 1) * 1024])
        m["w2_f"] = np.ascontiguousarray(inputs["L0_ffn_w2"][c * 1024:(c + 1) * 1024, :])
        selmy = np.zeros((128, 16), np.float32)
        for k in range(16):
            selmy[64 * B + 16 * cb + k, k] = 1.0
        m["selmy"] = selmy
        meta = np.zeros((128, 16), np.float32)
        meta[:, 0] = c
        meta[:, 1] = B
        meta[:, 3] = 256 * c
        meta[:, 4] = 2 * c
        for r in range(NCORES - 1):
            meta[:, 5 + r] = 1.0 if (r // 4 == B and r % 4 < cb) else 0.0
        m["core_meta"] = meta
        in_maps.append(m)

    res = run_bass_kernel_spmd(nc, in_maps, core_ids=list(range(NCORES)))
    globals()["_last_res"] = res
    out = np.empty((2, SEQ, DM), np.float32)
    for c in range(NCORES):
        B, cb = c // 4, c % 4
        out[B, cb * TPC:(cb + 1) * TPC, :] = res.results[c]["out_x"]
    return out


# version 20
# speedup vs baseline: 1.0064x; 1.0064x over previous
import os
import math
import numpy as np
import ml_dtypes
import concourse.bass as bass
import concourse.mybir as mybir
from concourse.bass_utils import run_bass_kernel_spmd
from concourse.bass import IndirectOffsetOnAxis
from contextlib import ExitStack

F32 = mybir.dt.float32
BF16 = mybir.dt.bfloat16
I32 = mybir.dt.int32
AF = mybir.ActivationFunctionType
ALU = mybir.AluOpType

NCORES = 8
DM = 2048
TPC = 2048
SEQ = 8192
BIS = 36
SEL = 1024
FFNH = 8192
DEBUG = os.environ.get("MK_DEBUG", "0") == "1"
ONLY = os.environ.get("MK_ONLY", "")

ENGS = ("tensor", "vector", "scalar", "gpsimd", "sync")


class Sched:

    def __init__(self, nc, ctx):
        self.nc = nc
        self.ctx = ctx
        self.q = {e: [] for e in ENGS}
        self.sems = {}
        self.track = {}
        self.waited = {e: {} for e in ENGS}
        self.dma_cts = set()
        self.dma_max_waited = {}
        self.dma_selfwait = {}
        for e in ENGS:
            self.sems["ct_" + e] = [self.ctx.enter_context(nc.semaphore("ct_" + e)), 0]

    def sem(self, name):
        if name not in self.sems:
            self.sems[name] = [self.ctx.enter_context(self.nc.semaphore(name)), 0]
        return self.sems[name]

    def _key(self, t):
        h = getattr(t, "tensor", t)
        return getattr(h, "name", None) or id(h)

    def _need_waits(self, eng, ins, outs):
        need = {}
        for t in ins:
            rec = self.track.get(self._key(t))
            if rec and rec.get("w"):
                ct, val = rec["w"]
                need[ct] = max(need.get(ct, 0), val)
        for t in outs:
            rec = self.track.get(self._key(t))
            if rec:
                if rec.get("w"):
                    ct, val = rec["w"]
                    need[ct] = max(need.get(ct, 0), val)
                for ct, val in rec.get("r", {}).items():
                    need[ct] = max(need.get(ct, 0), val)
        out = []
        for ct, val in need.items():
            if val <= 0:
                continue
            if ct in self.dma_cts:
                val = self.sems[ct][1]
                self.dma_max_waited[ct] = max(self.dma_max_waited.get(ct, 0), val)
            if self.waited[eng].get(ct, 0) >= val:
                continue
            if ct == "ct_" + eng and eng == "tensor":
                continue
            out.append((ct, val))
            self.waited[eng][ct] = val
        return out

    def _mark(self, eng, ins, outs, ctname, val):
        for t in ins:
            rec = self.track.setdefault(self._key(t), {"w": None, "r": {}})
            r = rec["r"]
            r[ctname] = max(r.get(ctname, 0), val)
        for t in outs:
            self.track[self._key(t)] = {"w": (ctname, val), "r": {}}

    def op(self, eng, ins, outs, fn, inc=1, ct=None):
        ctname = ct or ("ct_" + eng)
        sem_entry = self.sem(ctname)
        pre_selfwait = None
        if inc >= 16:
            self.dma_cts.add(ctname)
            if self.dma_max_waited.get(ctname, 0) > self.dma_selfwait.get(ctname, 0):
                pre_selfwait = sem_entry[1]
                self.dma_selfwait[ctname] = sem_entry[1]
        waits = self._need_waits(eng, ins, outs)
        if pre_selfwait:
            waits = [(ctname, pre_selfwait)] + [w for w in waits if w[0] != ctname or w[1] > pre_selfwait]
            self.waited[eng][ctname] = max(self.waited[eng].get(ctname, 0), pre_selfwait)
        sem_entry[1] += inc
        val = sem_entry[1]
        self._mark(eng, ins, outs, ctname, val if inc else val + 1)
        sems = self.sems

        def run(e, waits=waits, fn=fn, sem=sem_entry[0], val=val, inc=inc):
            for wct, wval in waits:
                e.wait_ge(sems[wct][0], wval)
            instr = fn(e)
            if inc:
                instr.then_inc(sem, inc)
        self.q[eng].append(run)
        return val

    def raw(self, eng, fn):
        self.q[eng].append(fn)

    def barrier(self):
        snap = [(name, ent[1]) for name, ent in self.sems.items() if ent[1] > 0]
        sems = self.sems
        for e in ENGS:
            todo = [(n, v) for (n, v) in snap
                    if self.waited[e].get(n, 0) < v and not (n == "ct_" + e)]
            for n, v in todo:
                self.waited[e][n] = v

            def run(eng_obj, todo=todo):
                for n, v in todo:
                    eng_obj.wait_ge(sems[n][0], v)
            self.q[e].append(run)

    def flush(self, blk):
        for e in ENGS:
            fns = self.q[e]
            getattr(blk, e)(lambda eng, fns=fns: [fn(eng) for fn in fns])

    def dma(self, queue, out, in_, ct=None, **kw):
        ctname = ct or ("dma_" + queue)
        return self.op(queue, [in_], [out],
                       lambda e: e.dma_start(out=out, in_=in_, **kw), inc=16, ct=ctname)

    def dmat(self, queue, out, in_, ct=None):
        ctname = ct or ("dma_" + queue)
        return self.op(queue, [in_], [out],
                       lambda e: e.dma_start_transpose(out, in_), inc=16, ct=ctname)

    def mm(self, outs, lhsT, rhs, start, stop, inc=True):
        return self.op("tensor", [lhsT, rhs], outs if (stop or start) else [],
                       lambda e: e.matmul(outs[0], lhsT, rhs, start=start, stop=stop),
                       inc=1 if stop else 0)

    def trans(self, out_psum, in_sb, ident):
        return self.op("tensor", [in_sb, ident], [out_psum],
                       lambda e: e.transpose(out_psum, in_sb, ident))

    def v(self, ins, outs, fn):
        return self.op("vector", ins, outs, fn)

    def s(self, ins, outs, fn):
        return self.op("scalar", ins, outs, fn)

    def g(self, ins, outs, fn, inc=1, ct=None):
        return self.op("gpsimd", ins, outs, fn, inc=inc, ct=ct)


def _np_bf16(x):
    return x.astype(ml_dtypes.bfloat16)


def host_constants():
    c = {}
    t = np.zeros((128, 128), np.float32)
    for po in range(128):
        h = po // 64
        t[h * 64:po, po] = 1.0
    c["tri_strict"] = t
    o2 = np.zeros((128, 2), np.float32)
    o2[:64, 0] = 1.0; o2[64:, 1] = 1.0
    c["ones2"] = o2
    m2 = np.zeros((2, 128), np.float32)
    m2[0, :64] = 1.0; m2[1, 64:] = 1.0
    c["map2"] = m2
    bm = np.zeros((128, 128), np.float32)
    bm[:64, :64] = 1.0; bm[64:, 64:] = 1.0
    c["bmask"] = bm
    c["ident128"] = np.eye(128, dtype=np.float32)
    bc = np.zeros((128, 1), np.float32)
    bc[64:, 0] = float(SEL)
    c["bcol1024"] = bc
    mp = np.zeros((64, 256), np.float32)
    for h in range(4):
        for q in range(64):
            mp[: q + 1, h * 64 + q] = 1.0
    c["maskPT"] = np.concatenate([mp, mp], axis=0)
    return c


class Prog:
    def __init__(self):
        self.nc = bass.Bass("TRN2", target_bir_lowering=False, debug=False,
                            dynamic_dma_scratch_size=32768)
        self.T = {}
        self.ctx = ExitStack()

    def inp(self, name, shape, dt=F32):
        self.T[name] = self.nc.dram_tensor(name, list(shape), dt, kind="ExternalInput").ap()
        return self.T[name]

    def scr(self, name, shape, dt=F32):
        self.T[name] = self.nc.dram_tensor(name, list(shape), dt, kind="Internal").ap()
        return self.T[name]

    def outp(self, name, shape, dt=F32):
        self.T[name] = self.nc.dram_tensor(name, list(shape), dt, kind="ExternalOutput").ap()
        return self.T[name]

    def sb(self, name, shape, dt):
        return self.ctx.enter_context(self.nc.sbuf_tensor(name, list(shape), dt))

    def ps(self, name, shape, dt=F32):
        return self.ctx.enter_context(self.nc.psum_tensor(name, list(shape), dt))

    def declare(self):
        inp = self.inp
        inp("x", (TPC, DM))
        inp("wqT_a", (DM, DM)); inp("wkT_a", (DM, 512)); inp("wvT_a", (DM, 512))
        inp("bq_a", (128, 16)); inp("bk_a", (128, 4)); inp("bv_b", (128, 512))
        inp("wq_f", (DM, 256)); inp("wk_f", (DM, 256)); inp("wv_f", (DM, 256))
        inp("wo_f", (DM, DM))
        inp("w1_f", (DM, 1024)); inp("w3_f", (DM, 1024)); inp("w2_f", (1024, DM))
        inp("g1b", (128, DM)); inp("g2b", (128, DM)); inp("wrb", (128, DM))
        inp("maskT", (SEL, SEL))
        inp("cosx", (128, 2 * SEL)); inp("sinx", (128, 2 * SEL))
        inp("selmy", (128, 16)); inp("iota1", (128, 1)); inp("iota16", (128, 1))
        inp("tri_strict", (128, 128)); inp("ones2", (128, 2)); inp("map2", (2, 128))
        inp("bmask", (128, 128))
        inp("ident128", (128, 128)); inp("bcol1024", (128, 1)); inp("maskPT", (128, 256))
        inp("core_meta", (128, 16))
        self.outp("out_x", (TPC, DM))
        scr = self.scr
        scr("xa", (TPC, DM)); scr("xb", (TPC, DM))
        scr("ybuf", (TPC, DM)); scr("yb16", (TPC, DM), BF16)
        scr("lg_in", (1, TPC)); scr("lg_out", (NCORES, TPC))
        scr("off_my", (TPC, 1), I32)
        scr("rw_my", (TPC, 1))
        scr("ysel_in", (2 * SEL + 1, DM), BF16); scr("ysel_out", (2 * SEL, DM), BF16)
        scr("o_in", (256, 2 * SEL), BF16); scr("o_out", (DM, 2 * SEL), BF16)
        scr("h2_in", (256, DM), BF16); scr("h2_out", (2 * SEL, DM), BF16)
        scr("ffn_in", (2 * SEL + 1, DM), BF16); scr("ffn_out", (2 * SEL + 1, DM), BF16)
        scr("zrow", (4, SEL)); scr("snap", (128, 32 * 4 * 132))
        scr("qfm_d", (128, 16 * 2048), BF16); scr("x2_d", (256, DM), BF16)
        scr("h_d", (2 * SEL, DM), BF16)
        scr("gla_in", (128, 528)); scr("gla_out", (NCORES * 128, 528))
        if DEBUG:
            self.outp("dbg_y", (TPC, DM))
            self.outp("dbg_lg", (NCORES, TPC))
            self.outp("dbg_flags", (128, 128))
            self.outp("dbg_prefix", (128, 128))
            self.outp("dbg_rw", (128, 128))
            self.outp("dbg_offmy", (128, 16), I32)
            self.outp("dbg_ysel", (2 * SEL, DM), BF16)
            self.outp("dbg_x3", (2 * SEL, DM), BF16)
            self.outp("dbg_x1", (TPC, DM))
            self.outp("dbg_x2", (TPC, DM))
            self.outp("dbg_zrow", (4, SEL))
            self.outp("dbg_cumb", (128, 4 * 2049))
            self.outp("dbg_kfm", (128, 4 * 2048), BF16)
            self.outp("dbg_vtm", (128, 16 * 512), BF16)
            self.outp("dbg_qfmd", (128, 16 * 2048), BF16)
            self.outp("dbg_snap", (128, 32 * 528))
            self.outp("dbg_glao", (NCORES * 512, 132))
            self.outp("dbg_oout", (DM, 2 * SEL), BF16)
            self.outp("dbg_x2d", (256, DM), BF16)

    def build(self):
        self.declare()
        with self.ctx:
            blk = self.ctx.enter_context(self.nc.Block())
            sc = Sched(self.nc, self.ctx)
            self.sc = sc
            self.load_consts()
            sc.barrier()
            if ONLY == "gla":
                self.copy_dram(self.T["x"], self.T["xa"], (TPC, DM), F32, tag="cx")
                sc.barrier()
                self.ln_logits(src=self.T["xa"], call=2, logits=False)
                sc.barrier()
                self.gla()
                sc.barrier()
                self.copy_dram(self.T["xb"], self.T["out_x"], (TPC, DM), F32, tag="co")
                sc.barrier()
                sc.flush(blk)
                return self.nc
            self.ln_logits(src=self.T["x"], call=0)
            sc.barrier()
            self.router(call=0)
            sc.barrier()
            self.ysel(call=0)
            sc.barrier()
            self.tb(call=0)
            sc.barrier()
            self.epilogue(x_src=self.T["x"], x_dst=self.T["xa"], call=0, dbg="dbg_x1")
            sc.barrier()
            self.gla()
            sc.barrier()
            if DEBUG:
                self.copy_dram(self.T["xb"], self.T["dbg_x2"], (TPC, DM), F32)
            self.router(call=1)
            sc.barrier()
            self.ysel(call=1)
            sc.barrier()
            self.tb(call=1)
            sc.barrier()
            self.epilogue(x_src=self.T["xb"], x_dst=self.T["out_x"], call=1, dbg=None)
            sc.barrier()
            sc.flush(blk)
        return self.nc

    def tb(self, call):
        sc, T, C = self.sc, self.T, self.consts
        tag = f"tb{call}"
        with ExitStack() as px:
            sb = lambda n, s, d: px.enter_context(self.nc.sbuf_tensor(tag + n, list(s), d))
            ps = lambda n, s: px.enter_context(self.nc.psum_tensor(tag + n, list(s), F32))
            h2t = sb("h2t", (128, 2048), BF16)
            stq = sb("stq", (128, 8), F32)
            sqs = sb("sqs", (128, 2048), BF16)
            idxb = sb("idxb", (128, 4), I32)
            idxf = sb("idxf", (128, 4), F32)
            def mkpool(stk, nm, n=3, shape=(128, 512)):
                ts = [stk.enter_context(self.nc.psum_tensor(tag + nm + str(i),
                      list(shape), F32)) for i in range(n)]
                ii = [0]
                def nxt():
                    t = ts[ii[0] % n]; ii[0] += 1
                    return t
                return nxt
            with ExitStack() as ax:
                asb = lambda n, s, d: ax.enter_context(self.nc.sbuf_tensor(tag + n, list(s), d))
                qfm = asb("qfm", (128, 2 * 2048), BF16)
                kfm = asb("kfm", (128, 2 * 2048), BF16)
                vtm = asb("vtm", (128, 16 * 256), BF16)
                with ExitStack() as bx:
                    bsb = lambda n, s, d: bx.enter_context(self.nc.sbuf_tensor(tag + n, list(s), d))
                    pacc_next = mkpool(bx, "pqk")
                    g1b = bsb("g1b", (128, DM), BF16)
                    sc.dma("gpsimd", g1b[:], T["g1b"], ct="ldc2")
                    htis = [bsb(f"hti{r}", (128, DM), BF16) for r in range(2)]
                    htos = [bsb(f"hto{r}", (128, DM), BF16) for r in range(2)]
                    hsts = [bsb(f"hst{r}", (128, 4), F32) for r in range(2)]
                    hsqs = [bsb(f"hsq{r}", (128, DM), BF16) for r in range(2)]
                    for i0 in range(0, 16, 2):
                        for i in (i0, i0 + 1):
                            hti = htis[i % 2]
                            sc.dma("sync", hti[:],
                                   T["ysel_out"][i * 128:(i + 1) * 128, :],
                                   ct="ldh" + str(i % 2))
                        for i in (i0, i0 + 1):
                            hti, hto = htis[i % 2], htos[i % 2]
                            hst, hsq = hsts[i % 2], hsqs[i % 2]
                            sc.s([hti], [hsq, hst], lambda e, hti=hti, hsq=hsq, hst=hst: e.activation(
                                hsq[:], hti[:], AF.Square, accum_out=hst[:, 0:1]))
                            sc.s([hst], [hst], lambda e, hst=hst: e.activation(
                                hst[:, 1:2], hst[:, 0:1], AF.Ln, scale=1.0 / DM,
                                bias=self.ccol[:, 1:2]))
                            sc.s([hst], [hst], lambda e, hst=hst: e.activation(
                                hst[:, 2:3], hst[:, 1:2], AF.Exp, scale=-0.5))
                            sc.v([hti, hst, g1b], [hto], lambda e, hti=hti, hst=hst, hto=hto: e.scalar_tensor_tensor(
                                hto[:], hti[:], hst[:, 2:3], g1b[:], op0=ALU.mult, op1=ALU.mult))
                        for i in (i0, i0 + 1):
                            sc.dma("sync", T["h_d"][i * 128:(i + 1) * 128, :],
                                   htos[i % 2][:], ct="sthd" + str(i % 2))
                    yfm = bsb("yfm", (128, 16 * 2048), BF16)
                    for cch in range(16):
                        sc.dmat("scalar", yfm[:, cch * 2048:(cch + 1) * 2048],
                                T["h_d"][:, cch * 128:(cch + 1) * 128], ct="tpy")
                    s_wq = bsb("wq", (128, 16 * 256), BF16)
                    s_wk = bsb("wk", (128, 16 * 256), BF16)
                    s_wv = bsb("wv", (128, 16 * 256), BF16)
                    for nm, t in [("wq_f", s_wq), ("wk_f", s_wk), ("wv_f", s_wv)]:
                        sc.dma("gpsimd", t.ap().rearrange("p (c o) -> p c o", o=256),
                               T[nm].rearrange("(c p) o -> p c o", p=128), ct="ldw")
                    for dst, w, scale in [(qfm, s_wq, 1.0 / math.sqrt(128.0)),
                                          (kfm, s_wk, 1.0)]:
                        for h2 in range(2):
                            for ft in range(4):
                                pa = pacc_next()
                                for f in range(16):
                                    sc.mm([pa[:]], w[:, f * 256 + h2 * 128:f * 256 + h2 * 128 + 128],
                                          yfm[:, f * 2048 + ft * 512:f * 2048 + ft * 512 + 512],
                                          start=(f == 0), stop=(f == 15))
                                sl = dst[:, h2 * 2048 + ft * 512:h2 * 2048 + ft * 512 + 512]
                                sc.s([pa], [dst], lambda e, pa=pa, sl=sl, scale=scale: e.activation(
                                    sl, pa[:], AF.Copy, scale=scale))
                    for tt in range(16):
                        pa = pacc_next()
                        for f in range(16):
                            sc.mm([pa[:, 0:256]],
                                  yfm[:, f * 2048 + tt * 128:f * 2048 + tt * 128 + 128],
                                  s_wv[:, f * 256:(f + 1) * 256], start=(f == 0), stop=(f == 15))
                        sl = vtm[:, tt * 256:(tt + 1) * 256]
                        sc.s([pa], [vtm], lambda e, pa=pa, sl=sl: e.activation(sl, pa[:, 0:256], AF.Copy))
                sc.barrier()
                mt = asb("mt", (128, 8 * 1024), BF16)
                sc.dma("gpsimd", mt.ap().rearrange("p (c o) -> p c o", o=1024),
                       T["maskT"].rearrange("(c p) o -> p c o", p=128), ct="ldw")
                cosx = asb("cosx", (128, 2 * SEL), BF16)
                sinx = asb("sinx", (128, 2 * SEL), BF16)
                sc.dma("gpsimd", cosx[:], T["cosx"], ct="ldc2")
                sc.dma("gpsimd", sinx[:], T["sinx"], ct="ldc2")
                rot = asb("rot", (128, 2048), BF16)
                tmp = asb("tmp", (128, 2048), BF16)
                for t_ in (qfm, kfm):
                    for h in range(2):
                        sc.dma("sync", rot[0:64, :], t_[64:128, h * 2048:(h + 1) * 2048], ct="rsw")
                        sc.dma("sync", rot[64:128, :], t_[0:64, h * 2048:(h + 1) * 2048], ct="rsw")
                        sc.v([t_, cosx], [tmp], lambda e, t_=t_, h=h: e.tensor_mul(
                            tmp[:], t_[:, h * 2048:(h + 1) * 2048], cosx[:]))
                        sc.v([rot, sinx], [rot], lambda e: e.tensor_mul(
                            rot[:], rot[:], sinx[:]))
                        sc.v([tmp, rot], [t_], lambda e, t_=t_, h=h: e.tensor_add(
                            t_[:, h * 2048:(h + 1) * 2048], tmp[:], rot[:]))
                ptall = asb("ptall", (128, 8 * 1024), BF16)
                onesb = asb("ones1", (128, 1), BF16)
                sc.v([], [onesb], lambda e: e.memset(onesb[:], 1.0))
                otm = asb("otm", (128, 8 * 128), BF16)
                ofm = asb("ofm", (128, 2 * 2048), BF16)
                zcolv = asb("zcol", (128, 8), F32)
                p_sts = [ax.enter_context(self.nc.psum_tensor(tag + f"pst{i}", [128, 1024], F32))
                         for i in range(2)]
                p_zc = ax.enter_context(self.nc.psum_tensor(tag + "pzc", [128, 8], F32))
                p_oA = ax.enter_context(self.nc.psum_tensor(tag + "poA", [128, 512], F32))
                p_oB = ax.enter_context(self.nc.psum_tensor(tag + "poB", [128, 512], F32))
                for b in range(2):
                    for h in range(2):
                        for kc in range(8):
                            p_st = p_sts[kc % 2]
                            for qh in range(2):
                                sc.mm([p_st[:, qh * 512:(qh + 1) * 512]],
                                      kfm[:, h * 2048 + b * 1024 + kc * 128:
                                          h * 2048 + b * 1024 + kc * 128 + 128],
                                      qfm[:, h * 2048 + b * 1024 + qh * 512:
                                          h * 2048 + b * 1024 + qh * 512 + 512],
                                      start=True, stop=True)
                            if os.environ.get("MK_NOMASK", "0") != "1":
                                msl = mt[:, kc * 1024:(kc + 1) * 1024]
                                sc.v([mt, p_st], [p_st], lambda e, p_st=p_st, msl=msl:
                                     e.tensor_tensor(p_st[:], p_st[:], msl, op=ALU.add))
                            for qh in range(2):
                                sc.s([p_st], [ptall], lambda e, p_st=p_st, qh=qh, kc=kc:
                                     e.activation(
                                         ptall[:, kc * 1024 + qh * 512:kc * 1024 + qh * 512 + 512],
                                         p_st[:, qh * 512:(qh + 1) * 512], AF.Exp))
                        vs0 = (8 * b) * 256 + h * 128
                        for qq in range(8):
                            dst = p_oA if qq < 4 else p_oB
                            reg = dst[:, (qq % 4) * 128:(qq % 4) * 128 + 128]
                            for kc in range(8):
                                sc.mm([reg],
                                      ptall[:, kc * 1024 + qq * 128:kc * 1024 + qq * 128 + 128],
                                      vtm[:, vs0 + kc * 256: vs0 + kc * 256 + 128],
                                      start=(kc == 0), stop=(kc == 7))
                            for kc in range(8):
                                sc.mm([p_zc[:, qq:qq + 1]],
                                      ptall[:, kc * 1024 + qq * 128:kc * 1024 + qq * 128 + 128],
                                      onesb[:], start=(kc == 0), stop=(kc == 7))
                        sc.v([p_zc], [zcolv], lambda e: e.reciprocal(zcolv[:], p_zc[:]))
                        for qq in range(8):
                            srcp = (p_oA if qq < 4 else p_oB)[:, (qq % 4) * 128:(qq % 4) * 128 + 128]
                            sc.v([p_oA, p_oB, zcolv], [otm], lambda e, srcp=srcp, qq=qq:
                                 e.tensor_scalar(otm[:, qq * 128:(qq + 1) * 128], srcp,
                                                 zcolv[:, qq:qq + 1], None, op0=ALU.mult))
                        for qq in range(8):
                            sc.dmat("scalar", ofm[:, h * 2048 + b * 1024 + qq * 128:
                                                  h * 2048 + b * 1024 + qq * 128 + 128],
                                    otm[:, qq * 128:(qq + 1) * 128], ct="tpo")
                for h in range(2):
                    sc.dma("sync", T["o_in"][h * 128:(h + 1) * 128, :],
                           ofm[:, h * 2048:(h + 1) * 2048], ct="sto")
                sc.g([T["o_in"]], [T["o_out"]], lambda e: e.collective_compute(
                    "AllGather", ALU.bypass, replica_groups=[list(range(NCORES))],
                    ins=[T["o_in"]], outs=[T["o_out"]]), inc=1, ct="cc")
            sc.barrier()
            with ExitStack() as cx:
                csb = lambda n, s, d: cx.enter_context(self.nc.sbuf_tensor(tag + n, list(s), d))
                pacc_next = mkpool(cx, "pow")
                x2t = csb("x2t", (128, 2 * 2048), F32)
                g2b = csb("g2b", (128, DM), BF16)
                sc.dma("gpsimd", g2b[:], T["g2b"], ct="ldc2")
                s_wo = csb("wo", (128, 16 * 2048), BF16)
                sc.dma("gpsimd", s_wo.ap().rearrange("p (c o) -> p c o", o=2048),
                       T["wo_f"].rearrange("(c p) o -> p c o", p=128), ct="ldw")
                iosl = csb("iosl", (128, 16 * 128), BF16)
                ys_my = csb("ysmy", (128, 2 * 2048), BF16)
                o_view = T["o_out"].rearrange("r (c p) -> (r c) p", p=128)
                for tc in range(2):
                    sc.v([C["iota1"], C["core_meta"]], [idxf], lambda e, tc=tc: e.tensor_scalar(
                        idxf[:, 0:1], C["iota1"][:], C["core_meta"][:, 3:4], float(tc * 128),
                        op0=ALU.add, op1=ALU.add))
                    sc.v([idxf], [idxb], lambda e: e.tensor_copy(idxb[:, 0:1], idxf[:, 0:1]))
                    sc.op("gpsimd", [T["ysel_out"], idxb], [ys_my],
                          lambda e, tc=tc: e.indirect_dma_start(
                              out=ys_my[:, tc * 2048:(tc + 1) * 2048], out_offset=None,
                              in_=T["ysel_out"],
                              in_offset=IndirectOffsetOnAxis(idxb[:, 0:1], 0)),
                          inc=16, ct="gys")
                    for hc in range(16):
                        sc.v([C["iota16"], C["core_meta"]], [idxf], lambda e, hc=hc, tc=tc:
                             e.tensor_scalar(idxf[:, 1:2], C["iota16"][:],
                                             C["core_meta"][:, 4:5], float(tc + hc * 2048),
                                             op0=ALU.add, op1=ALU.add))
                        sc.v([idxf], [idxb], lambda e: e.tensor_copy(idxb[:, 1:2], idxf[:, 1:2]))
                        sc.op("gpsimd", [T["o_out"], idxb], [iosl],
                              lambda e, hc=hc: e.indirect_dma_start(
                                  out=iosl[:, hc * 128:(hc + 1) * 128], out_offset=None,
                                  in_=o_view,
                                  in_offset=IndirectOffsetOnAxis(idxb[:, 1:2], 0)),
                              inc=16, ct="gos")
                    for ft in range(4):
                        pa = pacc_next()
                        for hc in range(16):
                            sc.mm([pa[:]], iosl[:, hc * 128:(hc + 1) * 128],
                                  s_wo[:, hc * 2048 + ft * 512:hc * 2048 + ft * 512 + 512],
                                  start=(hc == 0), stop=(hc == 15))
                        ysl = ys_my[:, tc * 2048 + ft * 512: tc * 2048 + ft * 512 + 512]
                        x2sl = x2t[:, tc * 2048 + ft * 512: tc * 2048 + ft * 512 + 512]
                        sc.v([pa, ys_my], [x2t], lambda e, pa=pa, ysl=ysl, x2sl=x2sl: e.tensor_tensor(
                            x2sl, pa[:], ysl, op=ALU.add))
                    x2sl = x2t[:, tc * 2048:(tc + 1) * 2048]
                    sc.s([x2t], [sqs, stq], lambda e, x2sl=x2sl: e.activation(
                        sqs[:], x2sl, AF.Square, accum_out=stq[:, 0:1]))
                    sc.s([stq], [stq], lambda e: e.activation(
                        stq[:, 1:2], stq[:, 0:1], AF.Ln, scale=1.0 / DM, bias=self.ccol[:, 1:2]))
                    sc.s([stq], [stq], lambda e: e.activation(
                        stq[:, 2:3], stq[:, 1:2], AF.Exp, scale=-0.5))
                    sc.v([x2t, stq, g2b], [h2t], lambda e, x2sl=x2sl: e.scalar_tensor_tensor(
                        h2t[:], x2sl, stq[:, 2:3], g2b[:], op0=ALU.mult, op1=ALU.mult))
                    sc.dma("sync", T["h2_in"][tc * 128:(tc + 1) * 128, :], h2t[:], ct="sth")
                    sc.v([x2t], [sqs], lambda e, x2sl=x2sl: e.tensor_copy(sqs[:], x2sl))
                    sc.dma("sync", T["x2_d"][tc * 128:(tc + 1) * 128, :], sqs[:], ct="stx2d")
                sc.g([T["h2_in"]], [T["h2_out"]], lambda e: e.collective_compute(
                    "AllGather", ALU.bypass, replica_groups=[list(range(NCORES))],
                    ins=[T["h2_in"]], outs=[T["h2_out"]]), inc=1, ct="cc")
            sc.barrier()
            with ExitStack() as dx:
                dsb = lambda n, s, d: dx.enter_context(self.nc.sbuf_tensor(tag + n, list(s), d))
                pacc_next = mkpool(dx, "pff")
                s_w1 = dsb("w1", (128, 16 * 1024), BF16)
                s_w3 = dsb("w3", (128, 16 * 1024), BF16)
                s_w2 = dsb("w2", (128, 8 * 2048), BF16)
                sc.dma("gpsimd", s_w1.ap().rearrange("p (c o) -> p c o", o=1024),
                       T["w1_f"].rearrange("(c p) o -> p c o", p=128), ct="ldw")
                sc.dma("gpsimd", s_w3.ap().rearrange("p (c o) -> p c o", o=1024),
                       T["w3_f"].rearrange("(c p) o -> p c o", p=128), ct="ldw")
                sc.dma("gpsimd", s_w2.ap().rearrange("p (c o) -> p c o", o=2048),
                       T["w2_f"].rearrange("(c p) o -> p c o", p=128), ct="ldw")
                h2fm = dsb("h2fm", (128, 16 * 512), BF16)
                spr = dsb("spr", (128, 8 * 512), BF16)
                su = dsb("su", (128, 512), BF16)
                stg = dsb("stg", (128, 2048), BF16)
                for quar in range(4):
                    for cch in range(16):
                        sc.dmat("scalar", h2fm[:, cch * 512:(cch + 1) * 512],
                                T["h2_out"][quar * 512:(quar + 1) * 512,
                                            cch * 128:(cch + 1) * 128], ct="tph")
                    for fc in range(8):
                        pa = pacc_next()
                        for f in range(16):
                            sc.mm([pa[:]],
                                  s_w1[:, f * 1024 + fc * 128:f * 1024 + fc * 128 + 128],
                                  h2fm[:, f * 512:(f + 1) * 512],
                                  start=(f == 0), stop=(f == 15))
                        pb = pacc_next()
                        for f in range(16):
                            sc.mm([pb[:]],
                                  s_w3[:, f * 1024 + fc * 128:f * 1024 + fc * 128 + 128],
                                  h2fm[:, f * 512:(f + 1) * 512],
                                  start=(f == 0), stop=(f == 15))
                        sc.s([pa], [su], lambda e, pa=pa: e.activation(su[:], pa[:], AF.Silu))
                        psl = spr[:, fc * 512:(fc + 1) * 512]
                        sc.v([pb, su], [spr], lambda e, pb=pb, psl=psl: e.tensor_tensor(
                            psl, pb[:], su[:], op=ALU.mult))
                    for tt in range(4):
                        for ft in range(4):
                            pa = pacc_next()
                            for fc in range(8):
                                sc.mm([pa[:]],
                                      spr[:, fc * 512 + tt * 128:fc * 512 + tt * 128 + 128],
                                      s_w2[:, fc * 2048 + ft * 512:fc * 2048 + ft * 512 + 512],
                                      start=(fc == 0), stop=(fc == 7))
                            sc.s([pa], [stg], lambda e, pa=pa, ft=ft: e.activation(
                                stg[:, ft * 512:(ft + 1) * 512], pa[:], AF.Copy))
                        sc.dma("sync", T["ffn_in"][quar * 512 + tt * 128:
                                                   quar * 512 + tt * 128 + 128, :],
                               stg[:], ct="stf")
                for tc in range(2):
                    sc.dma("sync", stg[:], T["x2_d"][tc * 128:(tc + 1) * 128, :], ct="ldx2d")
                    sc.v([C["iota1"], C["core_meta"]], [idxf], lambda e, tc=tc: e.tensor_scalar(
                        idxf[:, 2:3], C["iota1"][:], C["core_meta"][:, 3:4], float(tc * 128),
                        op0=ALU.add, op1=ALU.add))
                    sc.v([idxf], [idxb], lambda e: e.tensor_copy(idxb[:, 2:3], idxf[:, 2:3]))
                    sc.op("gpsimd", [stg, idxb, T["ffn_in"]], [T["ffn_in"]],
                          lambda e: e.indirect_dma_start(
                              out=T["ffn_in"],
                              out_offset=IndirectOffsetOnAxis(idxb[:, 2:3], 0),
                              in_=stg[:], in_offset=None,
                              compute_op=ALU.add),
                          inc=16, ct="scax")
                sc.dma("sync", T["ffn_in"][2 * SEL:2 * SEL + 1, :],
                       self.zero16[0:1, :], ct="zft")
                sc.g([T["ffn_in"]], [T["ffn_out"]], lambda e: e.collective_compute(
                    "AllReduce", ALU.add, replica_groups=[list(range(NCORES))],
                    ins=[T["ffn_in"]], outs=[T["ffn_out"]]), inc=1, ct="cc")
            if DEBUG and call == 0:
                self.copy_dram(T["ffn_out"], T["dbg_x3"], (2 * SEL, DM), BF16, tag="dcx3")
                self.copy_dram(T["zrow"], T["dbg_zrow"], (4, SEL), F32, tag="dcz") if False else None
                with ExitStack() as zx:
                    zt = zx.enter_context(self.nc.sbuf_tensor("dbgzt", [4, SEL], F32))
                    sc.dma("sync", zt[:], T["zrow"], ct="dbg")
                    sc.dma("sync", T["dbg_zrow"], zt[:], ct="dbg")
                self.copy_dram(T["o_out"], T["dbg_oout"], (DM, 2 * SEL), BF16, tag="dco")
                with ExitStack() as zx:
                    zt = zx.enter_context(self.nc.sbuf_tensor("dbgx2t", [128, DM], BF16))
                    for i in range(2):
                        sc.dma("sync", zt[:], T["x2_d"][i * 128:(i + 1) * 128, :], ct="dbg")
                        sc.dma("sync", T["dbg_x2d"][i * 128:(i + 1) * 128, :], zt[:], ct="dbg")

    def gla(self):
        sc, T, C = self.sc, self.T, self.consts
        tag = "gl"
        with ExitStack() as px:
            sb = lambda n, s, d: px.enter_context(self.nc.sbuf_tensor(tag + n, list(s), d))
            ps = lambda n, s, d=F32: px.enter_context(self.nc.psum_tensor(tag + n, list(s), d))
            ident = sb("ident", (128, 128), BF16)
            maskPT = sb("maskPT", (128, 256), BF16)
            sc.dma("gpsimd", ident[:], T["ident128"], ct="ldc2")
            sc.dma("gpsimd", maskPT[:], T["maskPT"], ct="ldc2")
            kfm = sb("kfm", (128, 4 * 2048), BF16)
            vtm = sb("vtm", (128, 16 * 512), BF16)
            bqa = sb("bqa", (128, 16), F32)
            bka = sb("bka", (128, 4), F32)
            bqs = sb("bqs", (128, 16), F32)
            sc.dma("sync", bqa[:], T["bq_a"], ct="ldb")
            sc.dma("sync", bka[:], T["bk_a"], ct="ldb")
            bvb = sb("bvb", (128, 512), F32)
            sc.dma("sync", bvb[:], T["bv_b"], ct="ldb")
            sc.v([bqa], [bqs], lambda e: e.tensor_scalar(
                bqs[:], bqa[:], 1.0 / math.sqrt(128.0), None, op0=ALU.mult))
            etot = sb("etot", (128, 4 * 32), F32)
            midx = ExitStack()
            cumb = midx.enter_context(self.nc.sbuf_tensor(
                tag + "cumb", [128, 4 * 2049], F32))
            with ExitStack() as axx:
                asb = lambda n, s, d: axx.enter_context(self.nc.sbuf_tensor(tag + n, list(s), d))
                yfm = asb("yfm", (128, 16 * 2048), BF16)
                for cch in range(16):
                    sc.dmat("scalar", yfm[:, cch * 2048:(cch + 1) * 2048],
                            T["yb16"][:, cch * 128:(cch + 1) * 128], ct="tpy2")
                _pjs = [axx.enter_context(self.nc.psum_tensor(tag + f"pj{i}", [128, 512], F32)) for i in range(3)]
                _pji = [0]
                def pj_next():
                    t = _pjs[_pji[0] % 3]; _pji[0] += 1
                    return t
                with ExitStack() as b1:
                    bsb = lambda n, s, d: b1.enter_context(self.nc.sbuf_tensor(tag + n, list(s), d))
                    s_wvT = bsb("wvT", (128, 16 * 512), BF16)
                    sc.dma("gpsimd", s_wvT.ap().rearrange("p (c o) -> p c o", o=512),
                           T["wvT_a"].rearrange("(c p) o -> p c o", p=128), ct="ldw")
                    wkc = bsb("wkc", (128, 16 * 128), BF16)
                    t1g = bsb("t1g", (128, 2048), F32)
                    gplg = bsb("gplg", (128, 2048), F32)
                    zrow2 = self.zero16
                    for g4 in range(4):
                        sc.dma("gpsimd", wkc.ap().rearrange("p (c o) -> p c o", o=128),
                               T["wkT_a"][:, g4 * 128:(g4 + 1) * 128]
                               .rearrange("(c p) o -> p c o", p=128), ct="ldw")
                        for ft in range(4):
                            pa = pj_next()
                            for f in range(16):
                                sc.mm([pa[:]], wkc[:, f * 128:(f + 1) * 128],
                                      yfm[:, f * 2048 + ft * 512:f * 2048 + ft * 512 + 512],
                                      start=(f == 0), stop=(f == 15))
                            sl = t1g[:, ft * 512:(ft + 1) * 512]
                            sc.s([pa, bka], [t1g], lambda e, pa=pa, sl=sl, g4=g4: e.activation(
                                sl, pa[:], AF.Sigmoid, bias=bka[:, g4:g4 + 1]))
                        sc.v([t1g], [t1g], lambda e: e.tensor_scalar(
                            t1g[:], t1g[:], 0.95, None, op0=ALU.min))
                        ksl = kfm[:, g4 * 2048:(g4 + 1) * 2048]
                        sc.v([t1g], [kfm], lambda e, ksl=ksl: e.tensor_copy(ksl, t1g[:]))
                        sc.s([t1g], [gplg], lambda e: e.activation(
                            gplg[:], t1g[:], AF.Ln, scale=-1.0, bias=self.ccol[:, 2:3]))
                        base = g4 * 2049
                        sc.v([cumb], [cumb], lambda e, base=base: e.memset(
                            cumb[:, base:base + 1], 0.0))
                        sc.v([gplg, zrow2], [cumb], lambda e, base=base: e.tensor_tensor_scan(
                            cumb[:, base + 1:base + 2049], gplg[:], zrow2[:], 0.0,
                            op0=ALU.add, op1=ALU.add))
                    for tt in range(16):
                        pa = pj_next()
                        for f in range(16):
                            sc.mm([pa[:]], yfm[:, f * 2048 + tt * 128:f * 2048 + tt * 128 + 128],
                                  s_wvT[:, f * 512:(f + 1) * 512], start=(f == 0),
                                  stop=(f == 15))
                        sl = vtm[:, tt * 512:(tt + 1) * 512]
                        sc.v([pa, bvb], [vtm], lambda e, pa=pa, sl=sl: e.tensor_tensor(
                            sl, pa[:], bvb[:], op=ALU.add))
                sc.barrier()
                with ExitStack() as b2:
                    bsb = lambda n, s, d: b2.enter_context(self.nc.sbuf_tensor(tag + n, list(s), d))
                    wqc = bsb("wqc", (128, 16 * 128), BF16)
                    qstg = bsb("qstg", (128, 2048), BF16)
                    argb = bsb("argb", (128, 2048), F32)
                    eargb = bsb("eargb", (128, 2048), F32)
                    for hq in range(16):
                        sc.dma("gpsimd", wqc.ap().rearrange("p (c o) -> p c o", o=128),
                               T["wqT_a"][:, hq * 128:(hq + 1) * 128]
                               .rearrange("(c p) o -> p c o", p=128), ct="ldw")
                        if hq % 4 == 0:
                            base = (hq // 4) * 2049
                            bview = cumb.ap()[:, base:base + 2048:64].rearrange(
                                "p k -> p k ()").to_broadcast([128, 32, 64])
                            csl = cumb.ap()[:, base + 1:base + 2049].rearrange(
                                "p (k j) -> p k j", j=64)
                            sc.v([cumb], [argb], lambda e, bview=bview, csl=csl: e.tensor_tensor(
                                argb.ap().rearrange("p (k j) -> p k j", j=64), bview, csl,
                                op=ALU.subtract))
                            sc.s([argb], [eargb], lambda e: e.activation(
                                eargb[:], argb[:], AF.Exp, scale=-1.0))
                        for ft in range(4):
                            pa = pj_next()
                            for f in range(16):
                                sc.mm([pa[:]], wqc[:, f * 128:(f + 1) * 128],
                                      yfm[:, f * 2048 + ft * 512:f * 2048 + ft * 512 + 512],
                                      start=(f == 0), stop=(f == 15))
                            sl = qstg[:, ft * 512:(ft + 1) * 512]
                            sc.s([pa, bqs], [qstg], lambda e, pa=pa, sl=sl, hq=hq: e.activation(
                                sl, pa[:], AF.Relu, scale=1.0 / math.sqrt(128.0),
                                bias=bqs[:, hq:hq + 1]))
                        sc.v([qstg, eargb], [qstg], lambda e: e.tensor_tensor(
                            qstg[:], qstg[:], eargb[:], op=ALU.mult))
                        sc.dma("sync", T["qfm_d"][:, hq * 2048:(hq + 1) * 2048], qstg[:], ct="stq")
            sc.barrier()
            with ExitStack() as bxx:
                bsb = lambda n, s, d: bxx.enter_context(self.nc.sbuf_tensor(tag + n, list(s), d))
                arg = bsb("arg", (128, 2048), F32)
                earg = bsb("earg", (128, 2048), F32)
                for g4 in range(4):
                    base = g4 * 2049
                    bview = cumb.ap()[:, base:base + 2048:64]
                    bview = bview.rearrange("p k -> p k ()").to_broadcast([128, 32, 64])
                    csl = cumb.ap()[:, base + 1:base + 2049].rearrange("p (k j) -> p k j", j=64)
                    sc.v([cumb], [arg], lambda e, bview=bview, csl=csl: e.tensor_tensor(
                        arg.ap().rearrange("p (k j) -> p k j", j=64), bview, csl,
                        op=ALU.subtract))
                    sc.s([arg], [earg], lambda e: e.activation(earg[:], arg[:], AF.Exp))
                    ksl = kfm[:, g4 * 2048:(g4 + 1) * 2048]
                    sc.v([earg, kfm], [kfm], lambda e, ksl=ksl: e.tensor_mul(ksl, ksl, earg[:]))
                    a_ = cumb.ap()[:, base + 64:base + 2049:64]
                    b_ = cumb.ap()[:, base:base + 2048:64]
                    esl = etot[:, g4 * 32:(g4 + 1) * 32]
                    sc.v([cumb], [etot], lambda e, a_=a_, b_=b_, esl=esl: e.tensor_tensor(
                        esl, a_, b_, op=ALU.subtract))
                    sc.s([etot], [etot], lambda e, esl=esl: e.activation(esl, esl, AF.Exp))
            if DEBUG:
                with ExitStack() as zx:
                    zt = zx.enter_context(self.nc.sbuf_tensor("dbcu", [128, 4 * 2049], F32))
                    sc.v([cumb], [zt], lambda e, zt=zt: e.tensor_copy(zt[:], cumb[:]))
                    sc.dma("sync", T["dbg_cumb"], zt[:], ct="dbg")
                sc.dma("sync", T["dbg_kfm"], kfm[:], ct="dbg")
                sc.dma("sync", T["dbg_vtm"], vtm[:], ct="dbg")
            sc.barrier()
            with ExitStack() as cxx:
                csb = lambda n, s, d: cxx.enter_context(self.nc.sbuf_tensor(tag + n, list(s), d))
                Srt = [csb(f"Srt{i}", (128, 4 * 528), F32) for i in range(2)]
                slot = lambda k: (Srt[(k % 8) // 4], ((k % 8) % 4) * 528)
                ktrs = [csb(f"ktr{i}", (128, 512), BF16) for i in range(2)]
                p_trs = [cxx.enter_context(self.nc.psum_tensor(tag + f"ptr{i}", [128, 512], BF16)) for i in range(2)]
                p_ms = [cxx.enter_context(self.nc.psum_tensor(tag + f"pm{i}", [128, 512], F32)) for i in range(2)]
                for i in range(2):
                    sc.v([], [Srt[i]], lambda e, i=i: e.memset(Srt[i][:], 0.0))
                svw = lambda t, o: t.ap()[:, o:o + 528].rearrange(
                    "p (g c) -> p g c", c=132)[:, :, 0:128]
                for k in range(32):
                    tc, oc = slot(k)
                    tn, on = slot(k + 1)
                    par = (k % 2) * 64
                    ktr, p_tr, p_m = ktrs[k % 2], p_trs[k % 2], p_ms[k % 2]
                    for g4 in range(4):
                        ksl = kfm[:, g4 * 2048 + k * 64: g4 * 2048 + k * 64 + 64]
                        sc.trans(p_tr[par:par + 64, g4 * 128:(g4 + 1) * 128], ksl, ident[:])
                    sc.s([p_tr], [ktr], lambda e, par=par, ktr=ktr, p_tr=p_tr: e.activation(
                        ktr[par:par + 64, :], p_tr[par:par + 64, :], AF.Copy))
                    for g4 in range(4):
                        vsl = vtm[(k % 2) * 64:(k % 2) * 64 + 64,
                                  (k // 2) * 512 + g4 * 128: (k // 2) * 512 + g4 * 128 + 128]
                        sc.mm([p_m[:, g4 * 128:(g4 + 1) * 128]],
                              ktr[par:par + 64, g4 * 128:(g4 + 1) * 128], vsl,
                              start=True, stop=True)
                    pmv = p_m.ap().rearrange("p (g c) -> p g c", c=128)
                    ebr = etot.ap()[:, k:128:32].rearrange("p g -> p g ()").to_broadcast([128, 4, 128])
                    sc.v([tc, p_m], [tn], lambda e, a=svw(tc, oc), o=svw(tn, on), pmv=pmv: e.tensor_tensor(
                        o, a, pmv, op=ALU.add))
                    sc.v([tn, etot], [tn], lambda e, o=svw(tn, on), ebr=ebr: e.tensor_tensor(
                        o, o, ebr, op=ALU.mult))
                    sc.v([cumb], [tn], lambda e, tn=tn, on=on, k=k: e.tensor_copy(
                        tn.ap()[:, on + 128:on + 4 * 132:132],
                        cumb.ap()[:, 64 * (k + 1):4 * 2049:2049]))
                    if k % 4 == 3:
                        tsrc, _ = slot(k)
                        sc.dma("sync", T["snap"][:, (k - 3) * 528:(k + 1) * 528],
                               tsrc[:], ct="stsn")
                sc.dma("sync", T["gla_in"], Srt[0][:, 0:528], ct="stgi")
                sc.g([T["gla_in"]], [T["gla_out"]], lambda e: e.collective_compute(
                    "AllGather", ALU.bypass, replica_groups=[list(range(NCORES))],
                    ins=[T["gla_in"]], outs=[T["gla_out"]]), inc=1, ct="cc")
            if DEBUG:
                self.copy_dram(T["qfm_d"], T["dbg_qfmd"], (128, 16 * 2048), BF16, tag="dcq")
                self.copy_dram(T["snap"], T["dbg_snap"], (128, 32 * 528), F32, tag="dcs")
                with ExitStack() as zx:
                    zt = zx.enter_context(self.nc.sbuf_tensor("dbgo", [128, 132], F32))
                    for r in range(NCORES * 4):
                        sc.dma("sync", zt[:], T["gla_out"][r * 128:(r + 1) * 128, :], ct="dbg")
                        sc.dma("sync", T["dbg_glao"][r * 128:(r + 1) * 128, :], zt[:], ct="dbg")
            midx.close()
            sc.barrier()
            with ExitStack() as dxx:
                dsb = lambda n, s, d: dxx.enter_context(self.nc.sbuf_tensor(tag + n, list(s), d))
                qfm = dsb("qfm", (128, 16 * 2048), BF16)
                for hq in range(16):
                    sc.dma("scalar", qfm[:, hq * 2048:(hq + 1) * 2048],
                           T["qfm_d"][:, hq * 2048:(hq + 1) * 2048], ct="ldq")
                lnwrb = dsb("lnwrb", (128, DM), F32)
                sc.dma("sync", lnwrb[:], T["wrb"], ct="ldc2s")
                lnbufs2 = [{"pr": dsb(f"lnpr{i}", (128, DM), F32),
                            "st": dsb(f"lnst{i}", (128, 8), F32),
                            "y16": dsb(f"lny16{i}", (128, DM), BF16),
                            "lgcol": dsb(f"lnlg{i}", (128, 16), F32),
                            "wrb": lnwrb} for i in range(2)]
                for i in range(2):
                    sc.v([], [lnbufs2[i]["lgcol"]], lambda e, i=i: e.memset(
                        lnbufs2[i]["lgcol"][:], 0.0))
                Sstall = dsb("Sstall", (128, 512), F32)
                rbufs = [dsb(f"rbuf{i}", (128, 4 * 132), F32) for i in range(1)]
                eg4a = dsb("eg4a", (128, 4), F32)
                eg4b = dsb("eg4b", (128, 4), F32)
                tS = dsb("tS", (128, 512), F32)
                sstv = Sstall.ap().rearrange("p (g c) -> p g c", c=128)
                sc.v([], [Sstall], lambda e: e.memset(Sstall[:], 0.0))
                for r in range(NCORES - 1):
                    rbuf = rbufs[r % len(rbufs)]
                    sc.dma("sync", rbuf[:], T["gla_out"][r * 128:(r + 1) * 128, :],
                           ct="ldgo")
                    sc.v([rbuf, C["core_meta"]], [eg4a], lambda e, r=r, rbuf=rbuf: e.tensor_scalar(
                        eg4a[:], rbuf.ap()[:, 128:4 * 132:132],
                        C["core_meta"][:, 5 + r:6 + r], None, op0=ALU.mult))
                    sc.s([eg4a], [eg4b], lambda e: e.activation(eg4b[:], eg4a[:], AF.Exp))
                    rv = rbuf.ap().rearrange("p (g c) -> p g c", c=132)[:, :, 0:128]
                    tv = tS.ap().rearrange("p (g c) -> p g c", c=128)
                    sc.v([rbuf, C["core_meta"]], [tS], lambda e, r=r, rv=rv, tv=tv: e.tensor_scalar(
                        tv, rv, C["core_meta"][:, 5 + r:6 + r], None, op0=ALU.mult))
                    ebr = eg4b.ap().rearrange("p g -> p g ()").to_broadcast([128, 4, 128])
                    sc.v([Sstall, eg4b], [Sstall], lambda e, ebr=ebr: e.tensor_tensor(
                        sstv, sstv, ebr, op=ALU.mult))
                    sc.v([Sstall, tS], [Sstall], lambda e: e.tensor_tensor(
                        Sstall[:], Sstall[:], tS[:], op=ALU.add))
                snlp = [dsb(f"snlp{i}", (128, 2 * 528), F32) for i in range(2)]
                Skas = [dsb(f"Ska{i}", (128, 512), F32) for i in range(2)]
                Sk16s = [dsb(f"Sk16{i}", (128, 512), BF16) for i in range(2)]
                egks = [dsb(f"egk{i}", (128, 4), F32) for i in range(2)]
                ptss = [dsb(f"pts{i}", (128, 1024), BF16) for i in range(2)]
                x1ts = [dsb(f"x1t{i}", (128, 2048), F32) for i in range(2)]
                x2gs = [dsb(f"x2g{i}", (128, 2048), F32) for i in range(1)]
                p_pts = [dxx.enter_context(self.nc.psum_tensor(tag + f"ppt{i}", [128, 1024], F32)) for i in range(2)]
                p_og = [dxx.enter_context(self.nc.psum_tensor(tag + f"pog{g4}", [128, 512], F32)) for g4 in range(4)]
                for k in range(32):
                    Skall, Sk16 = Skas[k % 2], Sk16s[k % 2]
                    egk, pts, p_pt = egks[k % 2], ptss[k % 2], p_pts[k % 2]
                    par = (k % 2) * 64
                    snl, soff = snlp[(k // 2) % 2], (k % 2) * 528
                    if k % 2 == 0:
                        sc.dma("sync", snl[:], T["snap"][:, k * 528:(k + 2) * 528],
                               ct="ldsn")
                    sc.s([snl], [egk], lambda e, snl=snl, soff=soff, egk=egk: e.activation(
                        egk[:], snl.ap()[:, soff + 128:soff + 4 * 132:132], AF.Exp))
                    skv = Skall.ap().rearrange("p (g c) -> p g c", c=128)
                    snv = snl.ap()[:, soff:soff + 528].rearrange(
                        "p (g c) -> p g c", c=132)[:, :, 0:128]
                    ebr = egk.ap().rearrange("p g -> p g ()").to_broadcast([128, 4, 128])
                    sc.v([Sstall, egk], [Skall], lambda e, skv=skv, ebr=ebr: e.tensor_tensor(
                        skv, sstv, ebr, op=ALU.mult))
                    sc.v([Skall, snl], [Skall], lambda e, skv=skv, snv=snv: e.tensor_tensor(
                        skv, skv, snv, op=ALU.add))
                    sc.s([Skall], [Sk16], lambda e, Skall=Skall, Sk16=Sk16: e.activation(
                        Sk16[:], Skall[:], AF.Copy))
                    for g4 in range(4):
                        ksl = kfm[:, g4 * 2048 + k * 64: g4 * 2048 + k * 64 + 64]
                        qv4 = qfm.ap().rearrange("p (h t) -> p h t", t=2048)[
                            :, 4 * g4:4 * g4 + 4, k * 64:k * 64 + 64]
                        sc.mm([p_pt[par:par + 64, g4 * 256:(g4 + 1) * 256]], ksl, qv4,
                              start=True, stop=True)
                    mbr = maskPT.ap()[par:par + 64, :].rearrange(
                        "p t -> p () t").to_broadcast([64, 4, 256])
                    ptv = p_pt.ap()[par:par + 64, :].rearrange("p (g t) -> p g t", t=256)
                    psv = pts.ap()[par:par + 64, :].rearrange("p (g t) -> p g t", t=256)
                    sc.v([p_pt, maskPT], [pts], lambda e, psv=psv, ptv=ptv, mbr=mbr: e.tensor_tensor(
                        psv, ptv, mbr, op=ALU.mult))
                    for g4 in range(4):
                        vsl = vtm[(k % 2) * 64:(k % 2) * 64 + 64,
                                  (k // 2) * 512 + g4 * 128: (k // 2) * 512 + g4 * 128 + 128]
                        for qh in range(4):
                            qsl = qfm[:, (4 * g4 + qh) * 2048 + k * 64:
                                      (4 * g4 + qh) * 2048 + k * 64 + 64]
                            reg = p_og[g4][(k % 2) * 64:(k % 2) * 64 + 64,
                                           qh * 128:(qh + 1) * 128]
                            sc.mm([reg], qsl, Sk16[:, g4 * 128:(g4 + 1) * 128],
                                  start=True, stop=False)
                            sc.mm([reg], pts[par:par + 64, g4 * 256 + qh * 64:
                                             g4 * 256 + (qh + 1) * 64], vsl,
                                  start=False, stop=True)
                    if k % 2 == 0:
                        jn = k // 2
                        sc.dma("scalar", x1ts[jn % 2][:],
                               T["xa"][jn * 128:(jn + 1) * 128, :], ct="ldx1")
                    if k % 2 == 1:
                        j = k // 2
                        x1t, x2g = x1ts[j % 2], x2gs[j % len(x2gs)]
                        lnbufs2[j % 2]["y"] = x1t
                        for g4 in range(4):
                            sc.v([p_og[g4], x1t], [x2g], lambda e, g4=g4, x1t=x1t, x2g=x2g: e.tensor_tensor(
                                x2g[:, g4 * 512:(g4 + 1) * 512], p_og[g4][:],
                                x1t[:, g4 * 512:(g4 + 1) * 512], op=ALU.add))
                        sc.dma("gpsimd" if j % 2 else "sync",
                               T["xb"][j * 128:(j + 1) * 128, :], x2g[:],
                               ct="stx2" + str(j % 2))
                        self.ln_tile(x2g, j, lnbufs2[j % 2], logits=True)
                sc.v([lnbufs2[0]["lgcol"], lnbufs2[1]["lgcol"]],
                     [lnbufs2[0]["lgcol"]], lambda e: e.tensor_tensor(
                         lnbufs2[0]["lgcol"][:], lnbufs2[0]["lgcol"][:],
                         lnbufs2[1]["lgcol"][:], op=ALU.add))
                self.store_lg(lnbufs2[0]["lgcol"], dxx, p_reuse=p_pts[0])

    def copy_dram(self, src, dst, shape, dt):
        sc = self.sc
        tile = self.sb("cpy_t", (128, shape[1]), dt)
        for i in range(shape[0] // 128):
            sc.dma("sync", tile[:], src[i * 128:(i + 1) * 128, :])
            sc.dma("sync", dst[i * 128:(i + 1) * 128, :], tile[:])

    def load_consts(self):
        sc, T = self.sc, self.T
        self.consts = {}
        specs = [("tri_strict", (128, 128), F32), ("ones2", (128, 2), F32),
                 ("map2", (2, 128), F32), ("bmask", (128, 128), F32),
                 ("bcol1024", (128, 1), F32),
                 ("core_meta", (128, 16), F32),
                 ("selmy", (128, 16), F32), ("iota1", (128, 1), F32),
                 ("iota16", (128, 1), F32)]
        identf = self.sb("c_identf", (128, 128), F32)
        self.identf = identf
        sc.dma("sync", identf[:], T["ident128"])
        for nm, shp, dt in specs:
            t = self.sb("c_" + nm, shp, dt)
            self.consts[nm] = t
            if dt == F32:
                sc.dma("sync", t[:], T[nm])
            else:
                sc.dma("gpsimd", t[:], T[nm])
        self.zero16 = self.sb("zero16", (128, DM), BF16)
        sc.v([], [self.zero16], lambda e: e.memset(self.zero16[:], 0.0))
        self.ccol = self.sb("ccol", (128, 4), F32)
        sc.v([], [self.ccol], lambda e: e.memset(self.ccol[:, 0:1], 1e-5))
        sc.v([self.ccol], [self.ccol], lambda e: e.memset(self.ccol[:, 1:2], 1e-6))
        sc.v([self.ccol], [self.ccol], lambda e: e.memset(self.ccol[:, 2:3], 1.0))

    def ln_logits(self, src, call, logits=True):
        sc, T, C = self.sc, self.T, self.consts
        tag = f"ln{call}"
        px = ExitStack()
        psb = lambda n, s, d: px.enter_context(self.nc.sbuf_tensor(tag + n, list(s), d))
        xt = [psb(f"_x{i}", (128, DM), F32) for i in range(2)]
        yt = [psb(f"_y{i}", (128, DM), F32) for i in range(2)]
        y16 = [psb(f"_y16{i}", (128, DM), BF16) for i in range(2)]
        prs = [psb(f"_pr{i}", (128, DM), F32) for i in range(2)]
        sts = [psb(f"_st{i}", (128, 8), F32) for i in range(2)]
        lgcol = psb("_lg", (128, 16), F32)
        if logits:
            wrb = psb("_wrb", (128, DM), F32)
            sc.dma("sync", wrb[:], T["wrb"], ct="ldc")
        for i0 in range(0, TPC // 128, 2):
            for i in (i0, i0 + 1):
                b = i % 2
                sc.dma("sync", xt[b][:], src[i * 128:(i + 1) * 128, :], ct="ldx" + str(b))
            for i in (i0, i0 + 1):
                b = i % 2
                pr, st = prs[b], sts[b]
                sc.s([xt[b]], [pr, st], lambda e, b=b, pr=pr, st=st: e.activation(
                    pr[:], xt[b][:], AF.Square, accum_out=st[:, 1:2]))
                sc.s([xt[b]], [pr, st], lambda e, b=b, pr=pr, st=st: e.activation(
                    pr[:], xt[b][:], AF.Copy, accum_out=st[:, 0:1]))
                sc.v([st], [st], lambda e, st=st: e.tensor_scalar(
                    st[:, 2:3], st[:, 0:1], 1.0 / DM, None, op0=ALU.mult))
                sc.v([st], [st], lambda e, st=st: e.tensor_scalar(
                    st[:, 6:7], st[:, 2:3], st[:, 2:3], None, op0=ALU.mult))
                sc.v([st], [st], lambda e, st=st: e.tensor_scalar(
                    st[:, 3:4], st[:, 1:2], 1.0 / DM, st[:, 6:7], op0=ALU.mult, op1=ALU.subtract))
                sc.s([st], [st], lambda e, st=st: e.activation(st[:, 7:8], st[:, 3:4], AF.Ln, bias=self.ccol[:, 0:1]))
                sc.s([st], [st], lambda e, st=st: e.activation(st[:, 4:5], st[:, 7:8], AF.Exp, scale=-0.5))
                sc.v([xt[b], st], [yt[b]], lambda e, b=b, st=st: e.tensor_scalar(
                    yt[b][:], xt[b][:], st[:, 2:3], st[:, 4:5],
                    op0=ALU.subtract, op1=ALU.mult))
                sc.v([yt[b]], [y16[b]], lambda e, b=b: e.tensor_copy(y16[b][:], yt[b][:]))
                if logits:
                    sc.v([xt[b], st, wrb], [pr], lambda e, b=b, pr=pr, st=st: e.scalar_tensor_tensor(
                        pr[:], xt[b][:], st[:, 2:3], wrb[:], op0=ALU.subtract, op1=ALU.mult))
                    sc.s([pr, st], [pr, lgcol], lambda e, i=i, pr=pr, st=st: e.activation(
                        pr[:], pr[:], AF.Copy, scale=st[:, 4:5], accum_out=lgcol[:, i:i + 1]))
            for i in (i0, i0 + 1):
                b = i % 2
                if logits:
                    sc.dma("sync", T["ybuf"][i * 128:(i + 1) * 128, :], yt[b][:], ct="sty" + str(b))
                sc.dma("gpsimd", T["yb16"][i * 128:(i + 1) * 128, :], y16[b][:], ct="sty16" + str(b))
        if logits:
            self.store_lg(lgcol, px)
        px.close()
        if DEBUG and call == 0:
            self.copy_dram(T["ybuf"], T["dbg_y"], (TPC, DM), F32)

    def router(self, call):
        sc, T, C = self.sc, self.T, self.consts
        tag = f"rt{call}"
        rx = ExitStack()
        self.ps_scoped = lambda n, s: rx.enter_context(
            self.nc.psum_tensor(tag + n, list(s), F32))
        sc.g([T["lg_in"]], [T["lg_out"]], lambda e: e.collective_compute(
            "AllGather", ALU.bypass, replica_groups=[list(range(NCORES))],
            ins=[T["lg_in"]], outs=[T["lg_out"]]), inc=1, ct="cc")
        ll = self.sb(f"{tag}_ll", (128, 128), F32)
        sc.dma("sync", ll[:], T["lg_out"], ct="ldll")
        lohi = self.sb(f"{tag}_lh", (128, 8), F32)
        cnt = self.sb(f"{tag}_cnt", (128, 1), F32)
        fl = self.sb(f"{tag}_fl", (128, 128), F32)
        p_cnt = self.ps_scoped("_pc", (128, 2))
        ll2 = self.sb(f"{tag}_ll2", (128, 128), F32)
        mcol = self.sb(f"{tag}_mc", (128, 2), F32)
        dcol = self.sb(f"{tag}_dc", (128, 2), F32)
        sc.v([ll], [ll2], lambda e: e.tensor_scalar(
            ll2[:], ll[:], 2.0, None, op0=ALU.mult))
        sc.v([], [lohi], lambda e: e.memset(lohi[:, 0:1], -64.0))
        sc.v([lohi], [lohi], lambda e: e.memset(lohi[:, 1:2], 64.0))
        sc.v([lohi], [lohi], lambda e: e.memset(lohi[:, 2:3], 0.0))
        t2b = lohi.ap()[:, 2:3].to_broadcast([128, 2])
        for it in range(BIS):
            sc.v([ll2, lohi], [fl, cnt], lambda e: e.tensor_scalar(
                fl[:], ll2[:], lohi[:, 2:3], 0.0, op0=ALU.is_ge, op1=ALU.add,
                accum_out=cnt[:, 0:1]))
            sc.mm([p_cnt[:, 0:1]], C["bmask"][:], cnt[:], start=True, stop=True)
            sc.v([p_cnt], [mcol], lambda e: e.tensor_scalar(
                mcol[:, 0:1], p_cnt[:, 0:1], 1024.0, None, op0=ALU.is_ge))
            sc.v([mcol], [mcol], lambda e: e.tensor_scalar(
                mcol[:, 1:2], mcol[:, 0:1], -1.0, 1.0, op0=ALU.mult, op1=ALU.add))
            sc.v([lohi], [dcol], lambda e: e.scalar_tensor_tensor(
                dcol[:], t2b, -0.5, lohi[:, 0:2], op0=ALU.mult, op1=ALU.add))
            sc.v([mcol, dcol], [dcol], lambda e: e.tensor_mul(
                dcol[:], dcol[:], mcol[:]))
            sc.v([lohi, dcol], [lohi], lambda e: e.tensor_sub(
                lohi[:, 0:2], lohi[:, 0:2], dcol[:]))
            sc.v([lohi], [lohi], lambda e: e.tensor_add(
                lohi[:, 2:3], lohi[:, 0:1], lohi[:, 1:2]))
        sc.v([ll, lohi], [fl, cnt], lambda e: e.tensor_scalar(
            fl[:], ll[:], lohi[:, 0:1], 0.0, op0=ALU.is_ge, op1=ALU.add,
            accum_out=cnt[:, 0:1]))
        pref = self.sb(f"{tag}_pref", (128, 128), F32)
        zero1 = self.sb(f"{tag}_z1", (128, 128), F32)
        p_rb = self.ps_scoped("_prb", (128, 1))
        rbase = self.sb(f"{tag}_rb", (128, 1), F32)
        sc.v([], [zero1], lambda e: e.memset(zero1[:], 0.0))
        sc.v([fl, zero1], [pref], lambda e: e.tensor_tensor_scan(
            pref[:], fl[:], zero1[:], 0.0, op0=ALU.add, op1=ALU.add))
        sc.mm([p_rb[:]], C["tri_strict"][:], cnt[:], start=True, stop=True)
        sc.v([p_rb, C["bcol1024"]], [rbase], lambda e: e.tensor_tensor(
            rbase[:], p_rb[:], C["bcol1024"][:], op=ALU.add))
        sc.v([pref, rbase], [pref], lambda e: e.tensor_scalar(
            pref[:], pref[:], rbase[:, 0:1], None, op0=ALU.add))
        ex = self.sb(f"{tag}_ex", (128, 128), F32)
        neg = self.sb(f"{tag}_neg", (128, 1), F32)
        sc.v([lohi], [neg], lambda e: e.tensor_scalar(
            neg[:], lohi[:, 0:1], -1.0, None, op0=ALU.mult))
        sc.s([ll, neg], [ex], lambda e: e.activation(ex[:], ll[:], AF.Exp, bias=neg[:, 0:1]))
        sc.v([ex, fl], [ex], lambda e: e.tensor_mul(ex[:], ex[:], fl[:]))
        zc = self.sb(f"{tag}_zc", (128, 1), F32)
        sc.v([ex], [zc], lambda e: e.reduce_sum(zc[:], ex[:], axis=mybir.AxisListType.X))
        p_z = self.ps_scoped("_pz", (128, 2))
        z2 = self.sb(f"{tag}_z2", (128, 2), F32)
        sc.mm([p_z[0:2, 0:1]], C["ones2"][:], zc[:], start=True, stop=True)
        sc.v([p_z], [z2], lambda e: e.tensor_copy(z2[0:2, 0:1], p_z[0:2, 0:1]))
        sc.mm([p_z[:, 1:2]], C["map2"][0:2, :], z2[0:2, 0:1], start=True, stop=True)
        zinv = self.sb(f"{tag}_zi", (128, 1), F32)
        sc.v([p_z], [zinv], lambda e: e.reciprocal(zinv[:], p_z[:, 1:2]))
        sc.v([zinv], [zinv], lambda e: e.tensor_scalar(
            zinv[:], zinv[:], 0.5, None, op0=ALU.mult))
        rw = self.sb(f"{tag}_rw", (128, 128), F32)
        sc.v([ex, zinv], [rw], lambda e: e.tensor_scalar(
            rw[:], ex[:], zinv[:, 0:1], None, op0=ALU.mult))
        off = self.sb(f"{tag}_off", (128, 128), F32)
        BIG = 2048.0
        sc.v([pref], [off], lambda e: e.tensor_scalar(
            off[:], pref[:], -1.0 - BIG, None, op0=ALU.add))
        sc.v([off, fl], [off], lambda e: e.tensor_mul(off[:], off[:], fl[:]))
        sc.v([off], [off], lambda e: e.tensor_scalar(
            off[:], off[:], BIG, None, op0=ALU.add))
        p_my = self.ps_scoped("_pmy", (128, 32))
        sc.mm([p_my[:, 0:16]], off[:], C["selmy"][:], start=True, stop=True)
        sc.mm([p_my[:, 16:32]], rw[:], C["selmy"][:], start=True, stop=True)
        offi = self.sb(f"{tag}_offi", (128, 16), I32)
        rwmy = self.sb(f"{tag}_rwmy", (128, 16), F32)
        sc.v([p_my], [offi], lambda e: e.tensor_copy(offi[:], p_my[:, 0:16]))
        sc.v([p_my], [rwmy], lambda e: e.tensor_copy(rwmy[:], p_my[:, 16:32]))
        self.offi, self.rwmy = offi, rwmy
        rx.close()
        if DEBUG and call == 0:
            sc.dma("sync", T["dbg_lg"], T["lg_out"], ct="dbg")
            sc.dma("sync", T["dbg_flags"], fl[:], ct="dbg")
            sc.dma("sync", T["dbg_prefix"], pref[:], ct="dbg")
            sc.dma("sync", T["dbg_rw"], rw[:], ct="dbg")
            sc.dma("sync", T["dbg_offmy"], offi[:], ct="dbg")

    _lg_n = [0]

    def store_lg(self, lgcol, stack, p_reuse=None):
        sc, T = self.sc, self.T
        self._lg_n[0] += 1
        p = p_reuse if p_reuse is not None else stack.enter_context(
            self.nc.psum_tensor(f"plgT{self._lg_n[0]}", [128, 128], F32))
        lgT = stack.enter_context(self.nc.sbuf_tensor(
            f"slgT{self._lg_n[0]}", [16, 128], F32))
        sc.trans(p[0:16, 0:128], lgcol[:, 0:16], self.identf[:])
        sc.v([p], [lgT], lambda e, p=p, lgT=lgT: e.tensor_copy(lgT[:], p[0:16, 0:128]))
        sc.dma("sync", T["lg_in"].rearrange("r (i c) -> (r i) c", c=128), lgT[:],
               ct="stlg")

    def ln_tile(self, xsb, i, bufs, logits):
        sc, T = self.sc, self.T
        pr, st, y16 = bufs["pr"], bufs["st"], bufs["y16"]
        sc.s([xsb], [pr, st], lambda e: e.activation(
            pr[:], xsb[:], AF.Square, accum_out=st[:, 1:2]))
        sc.s([xsb], [pr, st], lambda e: e.activation(
            pr[:], xsb[:], AF.Copy, accum_out=st[:, 0:1]))
        sc.v([st], [st], lambda e: e.tensor_scalar(
            st[:, 2:3], st[:, 0:1], 1.0 / DM, None, op0=ALU.mult))
        sc.v([st], [st], lambda e: e.tensor_scalar(
            st[:, 6:7], st[:, 2:3], st[:, 2:3], None, op0=ALU.mult))
        sc.v([st], [st], lambda e: e.tensor_scalar(
            st[:, 3:4], st[:, 1:2], 1.0 / DM, st[:, 6:7], op0=ALU.mult,
            op1=ALU.subtract))
        sc.s([st], [st], lambda e: e.activation(
            st[:, 7:8], st[:, 3:4], AF.Ln, bias=self.ccol[:, 0:1]))
        sc.s([st], [st], lambda e: e.activation(
            st[:, 4:5], st[:, 7:8], AF.Exp, scale=-0.5))
        if logits:
            y, lgcol, wrb = bufs["y"], bufs["lgcol"], bufs["wrb"]
            sc.v([xsb, st], [y], lambda e: e.tensor_scalar(
                y[:], xsb[:], st[:, 2:3], st[:, 4:5],
                op0=ALU.subtract, op1=ALU.mult))
            sc.v([y], [y16], lambda e: e.tensor_copy(y16[:], y[:]))
            sc.v([xsb, st, wrb], [pr], lambda e: e.scalar_tensor_tensor(
                pr[:], xsb[:], st[:, 2:3], wrb[:], op0=ALU.subtract, op1=ALU.mult))
            sc.s([pr, st], [pr, lgcol], lambda e, i=i: e.activation(
                pr[:], pr[:], AF.Copy, scale=st[:, 4:5],
                accum_out=lgcol[:, i:i + 1]))
            sc.dma("sync", T["ybuf"][i * 128:(i + 1) * 128, :], y[:], ct="flny")
        else:
            sc.v([xsb, st], [y16], lambda e: e.tensor_scalar(
                y16[:], xsb[:], st[:, 2:3], st[:, 4:5],
                op0=ALU.subtract, op1=ALU.mult))
        sc.dma("gpsimd", T["yb16"][i * 128:(i + 1) * 128, :], y16[:], ct="flns")

    def ysel(self, call):
        sc, T = self.sc, self.T
        for k in range(16):
            sc.dma("sync", T["ysel_in"][k * 128:(k + 1) * 128, :], self.zero16[:],
                   ct="zys")
        offi = self.offi
        with ExitStack() as px:
            ytiles = [px.enter_context(self.nc.sbuf_tensor(
                f"ys{call}_t{i}", [128, DM], BF16)) for i in range(8)]
            for k in range(16):
                ytile = ytiles[k % 8]
                sc.dma("sync", ytile[:], T["yb16"][k * 128:(k + 1) * 128, :], ct="ldyst")
                sc.op("gpsimd", [ytile, offi, T["ysel_in"]], [T["ysel_in"]],
                      lambda e, k=k, ytile=ytile: e.indirect_dma_start(
                          out=T["ysel_in"],
                          out_offset=IndirectOffsetOnAxis(offi[:, k:k + 1], 0),
                          in_=ytile[:], in_offset=None),
                      inc=16, ct="scat")
        sc.g([T["ysel_in"]], [T["ysel_out"]], lambda e: e.collective_compute(
            "AllReduce", ALU.add, replica_groups=[list(range(NCORES))],
            ins=[T["ysel_in"][0:2 * SEL, :]], outs=[T["ysel_out"]]), inc=1, ct="cc")
        if DEBUG and call == 0:
            self.copy_dram(T["ysel_out"], T["dbg_ysel"], (2 * SEL, DM), BF16, tag="dcy")

    def epilogue(self, x_src, x_dst, call, dbg):
        sc, T = self.sc, self.T
        tag = f"ep{call}"
        fuse_ln = (call == 0)
        with ExitStack() as px:
            sb = lambda n, s, d: px.enter_context(self.nc.sbuf_tensor(tag + n, list(s), d))
            NR = 4
            g16s = [sb(f"g16{i}", (128, DM), BF16) for i in range(NR)]
            xts = [sb(f"xt{i}", (128, DM), F32) for i in range(NR)]
            yts = [sb(f"yt{i}", (128, DM), F32) for i in range(NR)]
            x1s = [sb(f"x1{i}", (128, DM), F32) for i in range(NR)]
            if fuse_ln:
                lnb = [{"pr": sb(f"lpr{i}", (128, DM), F32),
                        "st": sb(f"lst{i}", (128, 8), F32),
                        "y16": sb(f"ly16{i}", (128, DM), BF16)} for i in range(NR)]
            offi, rwmy = self.offi, self.rwmy
            for k0 in range(0, 16, NR):
                for k in range(k0, k0 + NR):
                    b = k % NR
                    g16, xt, yt = g16s[b], xts[b], yts[b]
                    sc.op("gpsimd", [T["ffn_out"], offi], [g16],
                          lambda e, k=k, g16=g16: e.indirect_dma_start(
                              out=g16[:], out_offset=None,
                              in_=T["ffn_out"],
                              in_offset=IndirectOffsetOnAxis(offi[:, k:k + 1], 0)),
                          inc=16, ct="gx3" + str(b))
                    sc.dma("sync", xt[:], x_src[k * 128:(k + 1) * 128, :], ct="lex" + str(b))
                    sc.dma("sync", yt[:], T["ybuf"][k * 128:(k + 1) * 128, :], ct="ley" + str(b))
                for k in range(k0, k0 + NR):
                    b = k % NR
                    g16, xt, yt, x1 = g16s[b], xts[b], yts[b], x1s[b]
                    sc.v([g16, rwmy, xt], [x1], lambda e, k=k, g16=g16, xt=xt, x1=x1: e.scalar_tensor_tensor(
                        x1[:], g16[:], rwmy[:, k:k + 1], xt[:], op0=ALU.mult, op1=ALU.add))
                    sc.v([yt, x1], [x1], lambda e, yt=yt, x1=x1: e.scalar_tensor_tensor(
                        x1[:], yt[:], 0.5, x1[:], op0=ALU.mult, op1=ALU.add))
                    if fuse_ln:
                        self.ln_tile(x1, k, lnb[b], logits=False)
                for k in range(k0, k0 + NR):
                    b = k % NR
                    sc.dma("sync", x_dst[k * 128:(k + 1) * 128, :], x1s[b][:],
                           ct="sex" + str(b))
            if DEBUG and dbg:
                self.copy_dram(x_dst, T[dbg], (TPC, DM), F32, tag="dce" + str(call))

    _dc_n = [0]

    def copy_dram(self, src, dst, shape, dt, tag="dc"):
        sc = self.sc
        self._dc_n[0] += 1
        with ExitStack() as dcx:
            tile = dcx.enter_context(self.nc.sbuf_tensor(
                f"{tag}_t{self._dc_n[0]}", [128, shape[1]], dt))
            for i in range(shape[0] // 128):
                sc.dma("sync", tile[:], src[i * 128:(i + 1) * 128, :], ct="dbg")
                sc.dma("sync", dst[i * 128:(i + 1) * 128, :], tile[:], ct="dbg")


def kernel(**inputs):
    full_x = np.ascontiguousarray(inputs["L0_attn_hidden_states"])
    consts = host_constants()
    prog = Prog()
    nc = prog.build()

    freqs = np.ascontiguousarray(inputs["L0_ffn_freqs_cis"])
    cosT = np.ascontiguousarray(freqs[:, :, 0].T)
    sinT = np.ascontiguousarray(freqs[:, :, 1].T)
    cosx = np.concatenate([cosT, cosT], axis=0)
    cosx = np.concatenate([cosx, cosx], axis=1)
    sinx = np.concatenate([-sinT, sinT], axis=0)
    sinx = np.concatenate([sinx, sinx], axis=1)

    base = {
        "wqT_a": np.ascontiguousarray(inputs["L0_attn_Wq"].T),
        "wkT_a": np.ascontiguousarray(inputs["L0_attn_Wk"].T),
        "wvT_a": np.ascontiguousarray(inputs["L0_attn_Wv"].T),
        "bq_a": np.ascontiguousarray(inputs["L0_attn_bq"].reshape(16, 128).T),
        "bk_a": np.ascontiguousarray(inputs["L0_attn_bk"].reshape(4, 128).T),
        "bv_b": np.broadcast_to(inputs["L0_attn_bv"], (128, 512)).copy(),
        "wo_f": np.ascontiguousarray(inputs["L0_ffn_wo"]),
        "g1b": np.broadcast_to(inputs["L0_ffn_g1"], (128, DM)).copy(),
        "g2b": np.broadcast_to(inputs["L0_ffn_g2"], (128, DM)).copy(),
        "wrb": np.broadcast_to(inputs["L0_ffn_w_router"], (128, DM)).copy(),
        "maskT": np.ascontiguousarray(inputs["L0_ffn_mask"].T),
        "cosx": cosx, "sinx": sinx,
        "tri_strict": consts["tri_strict"], "ones2": consts["ones2"],
        "map2": consts["map2"], "bmask": consts["bmask"],
        "ident128": consts["ident128"],
        "bcol1024": consts["bcol1024"], "maskPT": consts["maskPT"],
        "iota1": np.arange(128, dtype=np.float32).reshape(128, 1),
        "iota16": (np.arange(128, dtype=np.float32) * 16).reshape(128, 1),
    }
    in_maps = []
    for c in range(NCORES):
        B, cb = c // 4, c % 4
        m = dict(base)
        m["x"] = np.ascontiguousarray(full_x[B, cb * TPC:(cb + 1) * TPC, :])
        m["wq_f"] = np.ascontiguousarray(inputs["L0_ffn_wq"][:, 2 * c * 128:(2 * c + 2) * 128])
        m["wk_f"] = np.ascontiguousarray(inputs["L0_ffn_wk"][:, 2 * c * 128:(2 * c + 2) * 128])
        m["wv_f"] = np.ascontiguousarray(inputs["L0_ffn_wv"][:, 2 * c * 128:(2 * c + 2) * 128])
        m["w1_f"] = np.ascontiguousarray(inputs["L0_ffn_w1"][:, c * 1024:(c + 1) * 1024])
        m["w3_f"] = np.ascontiguousarray(inputs["L0_ffn_w3"][:, c * 1024:(c + 1) * 1024])
        m["w2_f"] = np.ascontiguousarray(inputs["L0_ffn_w2"][c * 1024:(c + 1) * 1024, :])
        selmy = np.zeros((128, 16), np.float32)
        for k in range(16):
            selmy[64 * B + 16 * cb + k, k] = 1.0
        m["selmy"] = selmy
        meta = np.zeros((128, 16), np.float32)
        meta[:, 0] = c
        meta[:, 1] = B
        meta[:, 3] = 256 * c
        meta[:, 4] = 2 * c
        for r in range(NCORES - 1):
            meta[:, 5 + r] = 1.0 if (r // 4 == B and r % 4 < cb) else 0.0
        m["core_meta"] = meta
        in_maps.append(m)

    res = run_bass_kernel_spmd(nc, in_maps, core_ids=list(range(NCORES)))
    globals()["_last_res"] = res
    out = np.empty((2, SEQ, DM), np.float32)
    for c in range(NCORES):
        B, cb = c // 4, c % 4
        out[B, cb * TPC:(cb + 1) * TPC, :] = res.results[c]["out_x"]
    return out


# version 21 (final)
# speedup vs baseline: 1.0143x; 1.0079x over previous
import os
import math
import numpy as np
import ml_dtypes
import concourse.bass as bass
import concourse.mybir as mybir
from concourse.bass_utils import run_bass_kernel_spmd
from concourse.bass import IndirectOffsetOnAxis
from contextlib import ExitStack

F32 = mybir.dt.float32
BF16 = mybir.dt.bfloat16
I32 = mybir.dt.int32
AF = mybir.ActivationFunctionType
ALU = mybir.AluOpType

NCORES = 8
DM = 2048
TPC = 2048
SEQ = 8192
BIS = 36
SEL = 1024
FFNH = 8192
DEBUG = os.environ.get("MK_DEBUG", "0") == "1"
ONLY = os.environ.get("MK_ONLY", "")

ENGS = ("tensor", "vector", "scalar", "gpsimd", "sync")


class Sched:

    def __init__(self, nc, ctx):
        self.nc = nc
        self.ctx = ctx
        self.q = {e: [] for e in ENGS}
        self.sems = {}
        self.track = {}
        self.waited = {e: {} for e in ENGS}
        self.dma_cts = set()
        self.dma_max_waited = {}
        self.dma_selfwait = {}
        for e in ENGS:
            self.sems["ct_" + e] = [self.ctx.enter_context(nc.semaphore("ct_" + e)), 0]

    def sem(self, name):
        if name not in self.sems:
            self.sems[name] = [self.ctx.enter_context(self.nc.semaphore(name)), 0]
        return self.sems[name]

    def _key(self, t):
        h = getattr(t, "tensor", t)
        return getattr(h, "name", None) or id(h)

    def _need_waits(self, eng, ins, outs):
        need = {}
        for t in ins:
            rec = self.track.get(self._key(t))
            if rec and rec.get("w"):
                ct, val = rec["w"]
                need[ct] = max(need.get(ct, 0), val)
        for t in outs:
            rec = self.track.get(self._key(t))
            if rec:
                if rec.get("w"):
                    ct, val = rec["w"]
                    need[ct] = max(need.get(ct, 0), val)
                for ct, val in rec.get("r", {}).items():
                    need[ct] = max(need.get(ct, 0), val)
        out = []
        for ct, val in need.items():
            if val <= 0:
                continue
            if ct in self.dma_cts:
                val = self.sems[ct][1]
                self.dma_max_waited[ct] = max(self.dma_max_waited.get(ct, 0), val)
            if self.waited[eng].get(ct, 0) >= val:
                continue
            if ct == "ct_" + eng and eng == "tensor":
                continue
            out.append((ct, val))
            self.waited[eng][ct] = val
        return out

    def _mark(self, eng, ins, outs, ctname, val):
        for t in ins:
            rec = self.track.setdefault(self._key(t), {"w": None, "r": {}})
            r = rec["r"]
            r[ctname] = max(r.get(ctname, 0), val)
        for t in outs:
            self.track[self._key(t)] = {"w": (ctname, val), "r": {}}

    def op(self, eng, ins, outs, fn, inc=1, ct=None):
        ctname = ct or ("ct_" + eng)
        sem_entry = self.sem(ctname)
        pre_selfwait = None
        if inc >= 16:
            self.dma_cts.add(ctname)
            if self.dma_max_waited.get(ctname, 0) > self.dma_selfwait.get(ctname, 0):
                pre_selfwait = sem_entry[1]
                self.dma_selfwait[ctname] = sem_entry[1]
        waits = self._need_waits(eng, ins, outs)
        if pre_selfwait:
            waits = [(ctname, pre_selfwait)] + [w for w in waits if w[0] != ctname or w[1] > pre_selfwait]
            self.waited[eng][ctname] = max(self.waited[eng].get(ctname, 0), pre_selfwait)
        sem_entry[1] += inc
        val = sem_entry[1]
        self._mark(eng, ins, outs, ctname, val if inc else val + 1)
        sems = self.sems

        def run(e, waits=waits, fn=fn, sem=sem_entry[0], val=val, inc=inc):
            for wct, wval in waits:
                e.wait_ge(sems[wct][0], wval)
            instr = fn(e)
            if inc:
                instr.then_inc(sem, inc)
        self.q[eng].append(run)
        return val

    def raw(self, eng, fn):
        self.q[eng].append(fn)

    def barrier(self):
        snap = [(name, ent[1]) for name, ent in self.sems.items() if ent[1] > 0]
        sems = self.sems
        for e in ENGS:
            todo = [(n, v) for (n, v) in snap
                    if self.waited[e].get(n, 0) < v and not (n == "ct_" + e)]
            for n, v in todo:
                self.waited[e][n] = v

            def run(eng_obj, todo=todo):
                for n, v in todo:
                    eng_obj.wait_ge(sems[n][0], v)
            self.q[e].append(run)

    def flush(self, blk):
        for e in ENGS:
            fns = self.q[e]
            getattr(blk, e)(lambda eng, fns=fns: [fn(eng) for fn in fns])

    def dma(self, queue, out, in_, ct=None, **kw):
        ctname = ct or ("dma_" + queue)
        return self.op(queue, [in_], [out],
                       lambda e: e.dma_start(out=out, in_=in_, **kw), inc=16, ct=ctname)

    def dmat(self, queue, out, in_, ct=None):
        ctname = ct or ("dma_" + queue)
        return self.op(queue, [in_], [out],
                       lambda e: e.dma_start_transpose(out, in_), inc=16, ct=ctname)

    def mm(self, outs, lhsT, rhs, start, stop, inc=True):
        return self.op("tensor", [lhsT, rhs], outs if (stop or start) else [],
                       lambda e: e.matmul(outs[0], lhsT, rhs, start=start, stop=stop),
                       inc=1 if stop else 0)

    def trans(self, out_psum, in_sb, ident):
        return self.op("tensor", [in_sb, ident], [out_psum],
                       lambda e: e.transpose(out_psum, in_sb, ident))

    def v(self, ins, outs, fn):
        return self.op("vector", ins, outs, fn)

    def s(self, ins, outs, fn):
        return self.op("scalar", ins, outs, fn)

    def g(self, ins, outs, fn, inc=1, ct=None):
        return self.op("gpsimd", ins, outs, fn, inc=inc, ct=ct)


def _np_bf16(x):
    return x.astype(ml_dtypes.bfloat16)


def host_constants():
    c = {}
    t = np.zeros((128, 128), np.float32)
    for po in range(128):
        h = po // 64
        t[h * 64:po, po] = 1.0
    c["tri_strict"] = t
    o2 = np.zeros((128, 2), np.float32)
    o2[:64, 0] = 1.0; o2[64:, 1] = 1.0
    c["ones2"] = o2
    m2 = np.zeros((2, 128), np.float32)
    m2[0, :64] = 1.0; m2[1, 64:] = 1.0
    c["map2"] = m2
    bm = np.zeros((128, 128), np.float32)
    bm[:64, :64] = 1.0; bm[64:, 64:] = 1.0
    c["bmask"] = bm
    c["ident128"] = np.eye(128, dtype=np.float32)
    bc = np.zeros((128, 1), np.float32)
    bc[64:, 0] = float(SEL)
    c["bcol1024"] = bc
    mp = np.zeros((64, 256), np.float32)
    for h in range(4):
        for q in range(64):
            mp[: q + 1, h * 64 + q] = 1.0
    c["maskPT"] = np.concatenate([mp, mp], axis=0)
    return c


class Prog:
    def __init__(self):
        self.nc = bass.Bass("TRN2", target_bir_lowering=False, debug=False,
                            dynamic_dma_scratch_size=32768)
        self.T = {}
        self.ctx = ExitStack()

    def inp(self, name, shape, dt=F32):
        self.T[name] = self.nc.dram_tensor(name, list(shape), dt, kind="ExternalInput").ap()
        return self.T[name]

    def scr(self, name, shape, dt=F32):
        self.T[name] = self.nc.dram_tensor(name, list(shape), dt, kind="Internal").ap()
        return self.T[name]

    def outp(self, name, shape, dt=F32):
        self.T[name] = self.nc.dram_tensor(name, list(shape), dt, kind="ExternalOutput").ap()
        return self.T[name]

    def sb(self, name, shape, dt):
        return self.ctx.enter_context(self.nc.sbuf_tensor(name, list(shape), dt))

    def ps(self, name, shape, dt=F32):
        return self.ctx.enter_context(self.nc.psum_tensor(name, list(shape), dt))

    def declare(self):
        inp = self.inp
        inp("x", (TPC, DM))
        inp("wqT_a", (DM, DM)); inp("wkT_a", (DM, 512)); inp("wvT_a", (DM, 512))
        inp("bq_a", (128, 16)); inp("bk_a", (128, 4)); inp("bv_b", (128, 512))
        inp("wq_f", (DM, 256)); inp("wk_f", (DM, 256)); inp("wv_f", (DM, 256))
        inp("wo_f", (DM, DM))
        inp("w1_f", (DM, 1024)); inp("w3_f", (DM, 1024)); inp("w2_f", (1024, DM))
        inp("g1b", (128, DM)); inp("g2b", (128, DM)); inp("wrb", (128, DM))
        inp("maskT", (SEL, SEL))
        inp("cosx", (128, 2 * SEL)); inp("sinx", (128, 2 * SEL))
        inp("selmy", (128, 16)); inp("iota1", (128, 1)); inp("iota16", (128, 1))
        inp("tri_strict", (128, 128)); inp("ones2", (128, 2)); inp("map2", (2, 128))
        inp("bmask", (128, 128))
        inp("ident128", (128, 128)); inp("bcol1024", (128, 1)); inp("maskPT", (128, 256))
        inp("core_meta", (128, 16))
        self.outp("out_x", (TPC, DM))
        scr = self.scr
        scr("xa", (TPC, DM)); scr("xb", (TPC, DM))
        scr("ybuf", (TPC, DM)); scr("yb16", (TPC, DM), BF16)
        scr("lg_in", (1, TPC)); scr("lg_out", (NCORES, TPC))
        scr("off_my", (TPC, 1), I32)
        scr("rw_my", (TPC, 1))
        scr("ysel_in", (2 * SEL + 1, DM), BF16); scr("ysel_out", (2 * SEL, DM), BF16)
        scr("o_in", (256, 2 * SEL), BF16); scr("o_out", (DM, 2 * SEL), BF16)
        scr("h2_in", (256, DM), BF16); scr("h2_out", (2 * SEL, DM), BF16)
        scr("ffn_in", (2 * SEL + 1, DM), BF16); scr("ffn_out", (2 * SEL + 1, DM), BF16)
        scr("zrow", (4, SEL)); scr("snap", (128, 32 * 4 * 132))
        scr("qfm_d", (128, 16 * 2048), BF16); scr("x2_d", (256, DM), BF16)
        scr("h_d", (2 * SEL, DM), BF16)
        scr("gla_in", (128, 528)); scr("gla_out", (NCORES * 128, 528))
        if DEBUG:
            self.outp("dbg_y", (TPC, DM))
            self.outp("dbg_lg", (NCORES, TPC))
            self.outp("dbg_flags", (128, 128))
            self.outp("dbg_prefix", (128, 128))
            self.outp("dbg_rw", (128, 128))
            self.outp("dbg_offmy", (128, 16), I32)
            self.outp("dbg_ysel", (2 * SEL, DM), BF16)
            self.outp("dbg_x3", (2 * SEL, DM), BF16)
            self.outp("dbg_x1", (TPC, DM))
            self.outp("dbg_x2", (TPC, DM))
            self.outp("dbg_zrow", (4, SEL))
            self.outp("dbg_cumb", (128, 4 * 2049))
            self.outp("dbg_kfm", (128, 4 * 2048), BF16)
            self.outp("dbg_vtm", (128, 16 * 512), BF16)
            self.outp("dbg_qfmd", (128, 16 * 2048), BF16)
            self.outp("dbg_snap", (128, 32 * 528))
            self.outp("dbg_glao", (NCORES * 512, 132))
            self.outp("dbg_oout", (DM, 2 * SEL), BF16)
            self.outp("dbg_x2d", (256, DM), BF16)

    def build(self):
        self.declare()
        with self.ctx:
            blk = self.ctx.enter_context(self.nc.Block())
            sc = Sched(self.nc, self.ctx)
            self.sc = sc
            self.load_consts()
            sc.barrier()
            if ONLY == "gla":
                self.copy_dram(self.T["x"], self.T["xa"], (TPC, DM), F32, tag="cx")
                sc.barrier()
                self.ln_logits(src=self.T["xa"], call=2, logits=False)
                sc.barrier()
                self.gla()
                sc.barrier()
                self.copy_dram(self.T["xb"], self.T["out_x"], (TPC, DM), F32, tag="co")
                sc.barrier()
                sc.flush(blk)
                return self.nc
            self.ln_logits(src=self.T["x"], call=0)
            sc.barrier()
            self.router(call=0)
            sc.barrier()
            self.ysel(call=0)
            sc.barrier()
            self.tb(call=0)
            sc.barrier()
            self.epilogue(x_src=self.T["x"], x_dst=self.T["xa"], call=0, dbg="dbg_x1")
            sc.barrier()
            self.gla()
            sc.barrier()
            if DEBUG:
                self.copy_dram(self.T["xb"], self.T["dbg_x2"], (TPC, DM), F32)
            self.router(call=1)
            sc.barrier()
            self.ysel(call=1)
            sc.barrier()
            self.tb(call=1)
            sc.barrier()
            self.epilogue(x_src=self.T["xb"], x_dst=self.T["out_x"], call=1, dbg=None)
            sc.barrier()
            sc.flush(blk)
        return self.nc

    def tb(self, call):
        sc, T, C = self.sc, self.T, self.consts
        tag = f"tb{call}"
        with ExitStack() as px:
            sb = lambda n, s, d: px.enter_context(self.nc.sbuf_tensor(tag + n, list(s), d))
            ps = lambda n, s: px.enter_context(self.nc.psum_tensor(tag + n, list(s), F32))
            h2t = sb("h2t", (128, 2048), BF16)
            stq = sb("stq", (128, 8), F32)
            sqs = sb("sqs", (128, 2048), BF16)
            idxb = sb("idxb", (128, 4), I32)
            idxf = sb("idxf", (128, 4), F32)
            def mkpool(stk, nm, n=3, shape=(128, 512)):
                ts = [stk.enter_context(self.nc.psum_tensor(tag + nm + str(i),
                      list(shape), F32)) for i in range(n)]
                ii = [0]
                def nxt():
                    t = ts[ii[0] % n]; ii[0] += 1
                    return t
                return nxt
            with ExitStack() as ax:
                asb = lambda n, s, d: ax.enter_context(self.nc.sbuf_tensor(tag + n, list(s), d))
                qfm = asb("qfm", (128, 2 * 2048), BF16)
                kfm = asb("kfm", (128, 2 * 2048), BF16)
                vtm = asb("vtm", (128, 16 * 256), BF16)
                with ExitStack() as bx:
                    bsb = lambda n, s, d: bx.enter_context(self.nc.sbuf_tensor(tag + n, list(s), d))
                    pacc_next = mkpool(bx, "pqk")
                    g1b = bsb("g1b", (128, DM), BF16)
                    sc.dma("gpsimd", g1b[:], T["g1b"], ct="ldc2")
                    htis = [bsb(f"hti{r}", (128, DM), BF16) for r in range(4)]
                    htos = [bsb(f"hto{r}", (128, DM), BF16) for r in range(4)]
                    hsts = [bsb(f"hst{r}", (128, 4), F32) for r in range(4)]
                    hsqs = [bsb(f"hsq{r}", (128, DM), BF16) for r in range(4)]
                    for i0 in range(0, 16, 4):
                        for i in range(i0, i0 + 4):
                            hti = htis[i % 4]
                            sc.dma("sync", hti[:],
                                   T["ysel_out"][i * 128:(i + 1) * 128, :],
                                   ct="ldh" + str(i % 4))
                        for i in range(i0, i0 + 4):
                            hti, hto = htis[i % 4], htos[i % 4]
                            hst, hsq = hsts[i % 4], hsqs[i % 4]
                            sc.s([hti], [hsq, hst], lambda e, hti=hti, hsq=hsq, hst=hst: e.activation(
                                hsq[:], hti[:], AF.Square, accum_out=hst[:, 0:1]))
                            sc.s([hst], [hst], lambda e, hst=hst: e.activation(
                                hst[:, 1:2], hst[:, 0:1], AF.Ln, scale=1.0 / DM,
                                bias=self.ccol[:, 1:2]))
                            sc.s([hst], [hst], lambda e, hst=hst: e.activation(
                                hst[:, 2:3], hst[:, 1:2], AF.Exp, scale=-0.5))
                            sc.v([hti, hst, g1b], [hto], lambda e, hti=hti, hst=hst, hto=hto: e.scalar_tensor_tensor(
                                hto[:], hti[:], hst[:, 2:3], g1b[:], op0=ALU.mult, op1=ALU.mult))
                        for i in range(i0, i0 + 4):
                            sc.dma("sync", T["h_d"][i * 128:(i + 1) * 128, :],
                                   htos[i % 4][:], ct="sthd" + str(i % 4))
                    yfm = bsb("yfm", (128, 16 * 2048), BF16)
                    for cch in range(16):
                        sc.dmat("scalar", yfm[:, cch * 2048:(cch + 1) * 2048],
                                T["h_d"][:, cch * 128:(cch + 1) * 128], ct="tpy")
                    s_wq = bsb("wq", (128, 16 * 256), BF16)
                    s_wk = bsb("wk", (128, 16 * 256), BF16)
                    s_wv = bsb("wv", (128, 16 * 256), BF16)
                    for nm, t in [("wq_f", s_wq), ("wk_f", s_wk), ("wv_f", s_wv)]:
                        sc.dma("gpsimd", t.ap().rearrange("p (c o) -> p c o", o=256),
                               T[nm].rearrange("(c p) o -> p c o", p=128), ct="ldw")
                    for dst, w, scale in [(qfm, s_wq, 1.0 / math.sqrt(128.0)),
                                          (kfm, s_wk, 1.0)]:
                        for h2 in range(2):
                            for ft in range(4):
                                pa = pacc_next()
                                for f in range(16):
                                    sc.mm([pa[:]], w[:, f * 256 + h2 * 128:f * 256 + h2 * 128 + 128],
                                          yfm[:, f * 2048 + ft * 512:f * 2048 + ft * 512 + 512],
                                          start=(f == 0), stop=(f == 15))
                                sl = dst[:, h2 * 2048 + ft * 512:h2 * 2048 + ft * 512 + 512]
                                sc.s([pa], [dst], lambda e, pa=pa, sl=sl, scale=scale: e.activation(
                                    sl, pa[:], AF.Copy, scale=scale))
                    for tt in range(16):
                        pa = pacc_next()
                        for f in range(16):
                            sc.mm([pa[:, 0:256]],
                                  yfm[:, f * 2048 + tt * 128:f * 2048 + tt * 128 + 128],
                                  s_wv[:, f * 256:(f + 1) * 256], start=(f == 0), stop=(f == 15))
                        sl = vtm[:, tt * 256:(tt + 1) * 256]
                        sc.s([pa], [vtm], lambda e, pa=pa, sl=sl: e.activation(sl, pa[:, 0:256], AF.Copy))
                sc.barrier()
                mt = asb("mt", (128, 8 * 1024), BF16)
                sc.dma("gpsimd", mt.ap().rearrange("p (c o) -> p c o", o=1024),
                       T["maskT"].rearrange("(c p) o -> p c o", p=128), ct="ldw")
                cosx = asb("cosx", (128, 2 * SEL), BF16)
                sinx = asb("sinx", (128, 2 * SEL), BF16)
                sc.dma("gpsimd", cosx[:], T["cosx"], ct="ldc2")
                sc.dma("gpsimd", sinx[:], T["sinx"], ct="ldc2")
                rot = asb("rot", (128, 2048), BF16)
                tmp = asb("tmp", (128, 2048), BF16)
                for t_ in (qfm, kfm):
                    for h in range(2):
                        sc.dma("sync", rot[0:64, :], t_[64:128, h * 2048:(h + 1) * 2048], ct="rsw")
                        sc.dma("sync", rot[64:128, :], t_[0:64, h * 2048:(h + 1) * 2048], ct="rsw")
                        sc.v([t_, cosx], [tmp], lambda e, t_=t_, h=h: e.tensor_mul(
                            tmp[:], t_[:, h * 2048:(h + 1) * 2048], cosx[:]))
                        sc.v([rot, sinx], [rot], lambda e: e.tensor_mul(
                            rot[:], rot[:], sinx[:]))
                        sc.v([tmp, rot], [t_], lambda e, t_=t_, h=h: e.tensor_add(
                            t_[:, h * 2048:(h + 1) * 2048], tmp[:], rot[:]))
                ptall = asb("ptall", (128, 8 * 1024), BF16)
                onesb = asb("ones1", (128, 1), BF16)
                sc.v([], [onesb], lambda e: e.memset(onesb[:], 1.0))
                otm = asb("otm", (128, 8 * 128), BF16)
                ofm = asb("ofm", (128, 2 * 2048), BF16)
                zcolv = asb("zcol", (128, 8), F32)
                p_sts = [ax.enter_context(self.nc.psum_tensor(tag + f"pst{i}", [128, 1024], F32))
                         for i in range(2)]
                p_zc = ax.enter_context(self.nc.psum_tensor(tag + "pzc", [128, 8], F32))
                p_oA = ax.enter_context(self.nc.psum_tensor(tag + "poA", [128, 512], F32))
                p_oB = ax.enter_context(self.nc.psum_tensor(tag + "poB", [128, 512], F32))
                for b in range(2):
                    for h in range(2):
                        for kc in range(8):
                            p_st = p_sts[kc % 2]
                            for qh in range(2):
                                sc.mm([p_st[:, qh * 512:(qh + 1) * 512]],
                                      kfm[:, h * 2048 + b * 1024 + kc * 128:
                                          h * 2048 + b * 1024 + kc * 128 + 128],
                                      qfm[:, h * 2048 + b * 1024 + qh * 512:
                                          h * 2048 + b * 1024 + qh * 512 + 512],
                                      start=True, stop=True)
                            if os.environ.get("MK_NOMASK", "0") != "1":
                                msl = mt[:, kc * 1024:(kc + 1) * 1024]
                                sc.v([mt, p_st], [p_st], lambda e, p_st=p_st, msl=msl:
                                     e.tensor_tensor(p_st[:], p_st[:], msl, op=ALU.add))
                            for qh in range(2):
                                sc.s([p_st], [ptall], lambda e, p_st=p_st, qh=qh, kc=kc:
                                     e.activation(
                                         ptall[:, kc * 1024 + qh * 512:kc * 1024 + qh * 512 + 512],
                                         p_st[:, qh * 512:(qh + 1) * 512], AF.Exp))
                        vs0 = (8 * b) * 256 + h * 128
                        for qq in range(8):
                            dst = p_oA if qq < 4 else p_oB
                            reg = dst[:, (qq % 4) * 128:(qq % 4) * 128 + 128]
                            for kc in range(8):
                                sc.mm([reg],
                                      ptall[:, kc * 1024 + qq * 128:kc * 1024 + qq * 128 + 128],
                                      vtm[:, vs0 + kc * 256: vs0 + kc * 256 + 128],
                                      start=(kc == 0), stop=(kc == 7))
                            for kc in range(8):
                                sc.mm([p_zc[:, qq:qq + 1]],
                                      ptall[:, kc * 1024 + qq * 128:kc * 1024 + qq * 128 + 128],
                                      onesb[:], start=(kc == 0), stop=(kc == 7))
                        sc.v([p_zc], [zcolv], lambda e: e.reciprocal(zcolv[:], p_zc[:]))
                        for qq in range(8):
                            srcp = (p_oA if qq < 4 else p_oB)[:, (qq % 4) * 128:(qq % 4) * 128 + 128]
                            sc.v([p_oA, p_oB, zcolv], [otm], lambda e, srcp=srcp, qq=qq:
                                 e.tensor_scalar(otm[:, qq * 128:(qq + 1) * 128], srcp,
                                                 zcolv[:, qq:qq + 1], None, op0=ALU.mult))
                        for qq in range(8):
                            sc.dmat("scalar", ofm[:, h * 2048 + b * 1024 + qq * 128:
                                                  h * 2048 + b * 1024 + qq * 128 + 128],
                                    otm[:, qq * 128:(qq + 1) * 128], ct="tpo")
                for h in range(2):
                    sc.dma("sync", T["o_in"][h * 128:(h + 1) * 128, :],
                           ofm[:, h * 2048:(h + 1) * 2048], ct="sto")
                sc.g([T["o_in"]], [T["o_out"]], lambda e: e.collective_compute(
                    "AllGather", ALU.bypass, replica_groups=[list(range(NCORES))],
                    ins=[T["o_in"]], outs=[T["o_out"]]), inc=1, ct="cc")
            sc.barrier()
            with ExitStack() as cx:
                csb = lambda n, s, d: cx.enter_context(self.nc.sbuf_tensor(tag + n, list(s), d))
                pacc_next = mkpool(cx, "pow")
                x2t = csb("x2t", (128, 2 * 2048), F32)
                g2b = csb("g2b", (128, DM), BF16)
                sc.dma("gpsimd", g2b[:], T["g2b"], ct="ldc2")
                s_wo = csb("wo", (128, 16 * 2048), BF16)
                sc.dma("gpsimd", s_wo.ap().rearrange("p (c o) -> p c o", o=2048),
                       T["wo_f"].rearrange("(c p) o -> p c o", p=128), ct="ldw")
                iosl = csb("iosl", (128, 16 * 128), BF16)
                ys_my = csb("ysmy", (128, 2 * 2048), BF16)
                o_view = T["o_out"].rearrange("r (c p) -> (r c) p", p=128)
                for tc in range(2):
                    sc.v([C["iota1"], C["core_meta"]], [idxf], lambda e, tc=tc: e.tensor_scalar(
                        idxf[:, 0:1], C["iota1"][:], C["core_meta"][:, 3:4], float(tc * 128),
                        op0=ALU.add, op1=ALU.add))
                    sc.v([idxf], [idxb], lambda e: e.tensor_copy(idxb[:, 0:1], idxf[:, 0:1]))
                    sc.op("gpsimd", [T["ysel_out"], idxb], [ys_my],
                          lambda e, tc=tc: e.indirect_dma_start(
                              out=ys_my[:, tc * 2048:(tc + 1) * 2048], out_offset=None,
                              in_=T["ysel_out"],
                              in_offset=IndirectOffsetOnAxis(idxb[:, 0:1], 0)),
                          inc=16, ct="gys")
                    for hc in range(16):
                        sc.v([C["iota16"], C["core_meta"]], [idxf], lambda e, hc=hc, tc=tc:
                             e.tensor_scalar(idxf[:, 1:2], C["iota16"][:],
                                             C["core_meta"][:, 4:5], float(tc + hc * 2048),
                                             op0=ALU.add, op1=ALU.add))
                        sc.v([idxf], [idxb], lambda e: e.tensor_copy(idxb[:, 1:2], idxf[:, 1:2]))
                        sc.op("gpsimd", [T["o_out"], idxb], [iosl],
                              lambda e, hc=hc: e.indirect_dma_start(
                                  out=iosl[:, hc * 128:(hc + 1) * 128], out_offset=None,
                                  in_=o_view,
                                  in_offset=IndirectOffsetOnAxis(idxb[:, 1:2], 0)),
                              inc=16, ct="gos")
                    for ft in range(4):
                        pa = pacc_next()
                        for hc in range(16):
                            sc.mm([pa[:]], iosl[:, hc * 128:(hc + 1) * 128],
                                  s_wo[:, hc * 2048 + ft * 512:hc * 2048 + ft * 512 + 512],
                                  start=(hc == 0), stop=(hc == 15))
                        ysl = ys_my[:, tc * 2048 + ft * 512: tc * 2048 + ft * 512 + 512]
                        x2sl = x2t[:, tc * 2048 + ft * 512: tc * 2048 + ft * 512 + 512]
                        sc.v([pa, ys_my], [x2t], lambda e, pa=pa, ysl=ysl, x2sl=x2sl: e.tensor_tensor(
                            x2sl, pa[:], ysl, op=ALU.add))
                    x2sl = x2t[:, tc * 2048:(tc + 1) * 2048]
                    sc.s([x2t], [sqs, stq], lambda e, x2sl=x2sl: e.activation(
                        sqs[:], x2sl, AF.Square, accum_out=stq[:, 0:1]))
                    sc.s([stq], [stq], lambda e: e.activation(
                        stq[:, 1:2], stq[:, 0:1], AF.Ln, scale=1.0 / DM, bias=self.ccol[:, 1:2]))
                    sc.s([stq], [stq], lambda e: e.activation(
                        stq[:, 2:3], stq[:, 1:2], AF.Exp, scale=-0.5))
                    sc.v([x2t, stq, g2b], [h2t], lambda e, x2sl=x2sl: e.scalar_tensor_tensor(
                        h2t[:], x2sl, stq[:, 2:3], g2b[:], op0=ALU.mult, op1=ALU.mult))
                    sc.dma("sync", T["h2_in"][tc * 128:(tc + 1) * 128, :], h2t[:], ct="sth")
                    sc.v([x2t], [sqs], lambda e, x2sl=x2sl: e.tensor_copy(sqs[:], x2sl))
                    sc.dma("sync", T["x2_d"][tc * 128:(tc + 1) * 128, :], sqs[:], ct="stx2d")
                sc.g([T["h2_in"]], [T["h2_out"]], lambda e: e.collective_compute(
                    "AllGather", ALU.bypass, replica_groups=[list(range(NCORES))],
                    ins=[T["h2_in"]], outs=[T["h2_out"]]), inc=1, ct="cc")
            sc.barrier()
            with ExitStack() as dx:
                dsb = lambda n, s, d: dx.enter_context(self.nc.sbuf_tensor(tag + n, list(s), d))
                pacc_next = mkpool(dx, "pff")
                s_w1 = dsb("w1", (128, 16 * 1024), BF16)
                s_w3 = dsb("w3", (128, 16 * 1024), BF16)
                s_w2 = dsb("w2", (128, 8 * 2048), BF16)
                sc.dma("gpsimd", s_w1.ap().rearrange("p (c o) -> p c o", o=1024),
                       T["w1_f"].rearrange("(c p) o -> p c o", p=128), ct="ldw")
                sc.dma("gpsimd", s_w3.ap().rearrange("p (c o) -> p c o", o=1024),
                       T["w3_f"].rearrange("(c p) o -> p c o", p=128), ct="ldw")
                sc.dma("gpsimd", s_w2.ap().rearrange("p (c o) -> p c o", o=2048),
                       T["w2_f"].rearrange("(c p) o -> p c o", p=128), ct="ldw")
                h2fm = dsb("h2fm", (128, 16 * 512), BF16)
                spr = dsb("spr", (128, 8 * 512), BF16)
                su = dsb("su", (128, 512), BF16)
                stg = dsb("stg", (128, 2048), BF16)
                for quar in range(4):
                    for cch in range(16):
                        sc.dmat("scalar", h2fm[:, cch * 512:(cch + 1) * 512],
                                T["h2_out"][quar * 512:(quar + 1) * 512,
                                            cch * 128:(cch + 1) * 128], ct="tph")
                    for fc in range(8):
                        pa = pacc_next()
                        for f in range(16):
                            sc.mm([pa[:]],
                                  s_w1[:, f * 1024 + fc * 128:f * 1024 + fc * 128 + 128],
                                  h2fm[:, f * 512:(f + 1) * 512],
                                  start=(f == 0), stop=(f == 15))
                        pb = pacc_next()
                        for f in range(16):
                            sc.mm([pb[:]],
                                  s_w3[:, f * 1024 + fc * 128:f * 1024 + fc * 128 + 128],
                                  h2fm[:, f * 512:(f + 1) * 512],
                                  start=(f == 0), stop=(f == 15))
                        sc.s([pa], [su], lambda e, pa=pa: e.activation(su[:], pa[:], AF.Silu))
                        psl = spr[:, fc * 512:(fc + 1) * 512]
                        sc.v([pb, su], [spr], lambda e, pb=pb, psl=psl: e.tensor_tensor(
                            psl, pb[:], su[:], op=ALU.mult))
                    for tt in range(4):
                        for ft in range(4):
                            pa = pacc_next()
                            for fc in range(8):
                                sc.mm([pa[:]],
                                      spr[:, fc * 512 + tt * 128:fc * 512 + tt * 128 + 128],
                                      s_w2[:, fc * 2048 + ft * 512:fc * 2048 + ft * 512 + 512],
                                      start=(fc == 0), stop=(fc == 7))
                            sc.s([pa], [stg], lambda e, pa=pa, ft=ft: e.activation(
                                stg[:, ft * 512:(ft + 1) * 512], pa[:], AF.Copy))
                        sc.dma("sync", T["ffn_in"][quar * 512 + tt * 128:
                                                   quar * 512 + tt * 128 + 128, :],
                               stg[:], ct="stf")
                for tc in range(2):
                    sc.dma("sync", stg[:], T["x2_d"][tc * 128:(tc + 1) * 128, :], ct="ldx2d")
                    sc.v([C["iota1"], C["core_meta"]], [idxf], lambda e, tc=tc: e.tensor_scalar(
                        idxf[:, 2:3], C["iota1"][:], C["core_meta"][:, 3:4], float(tc * 128),
                        op0=ALU.add, op1=ALU.add))
                    sc.v([idxf], [idxb], lambda e: e.tensor_copy(idxb[:, 2:3], idxf[:, 2:3]))
                    sc.op("gpsimd", [stg, idxb, T["ffn_in"]], [T["ffn_in"]],
                          lambda e: e.indirect_dma_start(
                              out=T["ffn_in"],
                              out_offset=IndirectOffsetOnAxis(idxb[:, 2:3], 0),
                              in_=stg[:], in_offset=None,
                              compute_op=ALU.add),
                          inc=16, ct="scax")
                sc.dma("sync", T["ffn_in"][2 * SEL:2 * SEL + 1, :],
                       self.zero16[0:1, :], ct="zft")
                sc.g([T["ffn_in"]], [T["ffn_out"]], lambda e: e.collective_compute(
                    "AllReduce", ALU.add, replica_groups=[list(range(NCORES))],
                    ins=[T["ffn_in"]], outs=[T["ffn_out"]]), inc=1, ct="cc")
            if DEBUG and call == 0:
                self.copy_dram(T["ffn_out"], T["dbg_x3"], (2 * SEL, DM), BF16, tag="dcx3")
                self.copy_dram(T["zrow"], T["dbg_zrow"], (4, SEL), F32, tag="dcz") if False else None
                with ExitStack() as zx:
                    zt = zx.enter_context(self.nc.sbuf_tensor("dbgzt", [4, SEL], F32))
                    sc.dma("sync", zt[:], T["zrow"], ct="dbg")
                    sc.dma("sync", T["dbg_zrow"], zt[:], ct="dbg")
                self.copy_dram(T["o_out"], T["dbg_oout"], (DM, 2 * SEL), BF16, tag="dco")
                with ExitStack() as zx:
                    zt = zx.enter_context(self.nc.sbuf_tensor("dbgx2t", [128, DM], BF16))
                    for i in range(2):
                        sc.dma("sync", zt[:], T["x2_d"][i * 128:(i + 1) * 128, :], ct="dbg")
                        sc.dma("sync", T["dbg_x2d"][i * 128:(i + 1) * 128, :], zt[:], ct="dbg")

    def gla(self):
        sc, T, C = self.sc, self.T, self.consts
        tag = "gl"
        with ExitStack() as px:
            sb = lambda n, s, d: px.enter_context(self.nc.sbuf_tensor(tag + n, list(s), d))
            ps = lambda n, s, d=F32: px.enter_context(self.nc.psum_tensor(tag + n, list(s), d))
            ident = sb("ident", (128, 128), BF16)
            maskPT = sb("maskPT", (128, 256), BF16)
            sc.dma("gpsimd", ident[:], T["ident128"], ct="ldc2")
            sc.dma("gpsimd", maskPT[:], T["maskPT"], ct="ldc2")
            kfm = sb("kfm", (128, 4 * 2048), BF16)
            vtm = sb("vtm", (128, 16 * 512), BF16)
            bqa = sb("bqa", (128, 16), F32)
            bka = sb("bka", (128, 4), F32)
            bqs = sb("bqs", (128, 16), F32)
            sc.dma("sync", bqa[:], T["bq_a"], ct="ldb")
            sc.dma("sync", bka[:], T["bk_a"], ct="ldb")
            bvb = sb("bvb", (128, 512), F32)
            sc.dma("sync", bvb[:], T["bv_b"], ct="ldb")
            sc.v([bqa], [bqs], lambda e: e.tensor_scalar(
                bqs[:], bqa[:], 1.0 / math.sqrt(128.0), None, op0=ALU.mult))
            etot = sb("etot", (128, 4 * 32), F32)
            midx = ExitStack()
            cumb = midx.enter_context(self.nc.sbuf_tensor(
                tag + "cumb", [128, 4 * 2049], F32))
            with ExitStack() as axx:
                asb = lambda n, s, d: axx.enter_context(self.nc.sbuf_tensor(tag + n, list(s), d))
                yfm = asb("yfm", (128, 16 * 2048), BF16)
                for cch in range(16):
                    sc.dmat("scalar", yfm[:, cch * 2048:(cch + 1) * 2048],
                            T["yb16"][:, cch * 128:(cch + 1) * 128], ct="tpy2")
                _pjs = [axx.enter_context(self.nc.psum_tensor(tag + f"pj{i}", [128, 512], F32)) for i in range(3)]
                _pji = [0]
                def pj_next():
                    t = _pjs[_pji[0] % 3]; _pji[0] += 1
                    return t
                with ExitStack() as b1:
                    bsb = lambda n, s, d: b1.enter_context(self.nc.sbuf_tensor(tag + n, list(s), d))
                    s_wvT = bsb("wvT", (128, 16 * 512), BF16)
                    sc.dma("gpsimd", s_wvT.ap().rearrange("p (c o) -> p c o", o=512),
                           T["wvT_a"].rearrange("(c p) o -> p c o", p=128), ct="ldw")
                    wkc = bsb("wkc", (128, 16 * 128), BF16)
                    t1g = bsb("t1g", (128, 2048), F32)
                    gplg = bsb("gplg", (128, 2048), F32)
                    zrow2 = self.zero16
                    for g4 in range(4):
                        sc.dma("gpsimd", wkc.ap().rearrange("p (c o) -> p c o", o=128),
                               T["wkT_a"][:, g4 * 128:(g4 + 1) * 128]
                               .rearrange("(c p) o -> p c o", p=128), ct="ldw")
                        for ft in range(4):
                            pa = pj_next()
                            for f in range(16):
                                sc.mm([pa[:]], wkc[:, f * 128:(f + 1) * 128],
                                      yfm[:, f * 2048 + ft * 512:f * 2048 + ft * 512 + 512],
                                      start=(f == 0), stop=(f == 15))
                            sl = t1g[:, ft * 512:(ft + 1) * 512]
                            sc.s([pa, bka], [t1g], lambda e, pa=pa, sl=sl, g4=g4: e.activation(
                                sl, pa[:], AF.Sigmoid, bias=bka[:, g4:g4 + 1]))
                        sc.v([t1g], [t1g], lambda e: e.tensor_scalar(
                            t1g[:], t1g[:], 0.95, None, op0=ALU.min))
                        ksl = kfm[:, g4 * 2048:(g4 + 1) * 2048]
                        sc.v([t1g], [kfm], lambda e, ksl=ksl: e.tensor_copy(ksl, t1g[:]))
                        sc.s([t1g], [gplg], lambda e: e.activation(
                            gplg[:], t1g[:], AF.Ln, scale=-1.0, bias=self.ccol[:, 2:3]))
                        base = g4 * 2049
                        sc.v([cumb], [cumb], lambda e, base=base: e.memset(
                            cumb[:, base:base + 1], 0.0))
                        sc.v([gplg, zrow2], [cumb], lambda e, base=base: e.tensor_tensor_scan(
                            cumb[:, base + 1:base + 2049], gplg[:], zrow2[:], 0.0,
                            op0=ALU.add, op1=ALU.add))
                    for tt in range(16):
                        pa = pj_next()
                        for f in range(16):
                            sc.mm([pa[:]], yfm[:, f * 2048 + tt * 128:f * 2048 + tt * 128 + 128],
                                  s_wvT[:, f * 512:(f + 1) * 512], start=(f == 0),
                                  stop=(f == 15))
                        sl = vtm[:, tt * 512:(tt + 1) * 512]
                        sc.v([pa, bvb], [vtm], lambda e, pa=pa, sl=sl: e.tensor_tensor(
                            sl, pa[:], bvb[:], op=ALU.add))
                sc.barrier()
                with ExitStack() as b2:
                    bsb = lambda n, s, d: b2.enter_context(self.nc.sbuf_tensor(tag + n, list(s), d))
                    wqc = bsb("wqc", (128, 16 * 128), BF16)
                    qstg = bsb("qstg", (128, 2048), BF16)
                    argb = bsb("argb", (128, 2048), F32)
                    eargb = bsb("eargb", (128, 2048), F32)
                    for hq in range(16):
                        sc.dma("gpsimd", wqc.ap().rearrange("p (c o) -> p c o", o=128),
                               T["wqT_a"][:, hq * 128:(hq + 1) * 128]
                               .rearrange("(c p) o -> p c o", p=128), ct="ldw")
                        if hq % 4 == 0:
                            base = (hq // 4) * 2049
                            bview = cumb.ap()[:, base:base + 2048:64].rearrange(
                                "p k -> p k ()").to_broadcast([128, 32, 64])
                            csl = cumb.ap()[:, base + 1:base + 2049].rearrange(
                                "p (k j) -> p k j", j=64)
                            sc.v([cumb], [argb], lambda e, bview=bview, csl=csl: e.tensor_tensor(
                                argb.ap().rearrange("p (k j) -> p k j", j=64), bview, csl,
                                op=ALU.subtract))
                            sc.s([argb], [eargb], lambda e: e.activation(
                                eargb[:], argb[:], AF.Exp, scale=-1.0))
                        for ft in range(4):
                            pa = pj_next()
                            for f in range(16):
                                sc.mm([pa[:]], wqc[:, f * 128:(f + 1) * 128],
                                      yfm[:, f * 2048 + ft * 512:f * 2048 + ft * 512 + 512],
                                      start=(f == 0), stop=(f == 15))
                            sl = qstg[:, ft * 512:(ft + 1) * 512]
                            sc.s([pa, bqs], [qstg], lambda e, pa=pa, sl=sl, hq=hq: e.activation(
                                sl, pa[:], AF.Relu, scale=1.0 / math.sqrt(128.0),
                                bias=bqs[:, hq:hq + 1]))
                        sc.v([qstg, eargb], [qstg], lambda e: e.tensor_tensor(
                            qstg[:], qstg[:], eargb[:], op=ALU.mult))
                        sc.dma("sync", T["qfm_d"][:, hq * 2048:(hq + 1) * 2048], qstg[:], ct="stq")
            sc.barrier()
            with ExitStack() as bxx:
                bsb = lambda n, s, d: bxx.enter_context(self.nc.sbuf_tensor(tag + n, list(s), d))
                arg = bsb("arg", (128, 2048), F32)
                earg = bsb("earg", (128, 2048), F32)
                for g4 in range(4):
                    base = g4 * 2049
                    bview = cumb.ap()[:, base:base + 2048:64]
                    bview = bview.rearrange("p k -> p k ()").to_broadcast([128, 32, 64])
                    csl = cumb.ap()[:, base + 1:base + 2049].rearrange("p (k j) -> p k j", j=64)
                    sc.v([cumb], [arg], lambda e, bview=bview, csl=csl: e.tensor_tensor(
                        arg.ap().rearrange("p (k j) -> p k j", j=64), bview, csl,
                        op=ALU.subtract))
                    sc.s([arg], [earg], lambda e: e.activation(earg[:], arg[:], AF.Exp))
                    ksl = kfm[:, g4 * 2048:(g4 + 1) * 2048]
                    sc.v([earg, kfm], [kfm], lambda e, ksl=ksl: e.tensor_mul(ksl, ksl, earg[:]))
                    a_ = cumb.ap()[:, base + 64:base + 2049:64]
                    b_ = cumb.ap()[:, base:base + 2048:64]
                    esl = etot[:, g4 * 32:(g4 + 1) * 32]
                    sc.v([cumb], [etot], lambda e, a_=a_, b_=b_, esl=esl: e.tensor_tensor(
                        esl, a_, b_, op=ALU.subtract))
                    sc.s([etot], [etot], lambda e, esl=esl: e.activation(esl, esl, AF.Exp))
            if DEBUG:
                with ExitStack() as zx:
                    zt = zx.enter_context(self.nc.sbuf_tensor("dbcu", [128, 4 * 2049], F32))
                    sc.v([cumb], [zt], lambda e, zt=zt: e.tensor_copy(zt[:], cumb[:]))
                    sc.dma("sync", T["dbg_cumb"], zt[:], ct="dbg")
                sc.dma("sync", T["dbg_kfm"], kfm[:], ct="dbg")
                sc.dma("sync", T["dbg_vtm"], vtm[:], ct="dbg")
            sc.barrier()
            with ExitStack() as cxx:
                csb = lambda n, s, d: cxx.enter_context(self.nc.sbuf_tensor(tag + n, list(s), d))
                Srt = [csb(f"Srt{i}", (128, 4 * 528), F32) for i in range(2)]
                slot = lambda k: (Srt[(k % 8) // 4], ((k % 8) % 4) * 528)
                ktrs = [csb(f"ktr{i}", (128, 512), BF16) for i in range(2)]
                p_trs = [cxx.enter_context(self.nc.psum_tensor(tag + f"ptr{i}", [128, 512], BF16)) for i in range(2)]
                p_ms = [cxx.enter_context(self.nc.psum_tensor(tag + f"pm{i}", [128, 512], F32)) for i in range(2)]
                for i in range(2):
                    sc.v([], [Srt[i]], lambda e, i=i: e.memset(Srt[i][:], 0.0))
                svw = lambda t, o: t.ap()[:, o:o + 528].rearrange(
                    "p (g c) -> p g c", c=132)[:, :, 0:128]
                for k in range(32):
                    tc, oc = slot(k)
                    tn, on = slot(k + 1)
                    par = (k % 2) * 64
                    ktr, p_tr, p_m = ktrs[k % 2], p_trs[k % 2], p_ms[k % 2]
                    for g4 in range(4):
                        ksl = kfm[:, g4 * 2048 + k * 64: g4 * 2048 + k * 64 + 64]
                        sc.trans(p_tr[par:par + 64, g4 * 128:(g4 + 1) * 128], ksl, ident[:])
                    sc.s([p_tr], [ktr], lambda e, par=par, ktr=ktr, p_tr=p_tr: e.activation(
                        ktr[par:par + 64, :], p_tr[par:par + 64, :], AF.Copy))
                    for g4 in range(4):
                        vsl = vtm[(k % 2) * 64:(k % 2) * 64 + 64,
                                  (k // 2) * 512 + g4 * 128: (k // 2) * 512 + g4 * 128 + 128]
                        sc.mm([p_m[:, g4 * 128:(g4 + 1) * 128]],
                              ktr[par:par + 64, g4 * 128:(g4 + 1) * 128], vsl,
                              start=True, stop=True)
                    pmv = p_m.ap().rearrange("p (g c) -> p g c", c=128)
                    ebr = etot.ap()[:, k:128:32].rearrange("p g -> p g ()").to_broadcast([128, 4, 128])
                    sc.v([tc, p_m], [tn], lambda e, a=svw(tc, oc), o=svw(tn, on), pmv=pmv: e.tensor_tensor(
                        o, a, pmv, op=ALU.add))
                    sc.v([tn, etot], [tn], lambda e, o=svw(tn, on), ebr=ebr: e.tensor_tensor(
                        o, o, ebr, op=ALU.mult))
                    sc.v([cumb], [tn], lambda e, tn=tn, on=on, k=k: e.tensor_copy(
                        tn.ap()[:, on + 128:on + 4 * 132:132],
                        cumb.ap()[:, 64 * (k + 1):4 * 2049:2049]))
                    if k % 4 == 3:
                        tsrc, _ = slot(k)
                        sc.dma("sync", T["snap"][:, (k - 3) * 528:(k + 1) * 528],
                               tsrc[:], ct="stsn")
                sc.dma("sync", T["gla_in"], Srt[0][:, 0:528], ct="stgi")
                sc.g([T["gla_in"]], [T["gla_out"]], lambda e: e.collective_compute(
                    "AllGather", ALU.bypass, replica_groups=[list(range(NCORES))],
                    ins=[T["gla_in"]], outs=[T["gla_out"]]), inc=1, ct="cc")
            if DEBUG:
                self.copy_dram(T["qfm_d"], T["dbg_qfmd"], (128, 16 * 2048), BF16, tag="dcq")
                self.copy_dram(T["snap"], T["dbg_snap"], (128, 32 * 528), F32, tag="dcs")
                with ExitStack() as zx:
                    zt = zx.enter_context(self.nc.sbuf_tensor("dbgo", [128, 132], F32))
                    for r in range(NCORES * 4):
                        sc.dma("sync", zt[:], T["gla_out"][r * 128:(r + 1) * 128, :], ct="dbg")
                        sc.dma("sync", T["dbg_glao"][r * 128:(r + 1) * 128, :], zt[:], ct="dbg")
            midx.close()
            sc.barrier()
            with ExitStack() as dxx:
                dsb = lambda n, s, d: dxx.enter_context(self.nc.sbuf_tensor(tag + n, list(s), d))
                qfm = dsb("qfm", (128, 16 * 2048), BF16)
                for hq in range(16):
                    sc.dma("scalar", qfm[:, hq * 2048:(hq + 1) * 2048],
                           T["qfm_d"][:, hq * 2048:(hq + 1) * 2048], ct="ldq")
                lnwrb = dsb("lnwrb", (128, DM), F32)
                sc.dma("sync", lnwrb[:], T["wrb"], ct="ldc2s")
                lnbufs2 = [{"pr": dsb(f"lnpr{i}", (128, DM), F32),
                            "st": dsb(f"lnst{i}", (128, 8), F32),
                            "y16": dsb(f"lny16{i}", (128, DM), BF16),
                            "lgcol": dsb(f"lnlg{i}", (128, 16), F32),
                            "wrb": lnwrb} for i in range(2)]
                for i in range(2):
                    sc.v([], [lnbufs2[i]["lgcol"]], lambda e, i=i: e.memset(
                        lnbufs2[i]["lgcol"][:], 0.0))
                Sstall = dsb("Sstall", (128, 512), F32)
                rbufs = [dsb(f"rbuf{i}", (128, 4 * 132), F32) for i in range(1)]
                eg4a = dsb("eg4a", (128, 4), F32)
                eg4b = dsb("eg4b", (128, 4), F32)
                tS = dsb("tS", (128, 512), F32)
                sstv = Sstall.ap().rearrange("p (g c) -> p g c", c=128)
                sc.v([], [Sstall], lambda e: e.memset(Sstall[:], 0.0))
                for r in range(NCORES - 1):
                    rbuf = rbufs[r % len(rbufs)]
                    sc.dma("sync", rbuf[:], T["gla_out"][r * 128:(r + 1) * 128, :],
                           ct="ldgo")
                    sc.v([rbuf, C["core_meta"]], [eg4a], lambda e, r=r, rbuf=rbuf: e.tensor_scalar(
                        eg4a[:], rbuf.ap()[:, 128:4 * 132:132],
                        C["core_meta"][:, 5 + r:6 + r], None, op0=ALU.mult))
                    sc.s([eg4a], [eg4b], lambda e: e.activation(eg4b[:], eg4a[:], AF.Exp))
                    rv = rbuf.ap().rearrange("p (g c) -> p g c", c=132)[:, :, 0:128]
                    tv = tS.ap().rearrange("p (g c) -> p g c", c=128)
                    sc.v([rbuf, C["core_meta"]], [tS], lambda e, r=r, rv=rv, tv=tv: e.tensor_scalar(
                        tv, rv, C["core_meta"][:, 5 + r:6 + r], None, op0=ALU.mult))
                    ebr = eg4b.ap().rearrange("p g -> p g ()").to_broadcast([128, 4, 128])
                    sc.v([Sstall, eg4b], [Sstall], lambda e, ebr=ebr: e.tensor_tensor(
                        sstv, sstv, ebr, op=ALU.mult))
                    sc.v([Sstall, tS], [Sstall], lambda e: e.tensor_tensor(
                        Sstall[:], Sstall[:], tS[:], op=ALU.add))
                snlp = [dsb(f"snlp{i}", (128, 2 * 528), F32) for i in range(2)]
                Skas = [dsb(f"Ska{i}", (128, 512), F32) for i in range(2)]
                Sk16s = [dsb(f"Sk16{i}", (128, 512), BF16) for i in range(2)]
                egks = [dsb(f"egk{i}", (128, 4), F32) for i in range(2)]
                ptss = [dsb(f"pts{i}", (128, 1024), BF16) for i in range(2)]
                x1ts = [dsb(f"x1t{i}", (128, 2048), F32) for i in range(2)]
                x2gs = [dsb(f"x2g{i}", (128, 2048), F32) for i in range(1)]
                p_pts = [dxx.enter_context(self.nc.psum_tensor(tag + f"ppt{i}", [128, 1024], F32)) for i in range(2)]
                p_og = [dxx.enter_context(self.nc.psum_tensor(tag + f"pog{g4}", [128, 512], F32)) for g4 in range(4)]
                for k in range(32):
                    Skall, Sk16 = Skas[k % 2], Sk16s[k % 2]
                    egk, pts, p_pt = egks[k % 2], ptss[k % 2], p_pts[k % 2]
                    par = (k % 2) * 64
                    snl, soff = snlp[(k // 2) % 2], (k % 2) * 528
                    if k % 2 == 0:
                        sc.dma("sync", snl[:], T["snap"][:, k * 528:(k + 2) * 528],
                               ct="ldsn")
                    sc.s([snl], [egk], lambda e, snl=snl, soff=soff, egk=egk: e.activation(
                        egk[:], snl.ap()[:, soff + 128:soff + 4 * 132:132], AF.Exp))
                    skv = Skall.ap().rearrange("p (g c) -> p g c", c=128)
                    snv = snl.ap()[:, soff:soff + 528].rearrange(
                        "p (g c) -> p g c", c=132)[:, :, 0:128]
                    ebr = egk.ap().rearrange("p g -> p g ()").to_broadcast([128, 4, 128])
                    sc.v([Sstall, egk], [Skall], lambda e, skv=skv, ebr=ebr: e.tensor_tensor(
                        skv, sstv, ebr, op=ALU.mult))
                    sc.v([Skall, snl], [Skall], lambda e, skv=skv, snv=snv: e.tensor_tensor(
                        skv, skv, snv, op=ALU.add))
                    sc.s([Skall], [Sk16], lambda e, Skall=Skall, Sk16=Sk16: e.activation(
                        Sk16[:], Skall[:], AF.Copy))
                    for g4 in range(4):
                        ksl = kfm[:, g4 * 2048 + k * 64: g4 * 2048 + k * 64 + 64]
                        qv4 = qfm.ap().rearrange("p (h t) -> p h t", t=2048)[
                            :, 4 * g4:4 * g4 + 4, k * 64:k * 64 + 64]
                        sc.mm([p_pt[par:par + 64, g4 * 256:(g4 + 1) * 256]], ksl, qv4,
                              start=True, stop=True)
                    mbr = maskPT.ap()[par:par + 64, :].rearrange(
                        "p t -> p () t").to_broadcast([64, 4, 256])
                    ptv = p_pt.ap()[par:par + 64, :].rearrange("p (g t) -> p g t", t=256)
                    psv = pts.ap()[par:par + 64, :].rearrange("p (g t) -> p g t", t=256)
                    sc.v([p_pt, maskPT], [pts], lambda e, psv=psv, ptv=ptv, mbr=mbr: e.tensor_tensor(
                        psv, ptv, mbr, op=ALU.mult))
                    for g4 in range(4):
                        vsl = vtm[(k % 2) * 64:(k % 2) * 64 + 64,
                                  (k // 2) * 512 + g4 * 128: (k // 2) * 512 + g4 * 128 + 128]
                        for qh in range(4):
                            qsl = qfm[:, (4 * g4 + qh) * 2048 + k * 64:
                                      (4 * g4 + qh) * 2048 + k * 64 + 64]
                            reg = p_og[g4][(k % 2) * 64:(k % 2) * 64 + 64,
                                           qh * 128:(qh + 1) * 128]
                            sc.mm([reg], qsl, Sk16[:, g4 * 128:(g4 + 1) * 128],
                                  start=True, stop=False)
                            sc.mm([reg], pts[par:par + 64, g4 * 256 + qh * 64:
                                             g4 * 256 + (qh + 1) * 64], vsl,
                                  start=False, stop=True)
                    if k % 2 == 0:
                        jn = k // 2
                        sc.dma("scalar", x1ts[jn % 2][:],
                               T["xa"][jn * 128:(jn + 1) * 128, :], ct="ldx1")
                    if k % 2 == 1:
                        j = k // 2
                        x1t, x2g = x1ts[j % 2], x2gs[j % len(x2gs)]
                        lnbufs2[j % 2]["y"] = x1t
                        for g4 in range(4):
                            sc.v([p_og[g4], x1t], [x2g], lambda e, g4=g4, x1t=x1t, x2g=x2g: e.tensor_tensor(
                                x2g[:, g4 * 512:(g4 + 1) * 512], p_og[g4][:],
                                x1t[:, g4 * 512:(g4 + 1) * 512], op=ALU.add))
                        sc.dma("gpsimd" if j % 2 else "sync",
                               T["xb"][j * 128:(j + 1) * 128, :], x2g[:],
                               ct="stx2" + str(j % 2))
                        self.ln_tile(x2g, j, lnbufs2[j % 2], logits=True)
                sc.v([lnbufs2[0]["lgcol"], lnbufs2[1]["lgcol"]],
                     [lnbufs2[0]["lgcol"]], lambda e: e.tensor_tensor(
                         lnbufs2[0]["lgcol"][:], lnbufs2[0]["lgcol"][:],
                         lnbufs2[1]["lgcol"][:], op=ALU.add))
                self.store_lg(lnbufs2[0]["lgcol"], dxx, p_reuse=p_pts[0])

    def copy_dram(self, src, dst, shape, dt):
        sc = self.sc
        tile = self.sb("cpy_t", (128, shape[1]), dt)
        for i in range(shape[0] // 128):
            sc.dma("sync", tile[:], src[i * 128:(i + 1) * 128, :])
            sc.dma("sync", dst[i * 128:(i + 1) * 128, :], tile[:])

    def load_consts(self):
        sc, T = self.sc, self.T
        self.consts = {}
        specs = [("tri_strict", (128, 128), F32), ("ones2", (128, 2), F32),
                 ("map2", (2, 128), F32), ("bmask", (128, 128), F32),
                 ("bcol1024", (128, 1), F32),
                 ("core_meta", (128, 16), F32),
                 ("selmy", (128, 16), F32), ("iota1", (128, 1), F32),
                 ("iota16", (128, 1), F32)]
        identf = self.sb("c_identf", (128, 128), F32)
        self.identf = identf
        sc.dma("sync", identf[:], T["ident128"])
        for nm, shp, dt in specs:
            t = self.sb("c_" + nm, shp, dt)
            self.consts[nm] = t
            if dt == F32:
                sc.dma("sync", t[:], T[nm])
            else:
                sc.dma("gpsimd", t[:], T[nm])
        self.zero16 = self.sb("zero16", (128, DM), BF16)
        sc.v([], [self.zero16], lambda e: e.memset(self.zero16[:], 0.0))
        self.ccol = self.sb("ccol", (128, 4), F32)
        sc.v([], [self.ccol], lambda e: e.memset(self.ccol[:, 0:1], 1e-5))
        sc.v([self.ccol], [self.ccol], lambda e: e.memset(self.ccol[:, 1:2], 1e-6))
        sc.v([self.ccol], [self.ccol], lambda e: e.memset(self.ccol[:, 2:3], 1.0))

    def ln_logits(self, src, call, logits=True):
        sc, T, C = self.sc, self.T, self.consts
        tag = f"ln{call}"
        px = ExitStack()
        psb = lambda n, s, d: px.enter_context(self.nc.sbuf_tensor(tag + n, list(s), d))
        xt = [psb(f"_x{i}", (128, DM), F32) for i in range(2)]
        yt = [psb(f"_y{i}", (128, DM), F32) for i in range(2)]
        y16 = [psb(f"_y16{i}", (128, DM), BF16) for i in range(2)]
        prs = [psb(f"_pr{i}", (128, DM), F32) for i in range(2)]
        sts = [psb(f"_st{i}", (128, 8), F32) for i in range(2)]
        lgcol = psb("_lg", (128, 16), F32)
        if logits:
            wrb = psb("_wrb", (128, DM), F32)
            sc.dma("sync", wrb[:], T["wrb"], ct="ldc")
        for i0 in range(0, TPC // 128, 2):
            for i in (i0, i0 + 1):
                b = i % 2
                sc.dma("sync", xt[b][:], src[i * 128:(i + 1) * 128, :], ct="ldx" + str(b))
            for i in (i0, i0 + 1):
                b = i % 2
                pr, st = prs[b], sts[b]
                sc.s([xt[b]], [pr, st], lambda e, b=b, pr=pr, st=st: e.activation(
                    pr[:], xt[b][:], AF.Square, accum_out=st[:, 1:2]))
                sc.s([xt[b]], [pr, st], lambda e, b=b, pr=pr, st=st: e.activation(
                    pr[:], xt[b][:], AF.Copy, accum_out=st[:, 0:1]))
                sc.v([st], [st], lambda e, st=st: e.tensor_scalar(
                    st[:, 2:3], st[:, 0:1], 1.0 / DM, None, op0=ALU.mult))
                sc.v([st], [st], lambda e, st=st: e.tensor_scalar(
                    st[:, 6:7], st[:, 2:3], st[:, 2:3], None, op0=ALU.mult))
                sc.v([st], [st], lambda e, st=st: e.tensor_scalar(
                    st[:, 3:4], st[:, 1:2], 1.0 / DM, st[:, 6:7], op0=ALU.mult, op1=ALU.subtract))
                sc.s([st], [st], lambda e, st=st: e.activation(st[:, 7:8], st[:, 3:4], AF.Ln, bias=self.ccol[:, 0:1]))
                sc.s([st], [st], lambda e, st=st: e.activation(st[:, 4:5], st[:, 7:8], AF.Exp, scale=-0.5))
                sc.v([xt[b], st], [yt[b]], lambda e, b=b, st=st: e.tensor_scalar(
                    yt[b][:], xt[b][:], st[:, 2:3], st[:, 4:5],
                    op0=ALU.subtract, op1=ALU.mult))
                sc.v([yt[b]], [y16[b]], lambda e, b=b: e.tensor_copy(y16[b][:], yt[b][:]))
                if logits:
                    sc.v([xt[b], st, wrb], [pr], lambda e, b=b, pr=pr, st=st: e.scalar_tensor_tensor(
                        pr[:], xt[b][:], st[:, 2:3], wrb[:], op0=ALU.subtract, op1=ALU.mult))
                    sc.s([pr, st], [pr, lgcol], lambda e, i=i, pr=pr, st=st: e.activation(
                        pr[:], pr[:], AF.Copy, scale=st[:, 4:5], accum_out=lgcol[:, i:i + 1]))
            for i in (i0, i0 + 1):
                b = i % 2
                if logits:
                    sc.dma("sync", T["ybuf"][i * 128:(i + 1) * 128, :], yt[b][:], ct="sty" + str(b))
                sc.dma("gpsimd", T["yb16"][i * 128:(i + 1) * 128, :], y16[b][:], ct="sty16" + str(b))
        if logits:
            self.store_lg(lgcol, px)
        px.close()
        if DEBUG and call == 0:
            self.copy_dram(T["ybuf"], T["dbg_y"], (TPC, DM), F32)

    def router(self, call):
        sc, T, C = self.sc, self.T, self.consts
        tag = f"rt{call}"
        rx = ExitStack()
        self.ps_scoped = lambda n, s: rx.enter_context(
            self.nc.psum_tensor(tag + n, list(s), F32))
        sc.g([T["lg_in"]], [T["lg_out"]], lambda e: e.collective_compute(
            "AllGather", ALU.bypass, replica_groups=[list(range(NCORES))],
            ins=[T["lg_in"]], outs=[T["lg_out"]]), inc=1, ct="cc")
        ll = self.sb(f"{tag}_ll", (128, 128), F32)
        sc.dma("sync", ll[:], T["lg_out"], ct="ldll")
        lohi = self.sb(f"{tag}_lh", (128, 8), F32)
        cnt = self.sb(f"{tag}_cnt", (128, 1), F32)
        fl = self.sb(f"{tag}_fl", (128, 128), F32)
        p_cnt = self.ps_scoped("_pc", (128, 2))
        ll2 = self.sb(f"{tag}_ll2", (128, 128), F32)
        mcol = self.sb(f"{tag}_mc", (128, 2), F32)
        dcol = self.sb(f"{tag}_dc", (128, 2), F32)
        sc.v([ll], [ll2], lambda e: e.tensor_scalar(
            ll2[:], ll[:], 2.0, None, op0=ALU.mult))
        sc.v([], [lohi], lambda e: e.memset(lohi[:, 0:1], -64.0))
        sc.v([lohi], [lohi], lambda e: e.memset(lohi[:, 1:2], 64.0))
        sc.v([lohi], [lohi], lambda e: e.memset(lohi[:, 2:3], 0.0))
        t2b = lohi.ap()[:, 2:3].to_broadcast([128, 2])
        for it in range(BIS):
            sc.v([ll2, lohi], [fl, cnt], lambda e: e.tensor_scalar(
                fl[:], ll2[:], lohi[:, 2:3], 0.0, op0=ALU.is_ge, op1=ALU.add,
                accum_out=cnt[:, 0:1]))
            sc.mm([p_cnt[:, 0:1]], C["bmask"][:], cnt[:], start=True, stop=True)
            sc.v([p_cnt], [mcol], lambda e: e.tensor_scalar(
                mcol[:, 0:1], p_cnt[:, 0:1], 1024.0, None, op0=ALU.is_ge))
            sc.v([mcol], [mcol], lambda e: e.tensor_scalar(
                mcol[:, 1:2], mcol[:, 0:1], -1.0, 1.0, op0=ALU.mult, op1=ALU.add))
            sc.v([lohi], [dcol], lambda e: e.scalar_tensor_tensor(
                dcol[:], t2b, -0.5, lohi[:, 0:2], op0=ALU.mult, op1=ALU.add))
            sc.v([mcol, dcol], [dcol], lambda e: e.tensor_mul(
                dcol[:], dcol[:], mcol[:]))
            sc.v([lohi, dcol], [lohi], lambda e: e.tensor_sub(
                lohi[:, 0:2], lohi[:, 0:2], dcol[:]))
            sc.v([lohi], [lohi], lambda e: e.tensor_add(
                lohi[:, 2:3], lohi[:, 0:1], lohi[:, 1:2]))
        sc.v([ll, lohi], [fl, cnt], lambda e: e.tensor_scalar(
            fl[:], ll[:], lohi[:, 0:1], 0.0, op0=ALU.is_ge, op1=ALU.add,
            accum_out=cnt[:, 0:1]))
        pref = self.sb(f"{tag}_pref", (128, 128), F32)
        zero1 = self.sb(f"{tag}_z1", (128, 128), F32)
        p_rb = self.ps_scoped("_prb", (128, 1))
        rbase = self.sb(f"{tag}_rb", (128, 1), F32)
        sc.v([], [zero1], lambda e: e.memset(zero1[:], 0.0))
        sc.v([fl, zero1], [pref], lambda e: e.tensor_tensor_scan(
            pref[:], fl[:], zero1[:], 0.0, op0=ALU.add, op1=ALU.add))
        sc.mm([p_rb[:]], C["tri_strict"][:], cnt[:], start=True, stop=True)
        sc.v([p_rb, C["bcol1024"]], [rbase], lambda e: e.tensor_tensor(
            rbase[:], p_rb[:], C["bcol1024"][:], op=ALU.add))
        sc.v([pref, rbase], [pref], lambda e: e.tensor_scalar(
            pref[:], pref[:], rbase[:, 0:1], None, op0=ALU.add))
        ex = self.sb(f"{tag}_ex", (128, 128), F32)
        neg = self.sb(f"{tag}_neg", (128, 1), F32)
        sc.v([lohi], [neg], lambda e: e.tensor_scalar(
            neg[:], lohi[:, 0:1], -1.0, None, op0=ALU.mult))
        sc.s([ll, neg], [ex], lambda e: e.activation(ex[:], ll[:], AF.Exp, bias=neg[:, 0:1]))
        sc.v([ex, fl], [ex], lambda e: e.tensor_mul(ex[:], ex[:], fl[:]))
        zc = self.sb(f"{tag}_zc", (128, 1), F32)
        sc.v([ex], [zc], lambda e: e.reduce_sum(zc[:], ex[:], axis=mybir.AxisListType.X))
        p_z = self.ps_scoped("_pz", (128, 2))
        z2 = self.sb(f"{tag}_z2", (128, 2), F32)
        sc.mm([p_z[0:2, 0:1]], C["ones2"][:], zc[:], start=True, stop=True)
        sc.v([p_z], [z2], lambda e: e.tensor_copy(z2[0:2, 0:1], p_z[0:2, 0:1]))
        sc.mm([p_z[:, 1:2]], C["map2"][0:2, :], z2[0:2, 0:1], start=True, stop=True)
        zinv = self.sb(f"{tag}_zi", (128, 1), F32)
        sc.v([p_z], [zinv], lambda e: e.reciprocal(zinv[:], p_z[:, 1:2]))
        sc.v([zinv], [zinv], lambda e: e.tensor_scalar(
            zinv[:], zinv[:], 0.5, None, op0=ALU.mult))
        rw = self.sb(f"{tag}_rw", (128, 128), F32)
        sc.v([ex, zinv], [rw], lambda e: e.tensor_scalar(
            rw[:], ex[:], zinv[:, 0:1], None, op0=ALU.mult))
        off = self.sb(f"{tag}_off", (128, 128), F32)
        BIG = 2048.0
        sc.v([pref], [off], lambda e: e.tensor_scalar(
            off[:], pref[:], -1.0 - BIG, None, op0=ALU.add))
        sc.v([off, fl], [off], lambda e: e.tensor_mul(off[:], off[:], fl[:]))
        sc.v([off], [off], lambda e: e.tensor_scalar(
            off[:], off[:], BIG, None, op0=ALU.add))
        p_my = self.ps_scoped("_pmy", (128, 32))
        sc.mm([p_my[:, 0:16]], off[:], C["selmy"][:], start=True, stop=True)
        sc.mm([p_my[:, 16:32]], rw[:], C["selmy"][:], start=True, stop=True)
        offi = self.sb(f"{tag}_offi", (128, 16), I32)
        rwmy = self.sb(f"{tag}_rwmy", (128, 16), F32)
        sc.v([p_my], [offi], lambda e: e.tensor_copy(offi[:], p_my[:, 0:16]))
        sc.v([p_my], [rwmy], lambda e: e.tensor_copy(rwmy[:], p_my[:, 16:32]))
        self.offi, self.rwmy = offi, rwmy
        rx.close()
        if DEBUG and call == 0:
            sc.dma("sync", T["dbg_lg"], T["lg_out"], ct="dbg")
            sc.dma("sync", T["dbg_flags"], fl[:], ct="dbg")
            sc.dma("sync", T["dbg_prefix"], pref[:], ct="dbg")
            sc.dma("sync", T["dbg_rw"], rw[:], ct="dbg")
            sc.dma("sync", T["dbg_offmy"], offi[:], ct="dbg")

    _lg_n = [0]

    def store_lg(self, lgcol, stack, p_reuse=None):
        sc, T = self.sc, self.T
        self._lg_n[0] += 1
        p = p_reuse if p_reuse is not None else stack.enter_context(
            self.nc.psum_tensor(f"plgT{self._lg_n[0]}", [128, 128], F32))
        lgT = stack.enter_context(self.nc.sbuf_tensor(
            f"slgT{self._lg_n[0]}", [16, 128], F32))
        sc.trans(p[0:16, 0:128], lgcol[:, 0:16], self.identf[:])
        sc.v([p], [lgT], lambda e, p=p, lgT=lgT: e.tensor_copy(lgT[:], p[0:16, 0:128]))
        sc.dma("sync", T["lg_in"].rearrange("r (i c) -> (r i) c", c=128), lgT[:],
               ct="stlg")

    def ln_tile(self, xsb, i, bufs, logits):
        sc, T = self.sc, self.T
        pr, st, y16 = bufs["pr"], bufs["st"], bufs["y16"]
        sc.s([xsb], [pr, st], lambda e: e.activation(
            pr[:], xsb[:], AF.Square, accum_out=st[:, 1:2]))
        sc.s([xsb], [pr, st], lambda e: e.activation(
            pr[:], xsb[:], AF.Copy, accum_out=st[:, 0:1]))
        sc.v([st], [st], lambda e: e.tensor_scalar(
            st[:, 2:3], st[:, 0:1], 1.0 / DM, None, op0=ALU.mult))
        sc.v([st], [st], lambda e: e.tensor_scalar(
            st[:, 6:7], st[:, 2:3], st[:, 2:3], None, op0=ALU.mult))
        sc.v([st], [st], lambda e: e.tensor_scalar(
            st[:, 3:4], st[:, 1:2], 1.0 / DM, st[:, 6:7], op0=ALU.mult,
            op1=ALU.subtract))
        sc.s([st], [st], lambda e: e.activation(
            st[:, 7:8], st[:, 3:4], AF.Ln, bias=self.ccol[:, 0:1]))
        sc.s([st], [st], lambda e: e.activation(
            st[:, 4:5], st[:, 7:8], AF.Exp, scale=-0.5))
        if logits:
            y, lgcol, wrb = bufs["y"], bufs["lgcol"], bufs["wrb"]
            sc.v([xsb, st], [y], lambda e: e.tensor_scalar(
                y[:], xsb[:], st[:, 2:3], st[:, 4:5],
                op0=ALU.subtract, op1=ALU.mult))
            sc.v([y], [y16], lambda e: e.tensor_copy(y16[:], y[:]))
            sc.v([xsb, st, wrb], [pr], lambda e: e.scalar_tensor_tensor(
                pr[:], xsb[:], st[:, 2:3], wrb[:], op0=ALU.subtract, op1=ALU.mult))
            sc.s([pr, st], [pr, lgcol], lambda e, i=i: e.activation(
                pr[:], pr[:], AF.Copy, scale=st[:, 4:5],
                accum_out=lgcol[:, i:i + 1]))
            sc.dma("sync", T["ybuf"][i * 128:(i + 1) * 128, :], y[:], ct="flny")
        else:
            sc.v([xsb, st], [y16], lambda e: e.tensor_scalar(
                y16[:], xsb[:], st[:, 2:3], st[:, 4:5],
                op0=ALU.subtract, op1=ALU.mult))
        sc.dma("gpsimd", T["yb16"][i * 128:(i + 1) * 128, :], y16[:], ct="flns")

    def ysel(self, call):
        sc, T = self.sc, self.T
        for k in range(16):
            sc.dma("sync", T["ysel_in"][k * 128:(k + 1) * 128, :], self.zero16[:],
                   ct="zys")
        offi = self.offi
        with ExitStack() as px:
            ytiles = [px.enter_context(self.nc.sbuf_tensor(
                f"ys{call}_t{i}", [128, DM], BF16)) for i in range(8)]
            for k in range(16):
                ytile = ytiles[k % 8]
                sc.dma("sync", ytile[:], T["yb16"][k * 128:(k + 1) * 128, :], ct="ldyst")
                sc.op("gpsimd", [ytile, offi, T["ysel_in"]], [T["ysel_in"]],
                      lambda e, k=k, ytile=ytile: e.indirect_dma_start(
                          out=T["ysel_in"],
                          out_offset=IndirectOffsetOnAxis(offi[:, k:k + 1], 0),
                          in_=ytile[:], in_offset=None),
                      inc=16, ct="scat")
        sc.g([T["ysel_in"]], [T["ysel_out"]], lambda e: e.collective_compute(
            "AllReduce", ALU.add, replica_groups=[list(range(NCORES))],
            ins=[T["ysel_in"][0:2 * SEL, :]], outs=[T["ysel_out"]]), inc=1, ct="cc")
        if DEBUG and call == 0:
            self.copy_dram(T["ysel_out"], T["dbg_ysel"], (2 * SEL, DM), BF16, tag="dcy")

    def epilogue(self, x_src, x_dst, call, dbg):
        sc, T = self.sc, self.T
        tag = f"ep{call}"
        fuse_ln = (call == 0)
        with ExitStack() as px:
            sb = lambda n, s, d: px.enter_context(self.nc.sbuf_tensor(tag + n, list(s), d))
            NR = 4
            g16s = [sb(f"g16{i}", (128, DM), BF16) for i in range(NR)]
            xts = [sb(f"xt{i}", (128, DM), F32) for i in range(NR)]
            yts = [sb(f"yt{i}", (128, DM), F32) for i in range(NR)]
            x1s = [sb(f"x1{i}", (128, DM), F32) for i in range(NR)]
            if fuse_ln:
                lnb = [{"pr": sb(f"lpr{i}", (128, DM), F32),
                        "st": sb(f"lst{i}", (128, 8), F32),
                        "y16": sb(f"ly16{i}", (128, DM), BF16)} for i in range(NR)]
            offi, rwmy = self.offi, self.rwmy
            for k0 in range(0, 16, NR):
                for k in range(k0, k0 + NR):
                    b = k % NR
                    g16, xt, yt = g16s[b], xts[b], yts[b]
                    sc.op("gpsimd", [T["ffn_out"], offi], [g16],
                          lambda e, k=k, g16=g16: e.indirect_dma_start(
                              out=g16[:], out_offset=None,
                              in_=T["ffn_out"],
                              in_offset=IndirectOffsetOnAxis(offi[:, k:k + 1], 0)),
                          inc=16, ct="gx3" + str(b))
                    sc.dma("sync", xt[:], x_src[k * 128:(k + 1) * 128, :], ct="lex" + str(b))
                    sc.dma("sync", yt[:], T["ybuf"][k * 128:(k + 1) * 128, :], ct="ley" + str(b))
                for k in range(k0, k0 + NR):
                    b = k % NR
                    g16, xt, yt, x1 = g16s[b], xts[b], yts[b], x1s[b]
                    sc.v([g16, rwmy, xt], [x1], lambda e, k=k, g16=g16, xt=xt, x1=x1: e.scalar_tensor_tensor(
                        x1[:], g16[:], rwmy[:, k:k + 1], xt[:], op0=ALU.mult, op1=ALU.add))
                    sc.v([yt, x1], [x1], lambda e, yt=yt, x1=x1: e.scalar_tensor_tensor(
                        x1[:], yt[:], 0.5, x1[:], op0=ALU.mult, op1=ALU.add))
                    if fuse_ln:
                        self.ln_tile(x1, k, lnb[b], logits=False)
                for k in range(k0, k0 + NR):
                    b = k % NR
                    sc.dma("sync", x_dst[k * 128:(k + 1) * 128, :], x1s[b][:],
                           ct="sex" + str(b))
            if DEBUG and dbg:
                self.copy_dram(x_dst, T[dbg], (TPC, DM), F32, tag="dce" + str(call))

    _dc_n = [0]

    def copy_dram(self, src, dst, shape, dt, tag="dc"):
        sc = self.sc
        self._dc_n[0] += 1
        with ExitStack() as dcx:
            tile = dcx.enter_context(self.nc.sbuf_tensor(
                f"{tag}_t{self._dc_n[0]}", [128, shape[1]], dt))
            for i in range(shape[0] // 128):
                sc.dma("sync", tile[:], src[i * 128:(i + 1) * 128, :], ct="dbg")
                sc.dma("sync", dst[i * 128:(i + 1) * 128, :], tile[:], ct="dbg")


def kernel(**inputs):
    full_x = np.ascontiguousarray(inputs["L0_attn_hidden_states"])
    consts = host_constants()
    prog = Prog()
    nc = prog.build()

    freqs = np.ascontiguousarray(inputs["L0_ffn_freqs_cis"])
    cosT = np.ascontiguousarray(freqs[:, :, 0].T)
    sinT = np.ascontiguousarray(freqs[:, :, 1].T)
    cosx = np.concatenate([cosT, cosT], axis=0)
    cosx = np.concatenate([cosx, cosx], axis=1)
    sinx = np.concatenate([-sinT, sinT], axis=0)
    sinx = np.concatenate([sinx, sinx], axis=1)

    base = {
        "wqT_a": np.ascontiguousarray(inputs["L0_attn_Wq"].T),
        "wkT_a": np.ascontiguousarray(inputs["L0_attn_Wk"].T),
        "wvT_a": np.ascontiguousarray(inputs["L0_attn_Wv"].T),
        "bq_a": np.ascontiguousarray(inputs["L0_attn_bq"].reshape(16, 128).T),
        "bk_a": np.ascontiguousarray(inputs["L0_attn_bk"].reshape(4, 128).T),
        "bv_b": np.broadcast_to(inputs["L0_attn_bv"], (128, 512)).copy(),
        "wo_f": np.ascontiguousarray(inputs["L0_ffn_wo"]),
        "g1b": np.broadcast_to(inputs["L0_ffn_g1"], (128, DM)).copy(),
        "g2b": np.broadcast_to(inputs["L0_ffn_g2"], (128, DM)).copy(),
        "wrb": np.broadcast_to(inputs["L0_ffn_w_router"], (128, DM)).copy(),
        "maskT": np.ascontiguousarray(inputs["L0_ffn_mask"].T),
        "cosx": cosx, "sinx": sinx,
        "tri_strict": consts["tri_strict"], "ones2": consts["ones2"],
        "map2": consts["map2"], "bmask": consts["bmask"],
        "ident128": consts["ident128"],
        "bcol1024": consts["bcol1024"], "maskPT": consts["maskPT"],
        "iota1": np.arange(128, dtype=np.float32).reshape(128, 1),
        "iota16": (np.arange(128, dtype=np.float32) * 16).reshape(128, 1),
    }
    in_maps = []
    for c in range(NCORES):
        B, cb = c // 4, c % 4
        m = dict(base)
        m["x"] = np.ascontiguousarray(full_x[B, cb * TPC:(cb + 1) * TPC, :])
        m["wq_f"] = np.ascontiguousarray(inputs["L0_ffn_wq"][:, 2 * c * 128:(2 * c + 2) * 128])
        m["wk_f"] = np.ascontiguousarray(inputs["L0_ffn_wk"][:, 2 * c * 128:(2 * c + 2) * 128])
        m["wv_f"] = np.ascontiguousarray(inputs["L0_ffn_wv"][:, 2 * c * 128:(2 * c + 2) * 128])
        m["w1_f"] = np.ascontiguousarray(inputs["L0_ffn_w1"][:, c * 1024:(c + 1) * 1024])
        m["w3_f"] = np.ascontiguousarray(inputs["L0_ffn_w3"][:, c * 1024:(c + 1) * 1024])
        m["w2_f"] = np.ascontiguousarray(inputs["L0_ffn_w2"][c * 1024:(c + 1) * 1024, :])
        selmy = np.zeros((128, 16), np.float32)
        for k in range(16):
            selmy[64 * B + 16 * cb + k, k] = 1.0
        m["selmy"] = selmy
        meta = np.zeros((128, 16), np.float32)
        meta[:, 0] = c
        meta[:, 1] = B
        meta[:, 3] = 256 * c
        meta[:, 4] = 2 * c
        for r in range(NCORES - 1):
            meta[:, 5 + r] = 1.0 if (r // 4 == B and r % 4 < cb) else 0.0
        m["core_meta"] = meta
        in_maps.append(m)

    res = run_bass_kernel_spmd(nc, in_maps, core_ids=list(range(NCORES)))
    globals()["_last_res"] = res
    out = np.empty((2, SEQ, DM), np.float32)
    for c in range(NCORES):
        B, cb = c // 4, c % 4
        out[B, cb * TPC:(cb + 1) * TPC, :] = res.results[c]["out_x"]
    return out
